# version 02
# baseline (speedup 1.0000x reference)
from contextlib import ExitStack
import numpy as np
import ml_dtypes
import concourse.bass as bass
import concourse.mybir as mybir
from concourse.bass_utils import run_bass_kernel_spmd

F32 = mybir.dt.float32
BF16 = mybir.dt.bfloat16
I32 = mybir.dt.int32
AF = mybir.ActivationFunctionType
OP = mybir.AluOpType
AX = mybir.AxisListType
BF = ml_dtypes.bfloat16

D = 2048
H = 16
DK = 64
DV = 128
SCALE_A = DK ** -0.5
NMLP = 16384
NTT = 512
SERIAL_DELTA = False


def _perm(n):
    c = np.arange(n) // 128
    p = np.arange(n) % 128
    return 256 * (c // 2) + 2 * p + (c % 2)

PERM_D = _perm(D)
PERM_N = _perm(NMLP)

_cc = np.ascontiguousarray

def host_prep(inputs, T):
    g = lambda k: np.asarray(inputs[k])
    w = {}
    w["aa_qkw"] = _cc(g("L0_attn_a_qk_w").T[PERM_D])
    w["aa_vgw"] = _cc(g("L0_attn_a_vg_w").T[PERM_D])
    w["aa_ow"] = _cc(g("L0_attn_a_o_w")[PERM_D].T)
    w["aa_abw"] = _cc(np.concatenate([g("L0_attn_a_b_w").T[PERM_D],
                                      g("L0_attn_a_a_w").T[PERM_D]], axis=1))
    w["aa_vecT"] = _cc(np.stack([g("L0_attn_a_b_bias"), g("L0_attn_a_beta_bias"),
                                 g("L0_attn_a_A_log"), g("L0_attn_a_dt_bias")], axis=1))
    w["aa_conv"] = _cc(np.concatenate([g("L0_attn_a_q_conv_w")[:, 0, :],
                                       g("L0_attn_a_k_conv_w")[:, 0, :],
                                       g("L0_attn_a_v_conv_w")[:, 0, :]], axis=0))
    w["ab_wq"] = _cc(g("L0_attn_b_Wq").T[PERM_D])
    w["ab_wk"] = _cc(g("L0_attn_b_Wk").T[PERM_D])
    w["ab_wv"] = _cc(g("L0_attn_b_Wv").T[PERM_D])
    w["ab_wg"] = _cc(g("L0_attn_b_Wg").T[PERM_D])
    w["ab_wo"] = _cc(g("L0_attn_b_Wo")[PERM_D].T)
    w["ab_ws1"] = _cc(g("L0_attn_b_Ws1").T[PERM_D])
    w["ab_ws2T"] = _cc(g("L0_attn_b_Ws2").T)
    w["ab_bs2T"] = _cc(g("L0_attn_b_bs2").reshape(16, 64).T)
    w["ab_gnw"] = _cc(g("L0_attn_b_g_norm_weight").reshape(128, 1).astype(np.float32))
    w["f_gp"] = _cc(g("L0_ffn_gate_W_packed")[PERM_N].T)
    w["f_up"] = _cc(g("L0_ffn_up_W_packed")[PERM_N].T)
    w["f_dp"] = _cc(g("L0_ffn_down_W_packed")[PERM_D].T)
    w["f_gs"] = _cc(g("L0_ffn_gate_scales")[PERM_N].T)
    w["f_us"] = _cc(g("L0_ffn_up_scales")[PERM_N].T)
    w["f_ds"] = _cc(g("L0_ffn_down_scales")[PERM_D].T)

    i = np.arange(128)
    w["c_mstrict"] = _cc((i[:, None] < i[None, :]).astype(np.float32))
    w["c_mincl"] = _cc((i[:, None] <= i[None, :]).astype(np.float32))
    w["c_idenf"] = np.eye(128, dtype=np.float32)
    w["c_idenb"] = np.eye(128, dtype=np.float32).astype(BF)
    w["c_onesbT"] = np.ones((1, 128), BF)
    w["c_onesfT"] = np.ones((1, 128), np.float32)
    w["c_onescol"] = np.ones((128, 1), np.float32)
    ccs = np.zeros((128, 8), np.float32)
    ccs[:, 0] = 1e-5
    ccs[:, 1] = 1e-6
    ccs[:, 2] = np.finfo(np.float32).eps
    ccs[:, 3] = 1e-5
    ccs[:, 4] = 1.0
    w["c_cc"] = ccs
    bd = np.zeros((128, 8, 16), np.float32)
    for ti in range(8):
        bd[np.arange(128), ti, 2 * ti + np.arange(128) // 64] = 1.0
    w["c_bd"] = _cc(bd.reshape(128, 128))
    pat = np.zeros((8, 128), np.float32)
    pat[np.arange(128) // 16, np.arange(128)] = 1.0
    w["c_pat8"] = pat.astype(BF)
    dm = np.zeros((128, 4, 512), np.float32)
    for r in range(4):
        dm[:, r, :] = (np.arange(128)[:, None] + 128 * r <= np.arange(512)[None, :])
    w["c_dmask"] = _cc(dm.reshape(128, 4 * 512))
    pos = np.arange(T, dtype=np.float64)
    inv_freq = 1.0 / 10000.0 ** (np.arange(0, 128, 2, dtype=np.float64) / 128.0)
    ang = inv_freq[:, None] * pos[None, :]
    w["c_cos"] = np.cos(ang).astype(np.float32)
    w["c_sin"] = np.sin(ang).astype(np.float32)
    w["c_cosq"] = (np.cos(ang) * (128 ** -0.5)).astype(np.float32)
    w["c_sinq"] = (np.sin(ang) * (128 ** -0.5)).astype(np.float32)

    x = np.asarray(inputs["L0_attn_a_x"])
    xs = [_cc(x[c].T[PERM_D]) for c in range(x.shape[0])]
    return w, xs

def host_post(ys, T):
    out = np.empty((len(ys), T, D), np.float32)
    for c, y in enumerate(ys):
        out[c][:, PERM_D] = y.T
    return out


ALL_SEMS = ["qS", "qA", "qG", "sP", "sV", "sA", "sG"]

class Bld:
    def __init__(self, nc):
        self.nc = nc
        self.sem = {}
        self.cnt = {}
        for name in ALL_SEMS:
            self.sem[name] = nc.alloc_semaphore(name=name)
            self.cnt[name] = 0
        self.Q = {"qS": nc.sync, "qA": nc.scalar, "qG": nc.gpsimd}
        self.engs = [nc.sync, nc.scalar, nc.vector, nc.tensor, nc.gpsimd]

    def dma(self, q, out, in_, **kw):
        self.Q[q].dma_start(out=out, in_=in_, **kw).then_inc(self.sem[q], 16)
        self.cnt[q] += 16
        return self.cnt[q]

    def inc(self, ins, s):
        ins.then_inc(self.sem[s], 1)
        self.cnt[s] += 1
        return self.cnt[s]

    def dv(self, ins):
        c = self.inc(ins, "sV")
        self.nc.vector.wait_ge(self.sem["sV"], c)
        return c

    def wait(self, eng, s, val):
        if val > 0:
            eng.wait_ge(self.sem[s], val)

    def snap(self):
        return dict(self.cnt)

    def barrier(self):
        pt = self.snap()
        for eng in self.engs:
            for s in ALL_SEMS:
                self.wait(eng, s, pt[s])
        return pt


def build_nc(T, phases=None, debug_out=False, dbg_delta=None, dbg_dq=False):
    nc = bass.Bass("TRN2", target_bir_lowering=False, debug=False,
                   detect_race_conditions=False)
    b = Bld(nc)
    NT = T // NTT
    NCH = T // 128

    dram = {}
    def din(name, shape, dt):
        dram[name] = nc.dram_tensor(name, list(shape), dt, kind="ExternalInput").ap()
    def dscr(name, shape, dt):
        kind = "ExternalOutput" if debug_out else "Internal"
        dram[name] = nc.dram_tensor(name, list(shape), dt, kind=kind).ap()

    din("x_fm", [D, T], F32)
    dram["y_fm"] = nc.dram_tensor("y_fm", [D, T], F32, kind="ExternalOutput").ap()

    for nm, sh, dt in [
        ("aa_qkw", [D, 2048], F32), ("aa_vgw", [D, 4096], F32), ("aa_ow", [2048, D], F32),
        ("aa_abw", [D, 32], F32), ("aa_vecT", [16, 4], F32), ("aa_conv", [4096, 4], F32),
        ("ab_wq", [D, 2048], F32), ("ab_wk", [D, 2048], F32), ("ab_wv", [D, 2048], F32),
        ("ab_wg", [D, 2048], F32), ("ab_wo", [2048, D], F32), ("ab_ws1", [D, 16], F32),
        ("ab_ws2T", [16, 1024], F32), ("ab_bs2T", [64, 16], F32), ("ab_gnw", [128, 1], F32),
        ("f_gp", [1024, NMLP], I32), ("f_up", [1024, NMLP], I32), ("f_dp", [8192, D], I32),
        ("f_gs", [64, NMLP], F32), ("f_us", [64, NMLP], F32), ("f_ds", [512, D], F32),
        ("c_mstrict", [128, 128], F32), ("c_mincl", [128, 128], F32),
        ("c_idenf", [128, 128], F32), ("c_idenb", [128, 128], BF16),
        ("c_onesbT", [1, 128], BF16), ("c_onesfT", [1, 128], F32),
        ("c_onescol", [128, 1], F32), ("c_cc", [128, 8], F32),
        ("c_bd", [128, 128], F32), ("c_pat8", [8, 128], BF16),
        ("c_dmask", [128, 4 * 512], F32),
        ("c_cos", [64, T], F32), ("c_sin", [64, T], F32),
        ("c_cosq", [64, T], F32), ("c_sinq", [64, T], F32),
    ]:
        din(nm, sh, dt)

    dscr("xa", [D, T], F32)
    dscr("xb", [D, T], F32)
    dscr("ln_s", [D, T], BF16)
    dscr("qk_s", [2048, T], BF16)
    dscr("qkv_s", [4096, T], BF16)
    dscr("vg_s", [4096, T], BF16)
    dscr("o_s", [2048, T], BF16)
    dscr("inter_s", [NMLP, T], BF16)
    dscr("wg_bf", [128, 16 * NMLP], BF16)
    dscr("wu_bf", [128, 16 * NMLP], BF16)
    dscr("wd_bf", [NMLP, D], BF16)

    sb = {}
    def asb(name, shape, dt):
        sb[name] = nc.alloc_sbuf_tensor(name, list(shape), dt).ap()
        return sb[name]

    for nm, sh, dt in [
        ("k_mstrict", [128, 128], F32), ("k_mincl", [128, 128], F32),
        ("k_idenf", [128, 128], F32), ("k_idenb", [128, 128], BF16),
        ("k_onesbT", [1, 128], BF16), ("k_onesfT", [1, 128], F32),
        ("k_onescol", [128, 1], F32), ("k_cc", [128, 8], F32),
        ("k_bd", [128, 128], F32), ("k_pat8", [8, 128], BF16),
        ("k_dmask", [128, 4 * 512], F32),
        ("k_vecT", [16, 4], F32), ("k_conv", [128, 32 * 4], F32),
        ("k_bs2T", [64, 16], F32), ("k_gnw", [128, 1], F32),
        ("k_ws2Tb", [16, 1024], BF16),
        ("k_negeA", [16, 1], F32), ("k_bbias", [16, 1], F32),
    ]:
        asb(nm, sh, dt)

    rows_stack = ExitStack()
    def ensure_rows():
        if "r_beta" not in sb:
            for nm, sh, dt in [("r_beta", [16, T], F32), ("r_g", [16, T], F32),
                               ("r_ssq", [64, T], F32)]:
                sb[nm] = rows_stack.enter_context(nc.sbuf_tensor(nm, sh, dt)).ap()

    ph = list(phases) if phases is not None else [
        "init", "dequant", "ln1", "aproj", "abeta", "aconv", "adelta", "aout",
        "ln2", "ffn1", "ln3", "battn", "bout", "ln4", "ffn2"]

    if debug_out:
        din("xa_init", [D, T], F32)
        din("xb_init", [D, T], F32)

    def emit_init():
        tmp = nc.alloc_sbuf_tensor("i_ws2T", [16, 1024], F32).ap()
        for src, dst in [("c_mstrict", "k_mstrict"), ("c_mincl", "k_mincl"),
                         ("c_idenf", "k_idenf"), ("c_idenb", "k_idenb"),
                         ("c_onesbT", "k_onesbT"), ("c_onesfT", "k_onesfT"),
                         ("c_onescol", "k_onescol"), ("c_cc", "k_cc"),
                         ("c_bd", "k_bd"), ("c_pat8", "k_pat8"), ("c_dmask", "k_dmask"),
                         ("aa_vecT", "k_vecT"), ("ab_bs2T", "k_bs2T"), ("ab_gnw", "k_gnw")]:
            b.dma("qS", sb[dst], dram[src])
        b.dma("qS", tmp, dram["ab_ws2T"])
        b.dma("qS", sb["k_conv"].rearrange("p (a c) -> p a c", c=4),
              dram["aa_conv"].rearrange("(a p) c -> p a c", p=128))
        ld0 = b.cnt["qS"]
        b.wait(nc.scalar, "qS", ld0)
        nc.scalar.activation(out=sb["k_negeA"], in_=sb["k_vecT"][:, 2:3], func=AF.Exp)
        b.inc(nc.scalar.activation(out=sb["k_ws2Tb"], in_=tmp, func=AF.Copy), "sA")
        b.wait(nc.vector, "sA", b.cnt["sA"])
        nc.vector.tensor_scalar_mul(out=sb["k_negeA"], in0=sb["k_negeA"], scalar1=-1.0)
        b.inc(nc.vector.tensor_add(out=sb["k_bbias"], in0=sb["k_vecT"][:, 0:1],
                                   in1=sb["k_vecT"][:, 1:2]), "sV")
        if debug_out:
            b.dma("qS", dram["xa"], dram["xa_init"])
            b.dma("qS", dram["xb"], dram["xb_init"])

    def emit_ln(xsrc, tag):
        st = ExitStack()
        pX = [st.enter_context(nc.sbuf_tensor(f"{tag}_x{i}", [128, 16 * NTT], F32)).ap()
              for i in range(2)]
        sq = [st.enter_context(nc.sbuf_tensor(f"{tag}_sq{i}", [128, NTT], F32)).ap()
              for i in range(2)]
        rows = st.enter_context(nc.sbuf_tensor(f"{tag}_rows", [1, 5 * NTT], F32)).ap()
        lno = [st.enter_context(nc.sbuf_tensor(f"{tag}_o{i}", [128, NTT], BF16)).ap()
               for i in range(2)]
        t1 = st.enter_context(nc.sbuf_tensor(f"{tag}_t1", [128, NTT], F32)).ap()
        ps_sum = st.enter_context(nc.psum_tensor(f"{tag}_psum", [1, NTT], F32)).ap()
        ps_sq = st.enter_context(nc.psum_tensor(f"{tag}_psq", [1, NTT], F32)).ap()
        ps_mb = st.enter_context(nc.psum_tensor(f"{tag}_pmb", [128, NTT], F32)).ap()
        ps_rb = st.enter_context(nc.psum_tensor(f"{tag}_prb", [128, NTT], F32)).ap()
        norm_done = [0] * NT
        row_done = [0] * NT
        for tt in range(NT):
            buf = pX[tt % 2]
            if tt >= 2:
                b.wait(nc.sync, "sV", norm_done[tt - 2])
            for c in range(16):
                b.dma("qS", buf[:, c * NTT:(c + 1) * NTT],
                      dram[xsrc][c * 128:(c + 1) * 128, tt * NTT:(tt + 1) * NTT])
            lq = b.cnt["qS"]
            b.wait(nc.scalar, "qS", lq)
            b.wait(nc.tensor, "qS", lq)
            if tt >= 1:
                b.wait(nc.tensor, "sV", row_done[tt - 1])
            sA0 = b.cnt["sA"]
            sP0 = b.cnt["sP"]
            for c in range(16):
                if c >= 2:
                    b.wait(nc.scalar, "sP", sP0 + c - 1)
                b.inc(nc.scalar.activation(out=sq[c % 2], in_=buf[:, c * NTT:(c + 1) * NTT],
                                           func=AF.Square), "sA")
                nc.tensor.matmul(ps_sum, sb["k_onescol"], buf[:, c * NTT:(c + 1) * NTT],
                                 start=(c == 0), stop=(c == 15))
                b.wait(nc.tensor, "sA", sA0 + c + 1)
                mm = nc.tensor.matmul(ps_sq, sb["k_onescol"], sq[c % 2],
                                      start=(c == 0), stop=(c == 15))
                b.inc(mm, "sP")
            stats_done = b.cnt["sP"]
            b.wait(nc.vector, "sP", stats_done)
            mean, mean2, var, lnv, rstd = (rows[:, i * NTT:(i + 1) * NTT] for i in range(5))
            nc.vector.tensor_scalar_mul(out=mean, in0=ps_sum, scalar1=1.0 / D)
            nc.vector.tensor_tensor(out=mean2, in0=mean, in1=mean, op=OP.mult)
            rd = b.inc(nc.vector.scalar_tensor_tensor(out=var, in0=ps_sq, scalar=1.0 / D,
                                                      in1=mean2, op0=OP.mult, op1=OP.subtract), "sV")
            row_done[tt] = rd
            b.wait(nc.scalar, "sV", rd)
            nc.scalar.activation(out=lnv, in_=var, func=AF.Ln, bias=sb["k_cc"][0:1, 0:1])
            rs = b.inc(nc.scalar.activation(out=rstd, in_=lnv, func=AF.Exp, scale=-0.5), "sA")
            b.wait(nc.tensor, "sA", rs)
            if tt >= 1:
                b.wait(nc.tensor, "sV", norm_done[tt - 1])
            nc.tensor.matmul(ps_mb, sb["k_onesfT"], mean, start=True, stop=True)
            bc = b.inc(nc.tensor.matmul(ps_rb, sb["k_onesfT"], rstd, start=True, stop=True), "sP")
            b.wait(nc.vector, "sP", bc)
            for c in range(16):
                if c >= 2:
                    b.wait(nc.vector, "qG", b.cnt["qG"] - 16)
                nc.vector.tensor_sub(out=t1, in0=buf[:, c * NTT:(c + 1) * NTT], in1=ps_mb)
                v = b.inc(nc.vector.tensor_mul(out=lno[c % 2], in0=t1, in1=ps_rb), "sV")
                b.wait(nc.gpsimd, "sV", v)
                b.dma("qG", dram["ln_s"][c * 128:(c + 1) * 128, tt * NTT:(tt + 1) * NTT],
                      lno[c % 2])
            norm_done[tt] = b.cnt["sV"]
        st.close()

    def emit_proj(wname, act_src, Kdim, Dout, tag, evac, act_rows0=0):
        st = ExitStack()
        KC = Kdim // 128
        OC = Dout // 128
        wld = [st.enter_context(nc.sbuf_tensor(f"{tag}_wl{i}", [128, KC * 128], F32)).ap()
               for i in range(2)]
        wbf = st.enter_context(nc.sbuf_tensor(f"{tag}_wb", [128, OC * KC * 128], BF16)).ap()
        pA = [st.enter_context(nc.sbuf_tensor(f"{tag}_a{i}", [128, KC * NTT], BF16)).ap()
              for i in range(2)]
        pp = [st.enter_context(nc.psum_tensor(f"{tag}_p{i}", [128, NTT], F32)).ap()
              for i in range(2)]
        cvt_done = [0] * OC
        for oc in range(OC):
            if oc >= 2:
                b.wait(nc.sync, "sA", cvt_done[oc - 2])
            for kc in range(KC):
                b.dma("qS", wld[oc % 2][:, kc * 128:(kc + 1) * 128],
                      dram[wname][kc * 128:(kc + 1) * 128, oc * 128:(oc + 1) * 128])
            b.wait(nc.scalar, "qS", b.cnt["qS"])
            cvt_done[oc] = b.inc(
                nc.scalar.activation(out=wbf[:, oc * KC * 128:(oc + 1) * KC * 128],
                                     in_=wld[oc % 2], func=AF.Copy), "sA")
        wdone = b.cnt["sA"]
        evac_done = [0] * NT
        for tt in range(NT):
            abuf = pA[tt % 2]
            if tt >= 2:
                b.wait(nc.sync, "sP", evac_done[tt - 2])
            for c in range(KC):
                b.dma("qS", abuf[:, c * NTT:(c + 1) * NTT],
                      dram[act_src][(act_rows0 + c * 128):(act_rows0 + (c + 1) * 128),
                                    tt * NTT:(tt + 1) * NTT])
            lq = b.cnt["qS"]
            b.wait(nc.tensor, "qS", lq)
            if tt == 0:
                b.wait(nc.tensor, "sA", wdone)
            for oc in range(OC):
                ps = pp[oc % 2]
                if oc >= 2:
                    b.wait(nc.tensor, "sV", b.evac_v[(tt, oc - 2)])
                elif tt > 0 and oc == 0:
                    b.wait(nc.tensor, "sV", b.evac_v[(tt - 1, OC - 2)])
                elif tt > 0:
                    b.wait(nc.tensor, "sV", b.evac_v[(tt - 1, OC - 1)])
                wb = wbf[:, oc * KC * 128:(oc + 1) * KC * 128]
                for kc in range(KC):
                    mm = nc.tensor.matmul(ps, wb[:, kc * 128:(kc + 1) * 128],
                                          abuf[:, kc * NTT:(kc + 1) * NTT],
                                          start=(kc == 0), stop=(kc == KC - 1))
                mp = b.inc(mm, "sP")
                b.wait(nc.vector, "sP", mp)
                evac(oc, tt, ps)
                b.evac_v[(tt, oc)] = b.cnt["sV"]
            evac_done[tt] = b.cnt["sP"]
        st.close()

    def emit_aproj():
        st = ExitStack()
        stage = [st.enter_context(nc.sbuf_tensor(f"aq_s{i}", [128, NTT], BF16)).ap()
                 for i in range(2)]
        def mk_evac(dst):
            def evac(oc, tt, ps):
                s = stage[oc % 2]
                key = oc % 2
                if key in b.stage_dma:
                    b.wait(nc.vector, "qG", b.stage_dma[key])
                v = b.inc(nc.vector.tensor_copy(out=s, in_=ps), "sV")
                b.wait(nc.gpsimd, "sV", v)
                b.stage_dma[key] = b.dma(
                    "qG", dram[dst][oc * 128:(oc + 1) * 128, tt * NTT:(tt + 1) * NTT], s)
            return evac
        b.stage_dma = {}
        b.evac_v = {}
        emit_proj("aa_qkw", "ln_s", D, 2048, "aqk", mk_evac("qk_s"))
        b.barrier()
        b.evac_v = {}
        emit_proj("aa_vgw", "ln_s", D, 4096, "avg", mk_evac("vg_s"))
        st.close()

    def emit_abeta():
        ensure_rows()
        st = ExitStack()
        wl = st.enter_context(nc.sbuf_tensor("ab_wl", [128, 16 * 32], F32)).ap()
        wb = st.enter_context(nc.sbuf_tensor("ab_wb", [128, 16 * 32], BF16)).ap()
        aT = [st.enter_context(nc.sbuf_tensor(f"ab_a{i}", [128, 16 * NTT], BF16)).ap()
              for i in range(2)]
        xsb = st.enter_context(nc.sbuf_tensor("ab_x", [16, NTT], F32)).ap()
        sp = st.enter_context(nc.sbuf_tensor("ab_sp", [16, NTT], F32)).ap()
        l1 = st.enter_context(nc.sbuf_tensor("ab_l1", [16, NTT], F32)).ap()
        ps_b = st.enter_context(nc.psum_tensor("ab_pb", [16, NTT], F32)).ap()
        ps_a = st.enter_context(nc.psum_tensor("ab_pa", [16, NTT], F32)).ap()
        for kc in range(16):
            b.dma("qS", wl[:, kc * 32:(kc + 1) * 32],
                  dram["aa_abw"][kc * 128:(kc + 1) * 128, :])
        b.wait(nc.scalar, "qS", b.cnt["qS"])
        wc = b.inc(nc.scalar.activation(out=wb, in_=wl, func=AF.Copy), "sA")
        b.wait(nc.tensor, "sA", wc)
        prev_done = 0
        for tt in range(NT):
            abuf = aT[tt % 2]
            if tt >= 2:
                b.wait(nc.sync, "sP", b.aprev[tt - 2])
            for c in range(16):
                b.dma("qS", abuf[:, c * NTT:(c + 1) * NTT],
                      dram["ln_s"][c * 128:(c + 1) * 128, tt * NTT:(tt + 1) * NTT])
            b.wait(nc.tensor, "qS", b.cnt["qS"])
            b.wait(nc.tensor, "sA", prev_done)
            for kc in range(16):
                w2 = wb[:, kc * 32:(kc + 1) * 32]
                nc.tensor.matmul(ps_b, w2[:, 0:16], abuf[:, kc * NTT:(kc + 1) * NTT],
                                 start=(kc == 0), stop=(kc == 15))
                mm = nc.tensor.matmul(ps_a, w2[:, 16:32], abuf[:, kc * NTT:(kc + 1) * NTT],
                                      start=(kc == 0), stop=(kc == 15))
            mp = b.inc(mm, "sP")
            b.aprev = getattr(b, "aprev", [0] * NT)
            b.aprev[tt] = mp
            b.wait(nc.scalar, "sP", mp)
            sg = b.inc(nc.scalar.activation(out=sp, in_=ps_b, func=AF.Sigmoid,
                                            bias=sb["k_bbias"]), "sA")
            b.wait(nc.vector, "sA", sg)
            b.inc(nc.vector.tensor_scalar_mul(out=sb["r_beta"][:, tt * NTT:(tt + 1) * NTT],
                                              in0=sp, scalar1=2.0), "sV")
            nc.scalar.activation(out=xsb, in_=ps_a, func=AF.Identity,
                                 bias=sb["k_vecT"][:, 3:4])
            nc.scalar.activation(out=sp, in_=xsb, func=AF.Abs)
            nc.scalar.activation(out=sp, in_=sp, func=AF.Exp, scale=-1.0)
            pd = b.inc(nc.scalar.activation(out=l1, in_=sp, func=AF.Ln,
                                            bias=sb["k_cc"][0:16, 4:5]), "sA")
            prev_done = pd
            b.wait(nc.vector, "sA", pd)
            nc.vector.tensor_scalar_max(out=xsb, in0=xsb, scalar1=0.0)
            nc.vector.tensor_add(out=xsb, in0=xsb, in1=l1)
            b.inc(nc.vector.tensor_scalar_mul(out=sb["r_g"][:, tt * NTT:(tt + 1) * NTT],
                                              in0=xsb, scalar1=sb["k_negeA"]), "sV")
        st.close()

    def emit_aconv():
        ensure_rows()
        st = ExitStack()
        xin = [st.enter_context(nc.sbuf_tensor(f"cv_x{i}", [128, NTT + 3], BF16)).ap()
               for i in range(2)]
        acc = st.enter_context(nc.sbuf_tensor("cv_acc", [128, NTT], F32)).ap()
        acc2 = st.enter_context(nc.sbuf_tensor("cv_acc2", [128, NTT], F32)).ap()
        sil = st.enter_context(nc.sbuf_tensor("cv_sil", [128, NTT], F32)).ap()
        silb = [st.enter_context(nc.sbuf_tensor(f"cv_sb{i}", [128, NTT], BF16)).ap()
                for i in range(2)]
        sq = st.enter_context(nc.sbuf_tensor("cv_sq", [128, NTT], F32)).ap()
        ps_q = st.enter_context(nc.psum_tensor("cv_psq", [16, NTT], F32)).ap()
        ps_k = st.enter_context(nc.psum_tensor("cv_psk", [16, NTT], F32)).ap()
        sq = [sq, st.enter_context(nc.sbuf_tensor("cv_sq2", [128, NTT], F32)).ap()]
        rowcopy_done = 0
        dma_done = {}
        for tt in range(NT):
            t0 = tt * NTT
            for ct in range(32):
                src = "qk_s" if ct < 16 else "vg_s"
                row0 = ct * 128 if ct < 16 else (ct - 16) * 128
                dst_row0 = ct * 128
                buf = xin[ct % 2]
                b.wait(nc.sync, "sV", getattr(b, "cv_done", 0))
                if tt == 0:
                    b.wait(nc.vector, "qS", b.cnt["qS"])
                    mz = b.inc(nc.vector.memset(buf[:, 0:3], 0.0), "sV")
                    b.wait(nc.sync, "sV", mz)
                    b.dma("qS", buf[:, 3:NTT + 3],
                          dram[src][row0:row0 + 128, t0:t0 + NTT])
                else:
                    b.dma("qS", buf[:, 0:NTT + 3],
                          dram[src][row0:row0 + 128, t0 - 3:t0 + NTT])
                lq = b.cnt["qS"]
                b.wait(nc.vector, "qS", lq)
                wcol = lambda tau: sb["k_conv"][:, ct * 4 + tau:ct * 4 + tau + 1]
                nc.vector.tensor_scalar_mul(out=acc, in0=buf[:, 0:NTT], scalar1=wcol(0))
                nc.vector.tensor_scalar_mul(out=acc2, in0=buf[:, 1:NTT + 1], scalar1=wcol(1))
                nc.vector.tensor_add(out=acc, in0=acc, in1=acc2)
                nc.vector.tensor_scalar_mul(out=acc2, in0=buf[:, 2:NTT + 2], scalar1=wcol(2))
                nc.vector.tensor_add(out=acc, in0=acc, in1=acc2)
                nc.vector.tensor_scalar_mul(out=acc2, in0=buf[:, 3:NTT + 3], scalar1=wcol(3))
                cd = b.inc(nc.vector.tensor_add(out=acc, in0=acc, in1=acc2), "sV")
                b.cv_done = cd
                b.wait(nc.scalar, "sV", cd)
                if (tt, ct - 2) in dma_done:
                    b.wait(nc.scalar, "qG", dma_done[(tt, ct - 2)])
                    b.wait(nc.vector, "qG", dma_done[(tt, ct - 2)])
                if ct < 16:
                    ps = ps_q if ct < 8 else ps_k
                    sl = b.inc(nc.scalar.activation(out=sil, in_=acc, func=AF.Silu), "sA")
                    b.wait(nc.vector, "sA", sl)
                    b.inc(nc.vector.tensor_copy(out=silb[ct % 2], in_=sil), "sV")
                    if ct >= 2:
                        b.wait(nc.scalar, "sP", b.cnt["sP"])
                    sqd = b.inc(nc.scalar.activation(out=sq[ct % 2], in_=sil,
                                                     func=AF.Square), "sA")
                    b.wait(nc.tensor, "sA", sqd)
                    if ct % 8 == 0:
                        b.wait(nc.tensor, "sV", rowcopy_done)
                    mm = nc.tensor.matmul(ps, sb["k_bd"][:, (ct % 8) * 16:(ct % 8) * 16 + 16],
                                          sq[ct % 2], start=(ct % 8 == 0), stop=(ct % 8 == 7))
                    mp = b.inc(mm, "sP")
                    if ct % 8 == 7:
                        b.wait(nc.vector, "sP", mp)
                        roff = 0 if ct < 8 else 32
                        rowcopy_done = b.inc(nc.vector.tensor_copy(
                            out=sb["r_ssq"][roff:roff + 16, t0:t0 + NTT], in_=ps), "sV")
                else:
                    b.inc(nc.scalar.activation(out=silb[ct % 2], in_=acc, func=AF.Silu), "sA")
                wsem = "sV" if ct < 16 else "sA"
                b.wait(nc.gpsimd, wsem, b.cnt[wsem])
                dma_done[(tt, ct)] = b.dma("qG", dram["qkv_s"][dst_row0:dst_row0 + 128,
                                                               t0:t0 + NTT], silb[ct % 2])
        st.close()

    def emit_adelta():
        st = ExitStack()
        if dbg_delta is not None:
            dram["ddump"] = nc.dram_tensor("ddump", [128, 24 * 128], F32,
                                           kind="ExternalOutput").ap()
            ddmp = st.enter_context(nc.sbuf_tensor("ad_ddmp", [128, 24 * 128], F32)).ap()
        def dcap(n, h, slot, ap):
            if dbg_delta is not None and (n, h) == dbg_delta:
                nc.vector.tensor_copy(out=ddmp[0:ap.shape[0], slot * 128:
                                               slot * 128 + ap.shape[1]], in_=ap)
        Sst = st.enter_context(nc.sbuf_tensor("ad_S", [64, H * DV], F32)).ap()
        qch = [st.enter_context(nc.sbuf_tensor(f"ad_q{i}", [128, 8 * 128], BF16)).ap()
               for i in range(2)]
        kch = [st.enter_context(nc.sbuf_tensor(f"ad_k{i}", [128, 8 * 128], BF16)).ap()
               for i in range(2)]
        vch = [st.enter_context(nc.sbuf_tensor(f"ad_v{i}", [128, H * 128], BF16)).ap()
               for i in range(2)]
        gch = [st.enter_context(nc.sbuf_tensor(f"ad_g{i}", [128, H * 128], BF16)).ap()
               for i in range(2)]
        och = [st.enter_context(nc.sbuf_tensor(f"ad_o{i}", [128, H * 128], BF16)).ap()
               for i in range(2)]
        cols = {}
        for nm in ["gT", "ngT", "aT", "bT", "rkT", "rqT", "ccT", "barkT",
                   "rqsT", "oi2T", "gam"]:
            cols[nm] = st.enter_context(nc.sbuf_tensor(f"ad_{nm}", [128, 16], F32)).ap()
        aCbc = st.enter_context(nc.sbuf_tensor("ad_aCbc", [64, 16], F32)).ap()
        aCr = st.enter_context(nc.sbuf_tensor("ad_aCr", [1, 16], F32)).ap()
        gchunk = st.enter_context(nc.sbuf_tensor("ad_gch", [16, 128], F32)).ap()

        def mk2(nm, sh, dt):
            return [st.enter_context(nc.sbuf_tensor(f"ad_{nm}{i}", sh, dt)).ap()
                    for i in range(2)]
        D0 = mk2("D0", [128, 128], F32)
        S0b = mk2("S0b", [128, 128], BF16)
        Zs = mk2("Zs", [128, 128], BF16)
        Ys = mk2("Ys", [128, 128], BF16)
        Yn = mk2("Yn", [128, 128], BF16)
        Pqs = mk2("Pqs", [128, 128], BF16)
        Y2s = mk2("Y2s", [128, 128], BF16)
        Y2n = mk2("Y2n", [128, 128], BF16)
        Ps = mk2("Ps", [128, 128], BF16)
        Rb = mk2("Rb", [128, 128], BF16)
        KS0s = mk2("KS0s", [128, 128], F32)
        Zrb = mk2("Zrb", [128, 128], BF16)
        zcs = mk2("zcs", [128, 128], F32)
        Ub = mk2("Ub", [128, 128], BF16)
        OIs = mk2("OIs", [128, 128], F32)
        Of = mk2("Of", [128, 128], F32)
        Onb = mk2("Onb", [128, 128], BF16)
        Klam = mk2("Klam", [128, 64], BF16)
        S0sc = mk2("S0sc", [64, 128], F32)
        osq = mk2("osq", [128, 1], F32)
        lamrk = mk2("lamrk", [128, 1], F32)
        gsil = mk2("gsil", [128, 128], F32)
        Gm = mk2("Gm", [128, 128], F32)
        def mk2p(nm, sh, dt):
            return [st.enter_context(nc.psum_tensor(f"ad_{nm}{i}", sh, dt)).ap()
                    for i in range(2)]
        pfA = mk2p("pfA", [128, 512], F32)
        pfB = mk2p("pfB", [128, 512], F32)
        pbb = mk2p("pbb", [128, 512], BF16)
        pC = st.enter_context(nc.psum_tensor("ad_pC", [128, 512], F32)).ap()
        pCol = pC[:, 0:16]
        pRow = pC[0:1, 16:32]
        pBc = pC[0:64, 32:48]

        b.inc(nc.vector.memset(Sst, 0.0), "sV")
        if dbg_delta is not None:
            b.inc(nc.vector.memset(ddmp, 0.0), "sV")
        qkv_pl = dram["qkv_s"].rearrange("(a p) t -> p a t", p=128)

        def QcS(h, n):
            a = h // 2
            r0 = (h % 2) * 64
            return qch[n % 2][r0:r0 + 64, a * 128:(a + 1) * 128]
        def KcS(h, n):
            a = h // 2
            r0 = (h % 2) * 64
            return kch[n % 2][r0:r0 + 64, a * 128:(a + 1) * 128]

        veV = [0, 0]
        veA = [0, 0]
        peP = [0, 0]
        prev_o_dma = [0, 0]
        vfree = [0] * (NCH + 2)
        colfree = [0] * (NCH + 2)

        for n in range(NCH):
            par = n % 2
            t0 = n * 128
            if n >= 2:
                b.wait(nc.sync, "sV", b.ad_ochfree[n - 2])
            b.dma("qS", qch[par].rearrange("p (a t) -> p a t", t=128),
                  qkv_pl[:, 0:8, t0:t0 + 128])
            b.dma("qS", kch[par].rearrange("p (a t) -> p a t", t=128),
                  qkv_pl[:, 8:16, t0:t0 + 128])
            for half in range(2):
                b.dma("qS", vch[par][:, half * 1024:(half + 1) * 1024]
                      .rearrange("p (a t) -> p a t", t=128),
                      qkv_pl[:, 16 + 8 * half:16 + 8 * (half + 1), t0:t0 + 128])
            vg_pl = dram["vg_s"].rearrange("(a p) t -> p a t", p=128)
            for half in range(2):
                b.dma("qS", gch[par][:, half * 1024:(half + 1) * 1024]
                      .rearrange("p (a t) -> p a t", t=128),
                      vg_pl[:, 16 + 8 * half:16 + 8 * (half + 1), t0:t0 + 128])
            vg_ld = b.cnt["qS"]
            b.wait(nc.vector, "sP", colfree[n])
            sc = b.inc(nc.vector.tensor_tensor_scan(
                out=gchunk, data0=sb["r_g"][:, t0:t0 + 128],
                data1=sb["r_g"][:, t0:t0 + 128], initial=0.0,
                op0=OP.add, op1=OP.bypass), "sV")
            b.wait(nc.tensor, "sV", sc)
            b.wait(nc.tensor, "sA", veA[0])
            b.wait(nc.tensor, "sA", veA[1])
            tr1 = b.inc(nc.tensor.transpose(pCol, gchunk, sb["k_idenf"][0:16, 0:16]), "sP")
            b.wait(nc.vector, "sP", tr1)
            b.dv(nc.vector.tensor_copy(out=cols["gT"], in_=pCol))
            gv = b.inc(nc.vector.tensor_scalar_mul(out=cols["ngT"], in0=cols["gT"],
                                                   scalar1=-1.0), "sV")
            b.wait(nc.scalar, "sV", gv)
            aTd = b.inc(nc.scalar.activation(out=cols["aT"], in_=cols["gT"], func=AF.Exp), "sA")
            gcC = b.inc(nc.scalar.activation(out=cols["gam"][0:16, 0:1],
                                             in_=gchunk[:, 127:128], func=AF.Exp), "sA")
            b.wait(nc.tensor, "sA", gcC)
            tr2 = b.inc(nc.tensor.transpose(pRow, cols["gam"][0:16, 0:1],
                                            sb["k_idenf"][0:16, 0:16]), "sP")
            b.wait(nc.vector, "sP", tr2)
            rw = b.inc(nc.vector.tensor_copy(out=aCr, in_=pRow), "sV")
            b.wait(nc.tensor, "sV", rw)
            bc = b.inc(nc.tensor.matmul(pBc, sb["k_onesfT"][:, 0:64], aCr,
                                        start=True, stop=True), "sP")
            tr3 = b.inc(nc.tensor.transpose(pCol, sb["r_beta"][:, t0:t0 + 128],
                                            sb["k_idenf"][0:16, 0:16]), "sP")
            b.wait(nc.vector, "sP", tr3)
            nc.vector.tensor_copy(out=aCbc, in_=pBc)
            btd = b.dv(nc.vector.tensor_copy(out=cols["bT"], in_=pCol))
            b.wait(nc.tensor, "sV", btd)
            tr4 = b.inc(nc.tensor.transpose(pCol, sb["r_ssq"][0:16, t0:t0 + 128],
                                            sb["k_idenf"][0:16, 0:16]), "sP")
            b.wait(nc.scalar, "sP", tr4)
            nc.scalar.activation(out=cols["rqT"], in_=pCol, func=AF.Ln,
                                 bias=sb["k_cc"][:, 1:2])
            rqd = b.inc(nc.scalar.activation(out=cols["rqT"], in_=cols["rqT"],
                                             func=AF.Exp, scale=-0.5), "sA")
            b.wait(nc.tensor, "sA", rqd)
            tr5 = b.inc(nc.tensor.transpose(pCol, sb["r_ssq"][32:48, t0:t0 + 128],
                                            sb["k_idenf"][32:48, 32:48]), "sP")
            b.wait(nc.scalar, "sP", tr5)
            nc.scalar.activation(out=cols["rkT"], in_=pCol, func=AF.Ln,
                                 bias=sb["k_cc"][:, 1:2])
            rkd = b.inc(nc.scalar.activation(out=cols["rkT"], in_=cols["rkT"],
                                             func=AF.Exp, scale=-0.5), "sA")
            b.wait(nc.vector, "sA", rkd)
            b.dv(nc.vector.tensor_mul(out=cols["ccT"], in0=cols["bT"], in1=cols["rkT"]))
            nc.vector.tensor_mul(out=cols["barkT"], in0=cols["ccT"], in1=cols["aT"])
            b.dv(nc.vector.tensor_scalar_mul(out=cols["rqsT"], in0=cols["rqT"],
                                             scalar1=SCALE_A))
            cold = b.inc(nc.vector.tensor_mul(out=cols["oi2T"], in0=cols["aT"],
                                              in1=cols["rqsT"]), "sV")
            colfree[n + 1] = b.cnt["sP"]
            b.wait(nc.tensor, "sV", cold)
            b.wait(nc.tensor, "qS", vg_ld)
            b.wait(nc.scalar, "qS", vg_ld)

            for hp in range(8):
                pair = (2 * hp, 2 * hp + 1)

                def ch(h):
                    return cols

                def pe_wait(i):
                    if SERIAL_DELTA:
                        b.wait(nc.tensor, "sV", b.cnt["sV"])
                        b.wait(nc.tensor, "sA", b.cnt["sA"])
                    else:
                        b.wait(nc.tensor, "sV", veV[i])
                        b.wait(nc.tensor, "sA", veA[i])

                def ve_wait(i):
                    b.wait(nc.vector, "sP", b.cnt["sP"] if SERIAL_DELTA else peP[i])
                def va_wait(i):
                    b.wait(nc.scalar, "sP", b.cnt["sP"] if SERIAL_DELTA else peP[i])

                for h in pair:
                    i = h % 2
                    pe_wait(i)
                    nc.tensor.matmul(pfA[i][:, 0:128], KcS(h, n), KcS(h, n),
                                     start=True, stop=True)
                    peP[i] = b.inc(nc.tensor.matmul(pfA[i][:, 128:256], KcS(h, n),
                                                    QcS(h, n), start=True, stop=True), "sP")
                for h in pair:
                    i = h % 2
                    b.wait(nc.scalar, "sV", veV[i])
                    veA[i] = b.inc(nc.scalar.activation(
                        out=D0[i], in_=sb["k_mstrict"], func=AF.Identity, scale=0.0,
                        bias=cols["gT"][:, h:h + 1]), "sA")
                for h in pair:
                    i = h % 2
                    b.wait(nc.tensor, "sA", veA[i])
                    peP[i] = b.inc(nc.tensor.transpose(pfB[i][:, 0:128], D0[i],
                                                       sb["k_idenf"]), "sP")
                for h in pair:
                    i = h % 2
                    ve_wait(i)
                    b.wait(nc.vector, "sA", veA[i])
                    gm = b.inc(nc.vector.tensor_scalar(
                        out=Gm[i], in0=pfB[i][:, 0:128],
                        scalar1=cols["ngT"][:, h:h + 1], scalar2=0.0,
                        op0=OP.add, op1=OP.min), "sV")
                    dcap(n, h, 22, pfB[i][:, 0:128])
                    b.wait(nc.scalar, "sV", gm)
                    d0 = b.inc(nc.scalar.activation(out=D0[i], in_=Gm[i],
                                                    func=AF.Exp), "sA")
                    veA[i] = d0
                    b.wait(nc.vector, "sA", d0)
                    nc.vector.scalar_tensor_tensor(out=Zs[i], in0=pfA[i][:, 0:128],
                                                   scalar=cols["rkT"][:, h:h + 1],
                                                   in1=D0[i], op0=OP.mult, op1=OP.mult)
                    nc.vector.tensor_mul(out=Zs[i], in0=Zs[i], in1=sb["k_mstrict"])
                    nc.vector.tensor_scalar_mul(out=Ys[i], in0=Zs[i],
                                                scalar1=cols["ccT"][:, h:h + 1])
                    nc.vector.scalar_tensor_tensor(out=Pqs[i], in0=pfA[i][:, 128:256],
                                                   scalar=cols["rkT"][:, h:h + 1],
                                                   in1=D0[i], op0=OP.mult, op1=OP.mult)
                    nc.vector.tensor_mul(out=Pqs[i], in0=Pqs[i], in1=sb["k_mincl"])
                    b.dv(nc.vector.tensor_mul(out=lamrk[i], in0=D0[i][:, 127:128],
                                               in1=cols["rkT"][:, h:h + 1]))
                    nc.vector.tensor_copy(out=S0b[i][0:64, :], in_=Sst[:, h * DV:(h + 1) * DV])
                    nc.vector.tensor_copy(out=S0b[i][64:128, :], in_=Sst[:, h * DV:(h + 1) * DV])
                    dcap(n, h, 0, D0[i])
                    dcap(n, h, 1, Zs[i])
                    dcap(n, h, 2, Ys[i])
                    dcap(n, h, 10, Pqs[i])
                    dcap(n, h, 21, Gm[i])
                    veV[i] = b.inc(nc.vector.tensor_sub(out=Ps[i], in0=sb["k_idenb"],
                                                        in1=Ys[i]), "sV")
                for h in pair:
                    i = h % 2
                    pe_wait(i)
                    nc.tensor.transpose(pbb[i][:, 0:128], Ys[i], sb["k_idenb"])
                    r0 = (h % 2) * 64
                    peP[i] = b.inc(nc.tensor.transpose(pbb[i][:, 128:192], KcS(h, n),
                                                       sb["k_idenb"][r0:r0 + 64, r0:r0 + 64]), "sP")
                for h in pair:
                    i = h % 2
                    ve_wait(i)
                    nc.vector.tensor_copy(out=Yn[i], in_=pbb[i][:, 0:128])
                    veV[i] = b.inc(nc.vector.tensor_scalar_mul(
                        out=Klam[i], in0=pbb[i][:, 128:192], scalar1=lamrk[i]), "sV")
                cur_s, cur_n = Ys, Yn
                for k in range(6):
                    for h in pair:
                        i = h % 2
                        pe_wait(i)
                        nc.tensor.matmul(pfA[i][:, 0:128], cur_n[i], cur_s[i],
                                         start=True, stop=True)
                        peP[i] = b.inc(nc.tensor.matmul(pfA[i][:, 128:256], cur_s[i],
                                                        cur_n[i], start=True, stop=True), "sP")
                    for h in pair:
                        i = h % 2
                        ve_wait(i)
                        nc.vector.tensor_copy(out=Y2s[i], in_=pfA[i][:, 0:128])
                        veV[i] = b.inc(nc.vector.tensor_copy(out=Y2n[i],
                                                             in_=pfA[i][:, 128:256]), "sV")
                    for h in pair:
                        i = h % 2
                        pe_wait(i)
                        peP[i] = b.inc(nc.tensor.matmul(pfB[i][:, 0:128], Y2n[i], Ps[i],
                                                        start=True, stop=True), "sP")
                    for h in pair:
                        i = h % 2
                        ve_wait(i)
                        veV[i] = b.inc(nc.vector.tensor_add(out=Ps[i], in0=pfB[i][:, 0:128],
                                                            in1=Ps[i]), "sV")
                    cur_s, cur_n = Y2s, Y2n
                for h in pair:
                    i = h % 2
                    pe_wait(i)
                    r0 = (h % 2) * 64
                    nc.tensor.matmul(pfA[i][:, 0:128], KcS(h, n), S0b[i][r0:r0 + 64, :],
                                     start=True, stop=True)
                    peP[i] = b.inc(nc.tensor.transpose(pbb[i][:, 256:384],
                                                       vch[par][:, h * 128:(h + 1) * 128],
                                                       sb["k_idenb"]), "sP")
                for h in pair:
                    i = h % 2
                    va_wait(i)
                    b.wait(nc.scalar, "sV", veV[i])
                    ke = b.inc(nc.scalar.activation(out=KS0s[i], in_=pfA[i][:, 0:128],
                                                    func=AF.Copy,
                                                    scale=cols["barkT"][:, h:h + 1]), "sA")
                    veA[i] = ke
                    ve_wait(i)
                    b.wait(nc.vector, "sA", ke)
                    veV[i] = b.inc(nc.vector.scalar_tensor_tensor(
                        out=Rb[i], in0=pbb[i][:, 256:384],
                        scalar=cols["bT"][:, h:h + 1], in1=KS0s[i],
                        op0=OP.mult, op1=OP.subtract), "sV")
                for h in pair:
                    i = h % 2
                    pe_wait(i)
                    peP[i] = b.inc(nc.tensor.matmul(pfB[i][:, 0:128], Zs[i], Rb[i],
                                                    start=True, stop=True), "sP")
                for h in pair:
                    i = h % 2
                    ve_wait(i)
                    dcap(n, h, 5, Zrb[i]) if False else None
                    veV[i] = b.inc(nc.vector.tensor_copy(out=Zrb[i],
                                                         in_=pfB[i][:, 0:128]), "sV")
                    dcap(n, h, 5, Zrb[i])
                for h in pair:
                    i = h % 2
                    pe_wait(i)
                    peP[i] = b.inc(nc.tensor.matmul(pfA[i][:, 0:128], Ps[i], Zrb[i],
                                                    start=True, stop=True), "sP")
                for h in pair:
                    i = h % 2
                    va_wait(i)
                    b.wait(nc.scalar, "sV", veV[i])
                    zce = b.inc(nc.scalar.activation(out=zcs[i], in_=pfA[i][:, 0:128],
                                                     func=AF.Copy,
                                                     scale=cols["ccT"][:, h:h + 1]), "sA")
                    veA[i] = zce
                    ve_wait(i)
                    b.wait(nc.vector, "sA", zce)
                    veV[i] = b.inc(nc.vector.tensor_sub(out=Ub[i], in0=Rb[i],
                                                        in1=zcs[i]), "sV")
                for h in pair:
                    i = h % 2
                    pe_wait(i)
                    r0 = (h % 2) * 64
                    nc.tensor.matmul(pfB[i][:, 0:128], QcS(h, n), S0b[i][r0:r0 + 64, :],
                                     start=True, stop=True)
                    peP[i] = b.inc(nc.tensor.matmul(pfB[i][:, 128:256], Pqs[i], Ub[i],
                                                    start=True, stop=True), "sP")
                for h in pair:
                    i = h % 2
                    va_wait(i)
                    b.wait(nc.scalar, "sV", veV[i])
                    oie = b.inc(nc.scalar.activation(out=OIs[i], in_=pfB[i][:, 0:128],
                                                     func=AF.Copy,
                                                     scale=cols["oi2T"][:, h:h + 1]), "sA")
                    ve_wait(i)
                    b.wait(nc.vector, "sA", oie)
                    of = b.inc(nc.vector.scalar_tensor_tensor(
                        out=Of[i], in0=pfB[i][:, 128:256],
                        scalar=cols["rqsT"][:, h:h + 1], in1=OIs[i],
                        op0=OP.mult, op1=OP.add), "sV")
                    b.wait(nc.scalar, "sV", of)
                    nc.scalar.activation(out=zcs[i], in_=Of[i], func=AF.Square,
                                         accum_out=osq[i])
                    nc.scalar.activation(out=osq[i], in_=osq[i], func=AF.Ln,
                                         bias=sb["k_cc"][:, 2:3], scale=1.0 / DV)
                    rst = b.inc(nc.scalar.activation(out=osq[i], in_=osq[i],
                                                     func=AF.Exp, scale=-0.5), "sA")
                    veA[i] = rst
                    b.wait(nc.vector, "sA", rst)
                    dcap(n, h, 7, Of[i])
                    veV[i] = b.inc(nc.vector.tensor_scalar_mul(out=Onb[i], in0=Of[i],
                                                               scalar1=osq[i]), "sV")
                    dcap(n, h, 8, Onb[i])
                for h in pair:
                    i = h % 2
                    pe_wait(i)
                    nc.tensor.transpose(pbb[i][:, 384:512], Onb[i], sb["k_idenb"])
                    peP[i] = b.inc(nc.tensor.matmul(pfA[i][0:64, 0:128], Klam[i], Ub[i],
                                                    start=True, stop=True), "sP")
                for h in pair:
                    i = h % 2
                    va_wait(i)
                    b.wait(nc.scalar, "sV", veV[i])
                    gsl = nc.scalar.activation(out=gsil[i],
                                               in_=gch[par][:, h * 128:(h + 1) * 128],
                                               func=AF.Silu)
                    s0s = b.inc(nc.scalar.activation(out=S0sc[i],
                                                     in_=Sst[:, h * DV:(h + 1) * DV],
                                                     func=AF.Copy,
                                                     scale=aCbc[:, h:h + 1]), "sA")
                    veA[i] = s0s
                    ve_wait(i)
                    b.wait(nc.vector, "sA", s0s)
                    if n >= 2:
                        b.wait(nc.vector, "qG", prev_o_dma[par])
                    nc.vector.tensor_mul(out=och[par][:, h * 128:(h + 1) * 128],
                                         in0=pbb[i][:, 384:512], in1=gsil[i])
                    veV[i] = b.inc(nc.vector.tensor_add(out=Sst[:, h * DV:(h + 1) * DV],
                                                        in0=pfA[i][0:64, 0:128],
                                                        in1=S0sc[i]), "sV")
                    if dbg_delta is not None and (n, h) == dbg_delta:
                        cap = [D0[i], Zs[i], Ys[i], Ps[i], Rb[i], Zrb[i], Ub[i],
                               Of[i], Onb[i], KS0s[i], Pqs[i], Yn[i], zcs[i],
                               och[par][:, h * 128:(h + 1) * 128], gsil[i]]
                        for ci, cap_t in enumerate(cap):
                            nc.vector.tensor_copy(out=ddmp[:, ci * 128:(ci + 1) * 128],
                                                  in_=cap_t)
                        for cj, nm in enumerate(["gT", "ngT", "aT", "bT", "rkT",
                                                 "rqT", "ccT", "barkT", "rqsT",
                                                 "oi2T"]):
                            nc.vector.tensor_copy(
                                out=ddmp[:, 15 * 128 + cj * 16:15 * 128 + (cj + 1) * 16],
                                in_=cols[nm])
                        nc.vector.tensor_copy(out=ddmp[0:64, 17 * 128:17 * 128 + 128],
                                              in_=Sst[:, h * DV:(h + 1) * DV])
                        nc.vector.tensor_copy(out=ddmp[0:64, 18 * 128:18 * 128 + 128],
                                              in_=S0b[i][0:64, :])
                        nc.vector.tensor_copy(out=ddmp[:, 19 * 128:19 * 128 + 1],
                                              in_=lamrk[i])
                        nc.vector.tensor_copy(out=ddmp[0:64, 20 * 128:20 * 128 + 16],
                                              in_=aCbc)
                        dd = b.inc(nc.vector.tensor_copy(
                            out=ddmp[:, 21 * 128:21 * 128 + 128], in_=zcs[i]), "sV")
                        b.wait(nc.gpsimd, "sV", dd)
                        b.dma("qG", dram["ddump"], ddmp)
            ochd = b.cnt["sV"]
            b.wait(nc.gpsimd, "sV", ochd)
            for hh in range(16):
                b.dma("qG", dram["o_s"][hh * 128:(hh + 1) * 128, t0:t0 + 128],
                      och[par][:, hh * 128:(hh + 1) * 128])
            prev_o_dma[par] = b.cnt["qG"]
            b.ad_ochfree = getattr(b, "ad_ochfree", [0] * NCH)
            b.ad_ochfree[n] = ochd
            colfree[n + 1] = b.cnt["sP"]
            vfree[n] = b.cnt["sP"]
        st.close()
        rows_stack.close()
        for nm in ["r_beta", "r_g", "r_ssq"]:
            sb.pop(nm, None)

    def emit_outproj(wname, act_src, xsrc, xdst, tag):
        st = ExitStack()
        stage = [st.enter_context(nc.sbuf_tensor(f"{tag}_s{i}", [128, NTT], F32)).ap()
                 for i in range(2)]
        xres = [st.enter_context(nc.sbuf_tensor(f"{tag}_x{i}", [128, NTT], F32)).ap()
                for i in range(2)]
        b.stage_dma = {}
        b.xadd_done = {}
        def evac(oc, tt, ps):
            key = oc % 2
            xr = xres[key]
            if key in b.xadd_done:
                b.wait(nc.scalar, "sV", b.xadd_done[key])
            b.dma("qA", xr, dram[xsrc][oc * 128:(oc + 1) * 128, tt * NTT:(tt + 1) * NTT])
            b.wait(nc.vector, "qA", b.cnt["qA"])
            s = stage[key]
            if key in b.stage_dma:
                b.wait(nc.vector, "qG", b.stage_dma[key])
            v = b.inc(nc.vector.tensor_add(out=s, in0=ps, in1=xr), "sV")
            b.xadd_done[key] = v
            b.wait(nc.gpsimd, "sV", v)
            b.stage_dma[key] = b.dma(
                "qG", dram[xdst][oc * 128:(oc + 1) * 128, tt * NTT:(tt + 1) * NTT], s)
        b.evac_v = {}
        emit_proj(wname, act_src, 2048, D, tag, evac)
        st.close()

    def emit_dequant():
        st = ExitStack()
        pk = [st.enter_context(nc.sbuf_tensor(f"dq_pk{i}", [128, 512], I32)).ap()
              for i in range(2)]
        pk_echo = st.enter_context(nc.sbuf_tensor("dq_pke", [128, 512], I32)).ap()
        scf = st.enter_context(nc.sbuf_tensor("dq_scf", [8, NMLP], F32)).ap()
        scb = st.enter_context(nc.sbuf_tensor("dq_scb", [8, NMLP], BF16)).ap()
        mki = lambda n: st.enter_context(nc.sbuf_tensor(n, [128, 512], I32)).ap()
        vc_, vt_, vi1, vi2, vsg, vbits = (mki(f"dq_v{k}") for k in "012345")
        gc_, gt_, gi1, gi2, gsg, gbits = (mki(f"dq_g{k}") for k in "012345")
        out_lo = [st.enter_context(nc.sbuf_tensor(f"dq_ol{i}", [128, 512], BF16)).ap()
                  for i in range(2)]
        pck = [st.enter_context(nc.sbuf_tensor(f"dq_pc{i}", [128, 512], BF16)).ap()
               for i in range(2)]
        out_hi = [st.enter_context(nc.sbuf_tensor(f"dq_oh{i}", [128, 512], BF16)).ap()
                  for i in range(2)]
        ps_sc = [st.enter_context(nc.psum_tensor(f"dq_ps{i}", [128, 512], F32)).ap()
                 for i in range(2)]
        if dbg_dq:
            dram["dqdump"] = nc.dram_tensor("dqdump", [128, 6 * 512], I32,
                                            kind="ExternalOutput").ap()
            dqd = st.enter_context(nc.sbuf_tensor("dq_dqd", [128, 6 * 512], I32)).ap()

        def decode2(eng, vin, lo, hi):
            (lc, lt, li1, li2, lsg, lb) = lo
            (hc, ht, hi1, hi2, hsg, hb) = hi
            eng.tensor_scalar(out=lc, in0=vin, scalar1=15, scalar2=None,
                              op0=OP.bitwise_and)
            b.dv(eng.tensor_scalar(out=hc, in0=vin, scalar1=4, scalar2=15,
                                   op0=OP.logical_shift_right, op1=OP.bitwise_and))
            for c, t in ((lc, lt), (hc, ht)):
                eng.tensor_scalar(out=t, in0=c, scalar1=7, scalar2=6,
                                  op0=OP.bitwise_and, op1=OP.logical_shift_left)
            for c, sg in ((lc, lsg), (hc, hsg)):
                mm = eng.tensor_scalar(out=sg, in0=c, scalar1=8, scalar2=12,
                                       op0=OP.bitwise_and, op1=OP.logical_shift_left)
            b.dv(mm)
            for t, i1 in ((lt, li1), (ht, hi1)):
                eng.tensor_scalar(out=i1, in0=t, scalar1=64, scalar2=None, op0=OP.is_ge)
            for t, i2 in ((lt, li2), (ht, hi2)):
                mm = eng.tensor_scalar(out=i2, in0=t, scalar1=128, scalar2=None,
                                       op0=OP.is_ge)
            b.dv(mm)
            for i1, t, bits in ((li1, lt, lb), (hi1, ht, hb)):
                eng.scalar_tensor_tensor(out=bits, in0=i1, scalar=16064, in1=t,
                                         op0=OP.mult, op1=OP.add)
            for i2, sg in ((li2, lsg), (hi2, hsg)):
                mm = eng.scalar_tensor_tensor(out=sg, in0=i2, scalar=64, in1=sg,
                                              op0=OP.mult, op1=OP.add)
            b.dv(mm)
            eng.tensor_tensor(out=lb, in0=lb, in1=lsg, op=OP.add)
            return b.dv(eng.tensor_tensor(out=hb, in0=hb, in1=hsg, op=OP.add))

        mats = [("f_gp", "f_gs", "wg_bf", 8, 32, NMLP, True),
                ("f_up", "f_us", "wu_bf", 8, 32, NMLP, True),
                ("f_dp", "f_ds", "wd_bf", 64, 2048 // 512, 2048, False)]
        st_dma = {}
        dec_done = {}
        for mat, msc, mdst, NJ, NS, NCOLS, plane in mats:
            for j in range(NJ):
                b.wait(nc.sync, "sP", b.cnt["sP"])
                b.dma("qS", scf[0:8, 0:NCOLS], dram[msc][8 * j:8 * j + 8, :])
                b.wait(nc.scalar, "qS", b.cnt["qS"])
                b.wait(nc.scalar, "sP", b.cnt["sP"])
                scv = b.inc(nc.scalar.activation(out=scb[0:8, 0:NCOLS],
                                                 in_=scf[0:8, 0:NCOLS], func=AF.Copy), "sA")
                b.wait(nc.tensor, "sA", scv)
                for ns in range(NS):
                    it = (j * NS + ns) % 2
                    if (it, "prev") in dec_done:
                        b.wait(nc.sync, "sV", dec_done[(it, "prev")][1])
                    b.dma("qS", pk[it], dram[mat][j * 128:(j + 1) * 128,
                                                  ns * 512:(ns + 1) * 512])
                    b.dma("qS", pk_echo, dram[mat][j * 128:(j + 1) * 128,
                                                   ns * 512:(ns + 1) * 512])
                    lq = b.cnt["qS"]
                    if (it, "scused") in dec_done:
                        b.wait(nc.tensor, "sV", dec_done[(it, "scused")])
                    smm = b.inc(nc.tensor.matmul(
                        ps_sc[it], sb["k_pat8"],
                        scb[0:8, ns * 512:(ns + 1) * 512],
                        start=True, stop=True), "sP")
                    b.wait(nc.vector, "qS", lq)
                    if (it, "outdma") in dec_done:
                        b.wait(nc.vector, "qG", dec_done[(it, "outdma")])
                    hi_d = decode2(nc.vector, pk[it],
                                   (vc_, vt_, vi1, vi2, vsg, vbits),
                                   (gc_, gt_, gi1, gi2, gsg, gbits))
                    dec_done[(it, "prev")] = (hi_d, hi_d)
                    b.wait(nc.vector, "sP", smm)
                    if dbg_dq and mat == "f_gp" and j == 0 and ns == 0:
                        nc.vector.tensor_copy(out=dqd[:, 0:512], in_=pk[it])
                        nc.vector.tensor_copy(out=dqd[:, 512:1024], in_=vbits)
                        nc.vector.tensor_copy(out=dqd[:, 1024:1536], in_=vc_)
                        nc.vector.tensor_copy(out=dqd[:, 1536:2048], in_=vt_)
                        nc.vector.tensor_copy(out=dqd[:, 2048:2560], in_=vi1)
                        nc.vector.tensor_copy(out=dqd[:, 2560:3072],
                                              in_=ps_sc[it].bitcast(I32))
                        dd = b.inc(nc.vector.tensor_copy(out=dqd[:, 0:1],
                                                         in_=dqd[:, 0:1]), "sV")
                        b.wait(nc.gpsimd, "sV", dd)
                        b.dma("qG", dram["dqdump"], dqd)
                    nc.vector.tensor_copy(out=pck[0], in_=vbits.bitcast(BF16)[:, 0::2])
                    b.dv(nc.vector.tensor_copy(out=pck[1],
                                               in_=gbits.bitcast(BF16)[:, 0::2]))
                    b.dv(nc.vector.tensor_tensor(out=out_lo[it], in0=pck[0],
                                                 in1=ps_sc[it], op=OP.mult))
                    scu = b.inc(nc.vector.tensor_tensor(
                        out=out_hi[it], in0=pck[1],
                        in1=ps_sc[it], op=OP.mult), "sV")
                    dec_done[(it, "scused")] = scu
                    dec_done["last_scu"] = scu
                    b.wait(nc.scalar, "sV", scu)
                    if plane:
                        dst_lo = dram[mdst][:, (2 * j) * NMLP + ns * 512:
                                            (2 * j) * NMLP + (ns + 1) * 512]
                        dst_hi = dram[mdst][:, (2 * j + 1) * NMLP + ns * 512:
                                            (2 * j + 1) * NMLP + (ns + 1) * 512]
                    else:
                        dst_lo = dram[mdst][256 * j:256 * j + 128,
                                            ns * 512:(ns + 1) * 512]
                        dst_hi = dram[mdst][256 * j + 128:256 * j + 256,
                                            ns * 512:(ns + 1) * 512]
                    b.wait(nc.gpsimd, "sV", scu)
                    b.dma("qG", dst_lo, out_lo[it])
                    dec_done[(it, "outdma")] = b.dma("qG", dst_hi, out_hi[it])
        st.close()

    def emit_ffn(xsrc, xdst, tag):
        st = ExitStack()
        lnr = st.enter_context(nc.sbuf_tensor(f"{tag}_ln", [128, 16 * T], BF16)).ap()
        wgt = st.enter_context(nc.sbuf_tensor(f"{tag}_wg", [128, 16 * 4 * 128], BF16)).ap()
        wut = st.enter_context(nc.sbuf_tensor(f"{tag}_wu", [128, 16 * 4 * 128], BF16)).ap()
        gl = [st.enter_context(nc.sbuf_tensor(f"{tag}_gl{i}", [128, NTT], F32)).ap()
              for i in range(2)]
        itile = [st.enter_context(nc.sbuf_tensor(f"{tag}_it{i}", [128, NTT], BF16)).ap()
                 for i in range(2)]
        psA = [st.enter_context(nc.psum_tensor(f"{tag}_pa{i}", [128, NTT], F32)).ap()
               for i in range(2)]
        psB = [st.enter_context(nc.psum_tensor(f"{tag}_pb{i}", [128, NTT], F32)).ap()
               for i in range(2)]
        for kc in range(16):
            b.dma("qS", lnr[:, kc * T:(kc + 1) * T],
                  dram["ln_s"][kc * 128:(kc + 1) * 128, :])
        ln_ld = b.cnt["qS"]
        b.wait(nc.tensor, "qS", ln_ld)
        evd = {}
        itdma = {}
        for ng in range(NMLP // 512):
            b.wait(nc.sync, "sP", evd.get(("wfree", ng - 1), 0))
            for kc in range(16):
                b.dma("qS", wgt[:, kc * 512:(kc + 1) * 512],
                      dram["wg_bf"][:, kc * NMLP + ng * 512: kc * NMLP + (ng + 1) * 512])
            for kc in range(16):
                b.dma("qS", wut[:, kc * 512:(kc + 1) * 512],
                      dram["wu_bf"][:, kc * NMLP + ng * 512: kc * NMLP + (ng + 1) * 512])
            wq = b.cnt["qS"]
            b.wait(nc.tensor, "qS", wq)
            for tt in range(NT):
                for n4 in range(4):
                    ncx = ng * 4 + n4
                    it = ncx % 2
                    if ("ev", it) in evd:
                        b.wait(nc.tensor, "sV", evd[("ev", it)])
                        b.wait(nc.tensor, "sA", evd[("evA", it)])
                    for kc in range(16):
                        nc.tensor.matmul(psA[it], wgt[:, kc * 512 + n4 * 128:
                                                      kc * 512 + (n4 + 1) * 128],
                                         lnr[:, kc * T + tt * NTT: kc * T + (tt + 1) * NTT],
                                         start=(kc == 0), stop=(kc == 15))
                    for kc in range(16):
                        mm = nc.tensor.matmul(psB[it], wut[:, kc * 512 + n4 * 128:
                                                           kc * 512 + (n4 + 1) * 128],
                                              lnr[:, kc * T + tt * NTT: kc * T + (tt + 1) * NTT],
                                              start=(kc == 0), stop=(kc == 15))
                    mp = b.inc(mm, "sP")
                    b.wait(nc.scalar, "sP", mp)
                    if ("glfree", it) in evd:
                        b.wait(nc.scalar, "sV", evd[("glfree", it)])
                    ga = b.inc(nc.scalar.activation(out=gl[it], in_=psA[it],
                                                    func=AF.Gelu_apprx_tanh), "sA")
                    evd[("evA", it)] = ga
                    b.wait(nc.vector, "sA", ga)
                    b.wait(nc.vector, "sP", mp)
                    if ("itdma", it) in evd:
                        b.wait(nc.vector, "qG", evd[("itdma", it)])
                    ev = b.inc(nc.vector.tensor_tensor(out=itile[it], in0=gl[it],
                                                       in1=psB[it], op=OP.mult), "sV")
                    evd[("ev", it)] = ev
                    evd[("glfree", it)] = ev
                    b.wait(nc.gpsimd, "sV", ev)
                    evd[("itdma", it)] = b.dma(
                        "qG", dram["inter_s"][ncx * 128:(ncx + 1) * 128,
                                              tt * NTT:(tt + 1) * NTT], itile[it])
            evd[("wfree", ng)] = b.cnt["sP"]
        st.close()
        b.barrier()
        st = ExitStack()
        acc = st.enter_context(nc.sbuf_tensor(f"{tag}_acc", [128, 16 * NTT], F32)).ap()
        wds = [st.enter_context(nc.sbuf_tensor(f"{tag}_wd{i}", [128, 8 * 2048], BF16)).ap()
               for i in range(2)]
        its = [st.enter_context(nc.sbuf_tensor(f"{tag}_is{i}", [128, 8 * NTT], BF16)).ap()
               for i in range(2)]
        xres = [st.enter_context(nc.sbuf_tensor(f"{tag}_xr{i}", [128, NTT], F32)).ap()
                for i in range(2)]
        stage = [st.enter_context(nc.sbuf_tensor(f"{tag}_st{i}", [128, NTT], F32)).ap()
                 for i in range(2)]
        psD = [st.enter_context(nc.psum_tensor(f"{tag}_pd{i}", [128, NTT], F32)).ap()
               for i in range(2)]
        NSUP = NMLP // (8 * 128)
        bd = {}
        for tt in range(NT):
            for dc in range(16):
                if ("odma", dc % 2) in bd:
                    b.wait(nc.vector, "qG", bd[("odma", dc % 2)])
                b.inc(nc.vector.memset(acc[:, dc * NTT:(dc + 1) * NTT], 0.0), "sV")
            z0 = b.cnt["sV"]
            for sup in range(NSUP):
                it = sup % 2
                if ("bfree", it) in bd:
                    b.wait(nc.sync, "sP", bd[("bfree", it)])
                for ncl in range(8):
                    ncx = sup * 8 + ncl
                    b.dma("qS", wds[it][:, ncl * 2048:(ncl + 1) * 2048],
                          dram["wd_bf"][ncx * 128:(ncx + 1) * 128, :])
                    b.dma("qS", its[it][:, ncl * NTT:(ncl + 1) * NTT],
                          dram["inter_s"][ncx * 128:(ncx + 1) * 128,
                                          tt * NTT:(tt + 1) * NTT])
                lq = b.cnt["qS"]
                b.wait(nc.tensor, "qS", lq)
                if sup == 0:
                    b.wait(nc.tensor, "sV", z0)
                for dc in range(16):
                    pit = dc % 2
                    if ("acc", pit) in bd:
                        b.wait(nc.tensor, "sV", bd[("acc", pit)])
                    for ncl in range(8):
                        mm = nc.tensor.matmul(
                            psD[pit], wds[it][:, ncl * 2048 + dc * 128:
                                              ncl * 2048 + (dc + 1) * 128],
                            its[it][:, ncl * NTT:(ncl + 1) * NTT],
                            start=(ncl == 0), stop=(ncl == 7))
                    mp = b.inc(mm, "sP")
                    b.wait(nc.vector, "sP", mp)
                    bd[("acc", pit)] = b.inc(nc.vector.tensor_add(
                        out=acc[:, dc * NTT:(dc + 1) * NTT],
                        in0=psD[pit], in1=acc[:, dc * NTT:(dc + 1) * NTT]), "sV")
                bd[("bfree", it)] = b.cnt["sP"]
            for dc in range(16):
                key = dc % 2
                if ("xadd", key) in bd:
                    b.wait(nc.scalar, "sV", bd[("xadd", key)])
                b.dma("qA", xres[key],
                      dram[xsrc][dc * 128:(dc + 1) * 128, tt * NTT:(tt + 1) * NTT])
                b.wait(nc.vector, "qA", b.cnt["qA"])
                if ("odma", key) in bd:
                    b.wait(nc.vector, "qG", bd[("odma", key)])
                v = b.inc(nc.vector.tensor_add(out=stage[key],
                                               in0=acc[:, dc * NTT:(dc + 1) * NTT],
                                               in1=xres[key]), "sV")
                bd[("xadd", key)] = v
                b.wait(nc.gpsimd, "sV", v)
                dst = dram[xdst]
                bd[("odma", key)] = b.dma(
                    "qG", dst[dc * 128:(dc + 1) * 128, tt * NTT:(tt + 1) * NTT],
                    stage[key])
        st.close()

    def emit_bproj():
        st = ExitStack()
        lnr = st.enter_context(nc.sbuf_tensor("bp_ln", [128, 16 * T], BF16)).ap()
        wld = [st.enter_context(nc.sbuf_tensor(f"bp_wl{i}", [128, 16 * 128], F32)).ap()
               for i in range(2)]
        wbf = [st.enter_context(nc.sbuf_tensor(f"bp_wb{i}", [128, 16 * 128], BF16)).ap()
               for i in range(2)]
        rope = st.enter_context(nc.sbuf_tensor("bp_rope", [64, 4 * NTT], F32)).ap()
        ta = st.enter_context(nc.sbuf_tensor("bp_ta", [64, NTT], F32)).ap()
        tb = st.enter_context(nc.sbuf_tensor("bp_tb", [64, NTT], F32)).ap()
        stg = [st.enter_context(nc.sbuf_tensor(f"bp_st{i}", [128, NTT], BF16)).ap()
               for i in range(2)]
        s1s = st.enter_context(nc.sbuf_tensor("bp_s1", [16, NTT], F32)).ap()
        sb["r_s1b"] = st.enter_context(nc.sbuf_tensor("r_s1b", [16, T], BF16)).ap()
        wl1 = st.enter_context(nc.sbuf_tensor("bp_ws1l", [128, 16 * 16], F32)).ap()
        wb1 = st.enter_context(nc.sbuf_tensor("bp_ws1b", [128, 16 * 16], BF16)).ap()
        pp = [st.enter_context(nc.psum_tensor(f"bp_p{i}", [128, NTT], F32)).ap()
              for i in range(2)]
        ps1 = st.enter_context(nc.psum_tensor("bp_ps1", [16, NTT], F32)).ap()
        for kc in range(16):
            b.dma("qS", lnr[:, kc * T:(kc + 1) * T],
                  dram["ln_s"][kc * 128:(kc + 1) * 128, :])
        for kc in range(16):
            b.dma("qS", wl1[:, kc * 16:(kc + 1) * 16],
                  dram["ab_ws1"][kc * 128:(kc + 1) * 128, :])
        ln_ld = b.cnt["qS"]
        b.wait(nc.scalar, "qS", ln_ld)
        w1c = b.inc(nc.scalar.activation(out=wb1, in_=wl1, func=AF.Copy), "sA")
        b.wait(nc.tensor, "qS", ln_ld)
        b.wait(nc.tensor, "sA", w1c)
        s1done = 0
        for tt in range(NT):
            b.wait(nc.tensor, "sV", s1done)
            for kc in range(16):
                mm = nc.tensor.matmul(ps1, wb1[:, kc * 16:(kc + 1) * 16],
                                      lnr[:, kc * T + tt * NTT: kc * T + (tt + 1) * NTT],
                                      start=(kc == 0), stop=(kc == 15))
            mp = b.inc(mm, "sP")
            b.wait(nc.vector, "sP", mp)
            s1done = b.inc(nc.vector.tensor_copy(
                out=sb["r_s1b"][:, tt * NTT:(tt + 1) * NTT], in_=ps1), "sV")
        dsts = {"ab_wq": ("qkv_s", 0, "q"), "ab_wk": ("qkv_s", 2048, "k"),
                "ab_wv": ("vg_s", 0, None), "ab_wg": ("vg_s", 2048, None)}
        wcvt = {}
        evd = {}
        for mat, (dst, row0, rk) in dsts.items():
            for oc in range(16):
                it = oc % 2
                if ("wfree", it) in wcvt:
                    b.wait(nc.sync, "sP", wcvt[("wfree", it)])
                for kc in range(16):
                    b.dma("qS", wld[it][:, kc * 128:(kc + 1) * 128],
                          dram[mat][kc * 128:(kc + 1) * 128, oc * 128:(oc + 1) * 128])
                b.wait(nc.scalar, "qS", b.cnt["qS"])
                if ("wcvt", it) in wcvt:
                    b.wait(nc.scalar, "sP", wcvt[("wcvt", it)])
                wc = b.inc(nc.scalar.activation(out=wbf[it], in_=wld[it],
                                                func=AF.Copy), "sA")
                b.wait(nc.tensor, "sA", wc)
                for tt in range(NT):
                    if rk:
                        c1 = "c_cosq" if rk == "q" else "c_cos"
                        s1_ = "c_sinq" if rk == "q" else "c_sin"
                        b.wait(nc.sync, "sV", evd.get("ropeused", 0))
                        b.dma("qS", rope[:, 0:NTT],
                              dram[c1][:, tt * NTT:(tt + 1) * NTT])
                        b.dma("qS", rope[:, NTT:2 * NTT],
                              dram[s1_][:, tt * NTT:(tt + 1) * NTT])
                    rl = b.cnt["qS"]
                    pit = (oc * NT + tt) % 2
                    ps = pp[pit]
                    if ("ev", pit) in evd:
                        b.wait(nc.tensor, "sV", evd[("ev", pit)])
                    for kc in range(16):
                        mm = nc.tensor.matmul(ps, wbf[it][:, kc * 128:(kc + 1) * 128],
                                              lnr[:, kc * T + tt * NTT:
                                                  kc * T + (tt + 1) * NTT],
                                              start=(kc == 0), stop=(kc == 15))
                    mp = b.inc(mm, "sP")
                    wcvt[("wcvt", it)] = mp
                    b.wait(nc.vector, "sP", mp)
                    s = stg[pit]
                    if ("sdma", pit) in evd:
                        b.wait(nc.vector, "qG", evd[("sdma", pit)])
                    if rk:
                        b.wait(nc.vector, "qS", rl)
                        cosr = rope[:, 0:NTT]
                        sinr = rope[:, NTT:2 * NTT]
                        nc.vector.tensor_mul(out=ta, in0=ps[0:64, :], in1=cosr)
                        nc.vector.tensor_mul(out=tb, in0=ps[64:128, :], in1=sinr)
                        nc.vector.tensor_sub(out=s[0:64, :], in0=ta, in1=tb)
                        nc.vector.tensor_mul(out=ta, in0=ps[64:128, :], in1=cosr)
                        nc.vector.tensor_mul(out=tb, in0=ps[0:64, :], in1=sinr)
                        ev = b.inc(nc.vector.tensor_add(out=s[64:128, :], in0=ta,
                                                        in1=tb), "sV")
                        evd["ropeused"] = ev
                    else:
                        ev = b.inc(nc.vector.tensor_copy(out=s, in_=ps), "sV")
                    evd[("ev", pit)] = ev
                    b.wait(nc.gpsimd, "sV", ev)
                    evd[("sdma", pit)] = b.dma(
                        "qG", dram[dst][row0 + oc * 128:row0 + (oc + 1) * 128,
                                        tt * NTT:(tt + 1) * NTT], s)
                wcvt[("wfree", it)] = b.cnt["sP"]
        st.close()

    def emit_bheads():
        st = ExitStack()
        Qh = st.enter_context(nc.sbuf_tensor("bh_q", [128, T], BF16)).ap()
        Kh = st.enter_context(nc.sbuf_tensor("bh_k", [128, T], BF16)).ap()
        Vf = st.enter_context(nc.sbuf_tensor("bh_vf", [128, T], BF16)).ap()
        Vtm = st.enter_context(nc.sbuf_tensor("bh_vtm", [128, T], BF16)).ap()
        esTf = st.enter_context(nc.sbuf_tensor("bh_esf", [64, T], F32)).ap()
        esTb = st.enter_context(nc.sbuf_tensor("bh_esb", [64, T], BF16)).ap()
        cesT = st.enter_context(nc.sbuf_tensor("bh_ces", [64, T], F32)).ap()
        estm = st.enter_context(nc.sbuf_tensor("bh_estm", [128, (T // 128) * 64], BF16)).ap()
        enztm = st.enter_context(nc.sbuf_tensor("bh_enztm", [128, (T // 128) * 64], F32)).ap()
        shi = st.enter_context(nc.sbuf_tensor("bh_shi", [64, NTT], F32)).ap()
        Ab = [st.enter_context(nc.sbuf_tensor(f"bh_ab{i}", [128, NTT], BF16)).ap()
              for i in range(2)]
        okf = st.enter_context(nc.sbuf_tensor("bh_okf", [128, 64], F32)).ap()
        oke = st.enter_context(nc.sbuf_tensor("bh_oke", [128, 64], F32)).ap()
        mxc = st.enter_context(nc.sbuf_tensor("bh_mx", [128, 2], F32)).ap()
        qzb = st.enter_context(nc.sbuf_tensor("bh_qz", [128, 64], BF16)).ap()
        qzT = st.enter_context(nc.sbuf_tensor("bh_qzT", [64, NTT], BF16)).ap()
        sqs = st.enter_context(nc.sbuf_tensor("bh_sqs", [128, NTT], F32)).ap()
        rrow = st.enter_context(nc.sbuf_tensor("bh_rrow", [1, NTT], F32)).ap()
        gts = [st.enter_context(nc.sbuf_tensor(f"bh_gt{i}", [128, NTT], BF16)).ap()
               for i in range(2)]
        gsl = st.enter_context(nc.sbuf_tensor("bh_gsl", [128, NTT], F32)).ap()
        ons = st.enter_context(nc.sbuf_tensor("bh_ons", [128, NTT], F32)).ap()
        o1s = st.enter_context(nc.sbuf_tensor("bh_o1", [128, NTT], F32)).ap()
        ogs = [st.enter_context(nc.sbuf_tensor(f"bh_og{i}", [128, NTT], BF16)).ap()
               for i in range(2)]
        pq1 = st.enter_context(nc.psum_tensor("bh_pq1", [128, NTT], F32)).ap()
        pq2 = st.enter_context(nc.psum_tensor("bh_pq2", [128, NTT], F32)).ap()
        pok = st.enter_context(nc.psum_tensor("bh_pok", [128, 256], F32)).ap()
        pO = st.enter_context(nc.psum_tensor("bh_pO", [128, NTT], F32)).ap()
        pbf = st.enter_context(nc.psum_tensor("bh_pbf", [128, 512], BF16)).ap()
        pbf2 = st.enter_context(nc.psum_tensor("bh_pbf2", [128, 512], BF16)).ap()
        NB = T // 128
        for h in range(H):
            hv = b.snap()
            b.wait(nc.sync, "sP", hv["sP"])
            b.wait(nc.sync, "sV", hv["sV"])
            b.dma("qS", Qh, dram["qkv_s"][h * 128:(h + 1) * 128, :])
            b.dma("qS", Kh, dram["qkv_s"][2048 + h * 128:2048 + (h + 1) * 128, :])
            b.dma("qS", Vf, dram["vg_s"][h * 128:(h + 1) * 128, :])
            hld = b.cnt["qS"]
            b.wait(nc.tensor, "sV", hv["sV"])
            b.wait(nc.tensor, "sA", hv["sA"])
            sdone = 0
            for tt in range(NT):
                mp = b.inc(nc.tensor.matmul(
                    pq1[0:64, :], sb["k_ws2Tb"][:, h * 64:(h + 1) * 64],
                    sb["r_s1b"][:, tt * NTT:(tt + 1) * NTT],
                    start=True, stop=True), "sP")
                b.wait(nc.scalar, "sP", mp)
                sa = b.inc(nc.scalar.activation(out=shi, in_=pq1[0:64, :],
                                                func=AF.Identity,
                                                bias=sb["k_bs2T"][:, h:h + 1]), "sA")
                b.wait(nc.vector, "sA", sa)
                cl = b.inc(nc.vector.tensor_scalar(out=shi, in0=shi, scalar1=32.0,
                                                   scalar2=-32.0, op0=OP.min,
                                                   op1=OP.max), "sV")
                b.wait(nc.scalar, "sV", cl)
                ex = b.inc(nc.scalar.activation(
                    out=esTf[:, tt * NTT:(tt + 1) * NTT], in_=shi, func=AF.Exp), "sA")
                b.wait(nc.vector, "sA", ex)
                sdone = b.inc(nc.vector.tensor_copy(
                    out=esTb[:, tt * NTT:(tt + 1) * NTT],
                    in_=esTf[:, tt * NTT:(tt + 1) * NTT]), "sV")
                b.wait(nc.tensor, "sV", sdone)
            sc = b.inc(nc.vector.tensor_tensor_scan(out=cesT, data0=esTf, data1=esTf,
                                                    initial=0.0, op0=OP.add,
                                                    op1=OP.bypass), "sV")
            rc = b.inc(nc.vector.reciprocal(out=cesT, in_=cesT), "sV")
            b.wait(nc.tensor, "qS", hld)
            tp = {}
            for blk in range(NB):
                pb = (pbf, pbf2)[blk % 2]
                if blk >= 2:
                    b.wait(nc.tensor, "sV", tp[blk - 2])
                nc.tensor.transpose(pb[:, 0:128], Vf[:, blk * 128:(blk + 1) * 128],
                                    sb["k_idenb"])
                if blk == 0:
                    b.wait(nc.tensor, "sV", sdone)
                tr = b.inc(nc.tensor.transpose(pb[:, 128:192],
                                               esTb[:, blk * 128:(blk + 1) * 128],
                                               sb["k_idenb"][0:64, 0:64]), "sP")
                b.wait(nc.vector, "sP", tr)
                nc.vector.tensor_copy(out=Vtm[:, blk * 128:(blk + 1) * 128],
                                      in_=pb[:, 0:128])
                tp[blk] = b.inc(nc.vector.tensor_copy(
                    out=estm[:, blk * 64:(blk + 1) * 64], in_=pb[:, 128:192]), "sV")
            for blk in range(NB):
                pf = (pq1, pq2)[blk % 2]
                if blk == 0:
                    b.wait(nc.tensor, "sV", rc)
                    b.wait(nc.tensor, "sA", b.cnt["sA"])
                if blk >= 2:
                    b.wait(nc.tensor, "sV", tp[NB + blk - 2])
                tr = b.inc(nc.tensor.transpose(pf[:, 0:64],
                                               cesT[:, blk * 128:(blk + 1) * 128],
                                               sb["k_idenf"][0:64, 0:64]), "sP")
                b.wait(nc.vector, "sP", tr)
                tp[NB + blk] = b.inc(nc.vector.tensor_copy(
                    out=enztm[:, blk * 64:(blk + 1) * 64], in_=pf[:, 0:64]), "sV")
            prep_done = b.cnt["sV"]
            b.wait(nc.tensor, "sV", prep_done)
            od = {}
            for qt in range(NT):
                nkt = 4 * qt + 4
                for kt in range(nkt):
                    pit = kt % 2
                    pA = (pq1, pq2)[pit]
                    if ("aev", pit) in od:
                        b.wait(nc.tensor, "sV", od[("aev", pit)])
                    mp = b.inc(nc.tensor.matmul(
                        pA, Kh[:, kt * 128:(kt + 1) * 128],
                        Qh[:, qt * NTT:(qt + 1) * NTT], start=True, stop=True), "sP")
                    b.wait(nc.vector, "sP", mp)
                    r = kt - 4 * qt
                    if r >= 0:
                        ae = b.inc(nc.vector.tensor_mul(
                            out=Ab[pit], in0=pA,
                            in1=sb["k_dmask"][:, r * 512:(r + 1) * 512]), "sV")
                    else:
                        ae = b.inc(nc.vector.tensor_copy(out=Ab[pit], in_=pA), "sV")
                    od[("aev", pit)] = ae
                    b.wait(nc.tensor, "sV", ae)
                    for s4 in range(4):
                        mm = nc.tensor.matmul(
                            pok[:, s4 * 64:(s4 + 1) * 64],
                            Ab[pit][:, s4 * 128:(s4 + 1) * 128],
                            estm[:, kt * 64:(kt + 1) * 64],
                            start=(kt == 0 and s4 == 0),
                            stop=(kt == nkt - 1 and s4 == 3))
                    okp = b.inc(mm, "sP")
                b.wait(nc.vector, "sP", okp)
                for s4 in range(4):
                    tb4 = 4 * qt + s4
                    b.dv(nc.vector.tensor_mul(out=okf, in0=pok[:, s4 * 64:(s4 + 1) * 64],
                                              in1=enztm[:, tb4 * 64:(tb4 + 1) * 64]))
                    mx = b.inc(nc.vector.reduce_max(out=mxc[:, 0:1], in_=okf,
                                                    axis=AX.X, negate=True), "sV")
                    b.wait(nc.scalar, "sV", mx)
                    ea = b.inc(nc.scalar.activation(out=oke, in_=okf, func=AF.Exp,
                                                    bias=mxc[:, 0:1]), "sA")
                    b.wait(nc.vector, "sA", ea)
                    b.dv(nc.vector.reduce_sum(out=mxc[:, 1:2], in_=oke, axis=AX.X))
                    b.dv(nc.vector.reciprocal(out=mxc[:, 1:2], in_=mxc[:, 1:2]))
                    qz = b.inc(nc.vector.scalar_tensor_tensor(
                        out=qzb, in0=oke, scalar=mxc[:, 1:2],
                        in1=enztm[:, tb4 * 64:(tb4 + 1) * 64],
                        op0=OP.mult, op1=OP.mult), "sV")
                    b.wait(nc.tensor, "sV", qz)
                    pb = (pbf, pbf2)[s4 % 2]
                    tr = b.inc(nc.tensor.transpose(pb[0:64, 0:128],
                                                   qzb, sb["k_idenb"]), "sP")
                    b.wait(nc.vector, "sP", tr)
                    b.inc(nc.vector.tensor_copy(
                        out=qzT[:, s4 * 128:(s4 + 1) * 128], in_=pb[0:64, 0:128]), "sV")
                qzd = b.cnt["sV"]
                b.wait(nc.tensor, "sV", qzd)
                for kt in range(nkt):
                    pit = kt % 2
                    pW = (pq1, pq2)[pit]
                    if ("wev", pit) in od:
                        b.wait(nc.tensor, "sV", od[("wev", pit)])
                    mp = b.inc(nc.tensor.matmul(
                        pW, esTb[:, kt * 128:(kt + 1) * 128], qzT,
                        start=True, stop=True), "sP")
                    b.wait(nc.vector, "sP", mp)
                    r = kt - 4 * qt
                    if r >= 0:
                        we = b.inc(nc.vector.tensor_mul(
                            out=Ab[pit], in0=pW,
                            in1=sb["k_dmask"][:, r * 512:(r + 1) * 512]), "sV")
                    else:
                        we = b.inc(nc.vector.tensor_copy(out=Ab[pit], in_=pW), "sV")
                    od[("wev", pit)] = we
                    b.wait(nc.tensor, "sV", we)
                    mm = nc.tensor.matmul(pO, Vtm[:, kt * 128:(kt + 1) * 128],
                                          Ab[pit], start=(kt == 0),
                                          stop=(kt == nkt - 1))
                op = b.inc(mm, "sP")
                b.wait(nc.scalar, "sP", op)
                sq = b.inc(nc.scalar.activation(out=sqs, in_=pO, func=AF.Square), "sA")
                b.wait(nc.tensor, "sA", sq)
                rm = b.inc(nc.tensor.matmul(pq1[0:1, :], sb["k_onescol"], sqs,
                                            start=True, stop=True), "sP")
                b.wait(nc.scalar, "sP", rm)
                nc.scalar.activation(out=rrow, in_=pq1[0:1, :], func=AF.Ln,
                                     scale=1.0 / 128, bias=sb["k_cc"][0:1, 3:4])
                rr = b.inc(nc.scalar.activation(out=rrow, in_=rrow, func=AF.Exp,
                                                scale=-0.5), "sA")
                b.wait(nc.tensor, "sA", rr)
                bc = b.inc(nc.tensor.matmul(pq2, sb["k_onesfT"], rrow,
                                            start=True, stop=True), "sP")
                git = qt % 2
                if ("gdma", git) in od:
                    b.wait(nc.scalar, "qA", od[("gdma", git)])
                od[("gdma", git)] = b.dma(
                    "qA", gts[git], dram["vg_s"][2048 + h * 128:2048 + (h + 1) * 128,
                                                 qt * NTT:(qt + 1) * NTT])
                b.wait(nc.scalar, "qA", od[("gdma", git)])
                b.wait(nc.scalar, "sV", b.cnt["sV"])
                gs_ = b.inc(nc.scalar.activation(out=gsl, in_=gts[git],
                                                 func=AF.Silu), "sA")
                ons_ = b.inc(nc.scalar.activation(out=ons, in_=pO, func=AF.Copy,
                                                  scale=sb["k_gnw"]), "sA")
                b.wait(nc.vector, "sP", bc)
                b.wait(nc.vector, "sA", ons_)
                nc.vector.tensor_mul(out=o1s, in0=ons, in1=pq2)
                if ("odma", git) in od:
                    b.wait(nc.vector, "qG", od[("odma", git)])
                og = b.inc(nc.vector.tensor_mul(out=ogs[git], in0=o1s, in1=gsl), "sV")
                b.wait(nc.gpsimd, "sV", og)
                od[("odma", git)] = b.dma(
                    "qG", dram["o_s"][h * 128:(h + 1) * 128, qt * NTT:(qt + 1) * NTT],
                    ogs[git])
                od[("aev", 0)] = od[("aev", 1)] = og
                od[("wev", 0)] = od[("wev", 1)] = og
        st.close()

    emits = {
        "init": emit_init,
        "ln1": lambda: emit_ln("x_fm", "ln1"),
        "aproj": emit_aproj,
        "abeta": emit_abeta,
        "aconv": emit_aconv,
        "adelta": emit_adelta,
        "aout": lambda: emit_outproj("aa_ow", "o_s", "x_fm", "xa", "aout"),
        "ln2": lambda: emit_ln("xa", "ln2"),
        "dequant": emit_dequant,
        "ffn1": lambda: emit_ffn("xa", "xb", "f1"),
        "ln3": lambda: emit_ln("xb", "ln3"),
        "battn": lambda: (emit_bproj(), b.barrier(), emit_bheads()),
        "bout": lambda: emit_outproj("ab_wo", "o_s", "xb", "xa", "bout"),
        "ln4": lambda: emit_ln("xa", "ln4"),
        "ffn2": lambda: emit_ffn("xa", "y_fm", "f2"),
    }
    for p in ph:
        if p in emits:
            emits[p]()
            b.barrier()

    nc.sync.wait_ge(b.sem["qG"], b.cnt["qG"])
    nc.sync.wait_ge(b.sem["qA"], b.cnt["qA"])
    nc.sync.wait_ge(b.sem["qS"], b.cnt["qS"])
    return nc, b, dram


def kernel(**inputs):
    T = np.asarray(inputs["L0_attn_a_x"]).shape[1]
    w, xs = host_prep(inputs, T)
    nc, _, _ = build_nc(T)
    in_maps = []
    for c in range(len(xs)):
        m = dict(w)
        m["x_fm"] = xs[c]
        in_maps.append(m)
    res = run_bass_kernel_spmd(nc, in_maps, core_ids=list(range(len(xs))))
    return host_post([r["y_fm"] for r in res.results], T)


# version 03 (final)
# speedup vs baseline: 1.1116x; 1.1116x over previous
from contextlib import ExitStack
import numpy as np
import ml_dtypes
import concourse.bass as bass
import concourse.mybir as mybir
from concourse.bass_utils import run_bass_kernel_spmd

F32 = mybir.dt.float32
BF16 = mybir.dt.bfloat16
I32 = mybir.dt.int32
AF = mybir.ActivationFunctionType
OP = mybir.AluOpType
AX = mybir.AxisListType
BF = ml_dtypes.bfloat16

D = 2048
H = 16
DK = 64
DV = 128
SCALE_A = DK ** -0.5
NMLP = 16384
NTT = 512
SERIAL_DELTA = False


def _perm(n):
    c = np.arange(n) // 128
    p = np.arange(n) % 128
    return 256 * (c // 2) + 2 * p + (c % 2)

PERM_D = _perm(D)
PERM_N = _perm(NMLP)

_cc = np.ascontiguousarray

def host_prep(inputs, T):
    g = lambda k: np.asarray(inputs[k])
    w = {}
    w["aa_qkw"] = _cc(g("L0_attn_a_qk_w").T[PERM_D])
    w["aa_vgw"] = _cc(g("L0_attn_a_vg_w").T[PERM_D])
    w["aa_ow"] = _cc(g("L0_attn_a_o_w")[PERM_D].T)
    w["aa_abw"] = _cc(np.concatenate([g("L0_attn_a_b_w").T[PERM_D],
                                      g("L0_attn_a_a_w").T[PERM_D]], axis=1))
    w["aa_vecT"] = _cc(np.stack([g("L0_attn_a_b_bias"), g("L0_attn_a_beta_bias"),
                                 g("L0_attn_a_A_log"), g("L0_attn_a_dt_bias")], axis=1))
    w["aa_conv"] = _cc(np.concatenate([g("L0_attn_a_q_conv_w")[:, 0, :],
                                       g("L0_attn_a_k_conv_w")[:, 0, :],
                                       g("L0_attn_a_v_conv_w")[:, 0, :]], axis=0))
    w["ab_wq"] = _cc(g("L0_attn_b_Wq").T[PERM_D])
    w["ab_wk"] = _cc(g("L0_attn_b_Wk").T[PERM_D])
    w["ab_wv"] = _cc(g("L0_attn_b_Wv").T[PERM_D])
    w["ab_wg"] = _cc(g("L0_attn_b_Wg").T[PERM_D])
    w["ab_wo"] = _cc(g("L0_attn_b_Wo")[PERM_D].T)
    w["ab_ws1"] = _cc(g("L0_attn_b_Ws1").T[PERM_D])
    w["ab_ws2T"] = _cc(g("L0_attn_b_Ws2").T)
    w["ab_bs2T"] = _cc(g("L0_attn_b_bs2").reshape(16, 64).T)
    w["ab_gnw"] = _cc(g("L0_attn_b_g_norm_weight").reshape(128, 1).astype(np.float32))
    w["f_gp"] = _cc(g("L0_ffn_gate_W_packed")[PERM_N].T)
    w["f_up"] = _cc(g("L0_ffn_up_W_packed")[PERM_N].T)
    w["f_dp"] = _cc(g("L0_ffn_down_W_packed")[PERM_D].T)
    w["f_gs"] = _cc(g("L0_ffn_gate_scales")[PERM_N].T)
    w["f_us"] = _cc(g("L0_ffn_up_scales")[PERM_N].T)
    w["f_ds"] = _cc(g("L0_ffn_down_scales")[PERM_D].T)

    i = np.arange(128)
    w["c_mstrict"] = _cc((i[:, None] < i[None, :]).astype(np.float32))
    w["c_mincl"] = _cc((i[:, None] <= i[None, :]).astype(np.float32))
    w["c_idenf"] = np.eye(128, dtype=np.float32)
    w["c_idenb"] = np.eye(128, dtype=np.float32).astype(BF)
    w["c_onesbT"] = np.ones((1, 128), BF)
    w["c_onesfT"] = np.ones((1, 128), np.float32)
    w["c_onescol"] = np.ones((128, 1), np.float32)
    ccs = np.zeros((128, 8), np.float32)
    ccs[:, 0] = 1e-5
    ccs[:, 1] = 1e-6
    ccs[:, 2] = np.finfo(np.float32).eps
    ccs[:, 3] = 1e-5
    ccs[:, 4] = 1.0
    w["c_cc"] = ccs
    bd = np.zeros((128, 8, 16), np.float32)
    for ti in range(8):
        bd[np.arange(128), ti, 2 * ti + np.arange(128) // 64] = 1.0
    w["c_bd"] = _cc(bd.reshape(128, 128))
    pat = np.zeros((8, 128), np.float32)
    pat[np.arange(128) // 16, np.arange(128)] = 1.0
    w["c_pat8"] = pat.astype(BF)
    dm = np.zeros((128, 4, 512), np.float32)
    for r in range(4):
        dm[:, r, :] = (np.arange(128)[:, None] + 128 * r <= np.arange(512)[None, :])
    w["c_dmask"] = _cc(dm.reshape(128, 4 * 512))
    pos = np.arange(T, dtype=np.float64)
    inv_freq = 1.0 / 10000.0 ** (np.arange(0, 128, 2, dtype=np.float64) / 128.0)
    ang = inv_freq[:, None] * pos[None, :]
    w["c_cos"] = np.cos(ang).astype(np.float32)
    w["c_sin"] = np.sin(ang).astype(np.float32)
    w["c_cosq"] = (np.cos(ang) * (128 ** -0.5)).astype(np.float32)
    w["c_sinq"] = (np.sin(ang) * (128 ** -0.5)).astype(np.float32)

    x = np.asarray(inputs["L0_attn_a_x"])
    xs = [_cc(x[c].T[PERM_D]) for c in range(x.shape[0])]
    return w, xs

def host_post(ys, T):
    out = np.empty((len(ys), T, D), np.float32)
    for c, y in enumerate(ys):
        out[c][:, PERM_D] = y.T
    return out


ALL_SEMS = ["qS", "qA", "qG", "sP", "sV", "sA", "sG"]

class Bld:
    def __init__(self, nc):
        self.nc = nc
        self.sem = {}
        self.cnt = {}
        for name in ALL_SEMS:
            self.sem[name] = nc.alloc_semaphore(name=name)
            self.cnt[name] = 0
        self.Q = {"qS": nc.sync, "qA": nc.scalar, "qG": nc.gpsimd}
        self.engs = [nc.sync, nc.scalar, nc.vector, nc.tensor, nc.gpsimd]

    def dma(self, q, out, in_, **kw):
        self.Q[q].dma_start(out=out, in_=in_, **kw).then_inc(self.sem[q], 16)
        self.cnt[q] += 16
        return self.cnt[q]

    def inc(self, ins, s):
        ins.then_inc(self.sem[s], 1)
        self.cnt[s] += 1
        return self.cnt[s]

    def dv(self, ins):
        c = self.inc(ins, "sV")
        self.nc.vector.wait_ge(self.sem["sV"], c)
        return c

    def wait(self, eng, s, val):
        if val > 0:
            eng.wait_ge(self.sem[s], val)

    def snap(self):
        return dict(self.cnt)

    def barrier(self):
        pt = self.snap()
        for eng in self.engs:
            for s in ALL_SEMS:
                self.wait(eng, s, pt[s])
        return pt


def build_nc(T, phases=None, debug_out=False, dbg_delta=None, dbg_dq=False):
    nc = bass.Bass("TRN2", target_bir_lowering=False, debug=False,
                   detect_race_conditions=False)
    b = Bld(nc)
    NT = T // NTT
    NCH = T // 128

    dram = {}
    def din(name, shape, dt):
        dram[name] = nc.dram_tensor(name, list(shape), dt, kind="ExternalInput").ap()
    def dscr(name, shape, dt):
        kind = "ExternalOutput" if debug_out else "Internal"
        dram[name] = nc.dram_tensor(name, list(shape), dt, kind=kind).ap()

    din("x_fm", [D, T], F32)
    dram["y_fm"] = nc.dram_tensor("y_fm", [D, T], F32, kind="ExternalOutput").ap()

    for nm, sh, dt in [
        ("aa_qkw", [D, 2048], F32), ("aa_vgw", [D, 4096], F32), ("aa_ow", [2048, D], F32),
        ("aa_abw", [D, 32], F32), ("aa_vecT", [16, 4], F32), ("aa_conv", [4096, 4], F32),
        ("ab_wq", [D, 2048], F32), ("ab_wk", [D, 2048], F32), ("ab_wv", [D, 2048], F32),
        ("ab_wg", [D, 2048], F32), ("ab_wo", [2048, D], F32), ("ab_ws1", [D, 16], F32),
        ("ab_ws2T", [16, 1024], F32), ("ab_bs2T", [64, 16], F32), ("ab_gnw", [128, 1], F32),
        ("f_gp", [1024, NMLP], I32), ("f_up", [1024, NMLP], I32), ("f_dp", [8192, D], I32),
        ("f_gs", [64, NMLP], F32), ("f_us", [64, NMLP], F32), ("f_ds", [512, D], F32),
        ("c_mstrict", [128, 128], F32), ("c_mincl", [128, 128], F32),
        ("c_idenf", [128, 128], F32), ("c_idenb", [128, 128], BF16),
        ("c_onesbT", [1, 128], BF16), ("c_onesfT", [1, 128], F32),
        ("c_onescol", [128, 1], F32), ("c_cc", [128, 8], F32),
        ("c_bd", [128, 128], F32), ("c_pat8", [8, 128], BF16),
        ("c_dmask", [128, 4 * 512], F32),
        ("c_cos", [64, T], F32), ("c_sin", [64, T], F32),
        ("c_cosq", [64, T], F32), ("c_sinq", [64, T], F32),
    ]:
        din(nm, sh, dt)

    dscr("xa", [D, T], F32)
    dscr("xb", [D, T], F32)
    dscr("ln_s", [D, T], BF16)
    dscr("qk_s", [2048, T], BF16)
    dscr("qkv_s", [4096, T], BF16)
    dscr("vg_s", [4096, T], BF16)
    dscr("o_s", [2048, T], BF16)
    dscr("inter_s", [NMLP, T], BF16)
    dscr("wg_bf", [128, 16 * NMLP], BF16)
    dscr("wu_bf", [128, 16 * NMLP], BF16)
    dscr("wd_bf", [NMLP, D], BF16)

    sb = {}
    def asb(name, shape, dt):
        sb[name] = nc.alloc_sbuf_tensor(name, list(shape), dt).ap()
        return sb[name]

    for nm, sh, dt in [
        ("k_mstrict", [128, 128], F32), ("k_mincl", [128, 128], F32),
        ("k_idenf", [128, 128], F32), ("k_idenb", [128, 128], BF16),
        ("k_onesbT", [1, 128], BF16), ("k_onesfT", [1, 128], F32),
        ("k_onescol", [128, 1], F32), ("k_cc", [128, 8], F32),
        ("k_bd", [128, 128], F32), ("k_pat8", [8, 128], BF16),
        ("k_dmask", [128, 4 * 512], F32),
        ("k_vecT", [16, 4], F32), ("k_conv", [128, 32 * 4], F32),
        ("k_bs2T", [64, 16], F32), ("k_gnw", [128, 1], F32),
        ("k_ws2Tb", [16, 1024], BF16),
        ("k_negeA", [16, 1], F32), ("k_bbias", [16, 1], F32),
    ]:
        asb(nm, sh, dt)

    rows_stack = ExitStack()
    def ensure_rows():
        if "r_beta" not in sb:
            for nm, sh, dt in [("r_beta", [16, T], F32), ("r_g", [16, T], F32),
                               ("r_ssq", [64, T], F32)]:
                sb[nm] = rows_stack.enter_context(nc.sbuf_tensor(nm, sh, dt)).ap()

    ph = list(phases) if phases is not None else [
        "init", "dequant", "ln1", "aproj", "abeta", "aconv", "adelta", "aout",
        "ln2", "ffn1", "ln3", "battn", "bout", "ln4", "ffn2"]

    if debug_out:
        din("xa_init", [D, T], F32)
        din("xb_init", [D, T], F32)

    def emit_init():
        tmp = nc.alloc_sbuf_tensor("i_ws2T", [16, 1024], F32).ap()
        for src, dst in [("c_mstrict", "k_mstrict"), ("c_mincl", "k_mincl"),
                         ("c_idenf", "k_idenf"), ("c_idenb", "k_idenb"),
                         ("c_onesbT", "k_onesbT"), ("c_onesfT", "k_onesfT"),
                         ("c_onescol", "k_onescol"), ("c_cc", "k_cc"),
                         ("c_bd", "k_bd"), ("c_pat8", "k_pat8"), ("c_dmask", "k_dmask"),
                         ("aa_vecT", "k_vecT"), ("ab_bs2T", "k_bs2T"), ("ab_gnw", "k_gnw")]:
            b.dma("qS", sb[dst], dram[src])
        b.dma("qS", tmp, dram["ab_ws2T"])
        b.dma("qS", sb["k_conv"].rearrange("p (a c) -> p a c", c=4),
              dram["aa_conv"].rearrange("(a p) c -> p a c", p=128))
        ld0 = b.cnt["qS"]
        b.wait(nc.scalar, "qS", ld0)
        nc.scalar.activation(out=sb["k_negeA"], in_=sb["k_vecT"][:, 2:3], func=AF.Exp)
        b.inc(nc.scalar.activation(out=sb["k_ws2Tb"], in_=tmp, func=AF.Copy), "sA")
        b.wait(nc.vector, "sA", b.cnt["sA"])
        nc.vector.tensor_scalar_mul(out=sb["k_negeA"], in0=sb["k_negeA"], scalar1=-1.0)
        b.inc(nc.vector.tensor_add(out=sb["k_bbias"], in0=sb["k_vecT"][:, 0:1],
                                   in1=sb["k_vecT"][:, 1:2]), "sV")
        if debug_out:
            b.dma("qS", dram["xa"], dram["xa_init"])
            b.dma("qS", dram["xb"], dram["xb_init"])

    def emit_ln(xsrc, tag):
        st = ExitStack()
        pX = [st.enter_context(nc.sbuf_tensor(f"{tag}_x{i}", [128, 16 * NTT], F32)).ap()
              for i in range(2)]
        sq = [st.enter_context(nc.sbuf_tensor(f"{tag}_sq{i}", [128, NTT], F32)).ap()
              for i in range(2)]
        rows = st.enter_context(nc.sbuf_tensor(f"{tag}_rows", [1, 5 * NTT], F32)).ap()
        lno = [st.enter_context(nc.sbuf_tensor(f"{tag}_o{i}", [128, NTT], BF16)).ap()
               for i in range(2)]
        t1 = st.enter_context(nc.sbuf_tensor(f"{tag}_t1", [128, NTT], F32)).ap()
        ps_sum = st.enter_context(nc.psum_tensor(f"{tag}_psum", [1, NTT], F32)).ap()
        ps_sq = st.enter_context(nc.psum_tensor(f"{tag}_psq", [1, NTT], F32)).ap()
        ps_mb = st.enter_context(nc.psum_tensor(f"{tag}_pmb", [128, NTT], F32)).ap()
        ps_rb = st.enter_context(nc.psum_tensor(f"{tag}_prb", [128, NTT], F32)).ap()
        norm_done = [0] * NT
        row_done = [0] * NT
        for tt in range(NT):
            buf = pX[tt % 2]
            if tt >= 2:
                b.wait(nc.sync, "sV", norm_done[tt - 2])
            for c in range(16):
                b.dma("qS", buf[:, c * NTT:(c + 1) * NTT],
                      dram[xsrc][c * 128:(c + 1) * 128, tt * NTT:(tt + 1) * NTT])
            lq = b.cnt["qS"]
            b.wait(nc.scalar, "qS", lq)
            b.wait(nc.tensor, "qS", lq)
            if tt >= 1:
                b.wait(nc.tensor, "sV", row_done[tt - 1])
            sA0 = b.cnt["sA"]
            sP0 = b.cnt["sP"]
            for c in range(16):
                if c >= 2:
                    b.wait(nc.scalar, "sP", sP0 + c - 1)
                b.inc(nc.scalar.activation(out=sq[c % 2], in_=buf[:, c * NTT:(c + 1) * NTT],
                                           func=AF.Square), "sA")
                nc.tensor.matmul(ps_sum, sb["k_onescol"], buf[:, c * NTT:(c + 1) * NTT],
                                 start=(c == 0), stop=(c == 15))
                b.wait(nc.tensor, "sA", sA0 + c + 1)
                mm = nc.tensor.matmul(ps_sq, sb["k_onescol"], sq[c % 2],
                                      start=(c == 0), stop=(c == 15))
                b.inc(mm, "sP")
            stats_done = b.cnt["sP"]
            b.wait(nc.vector, "sP", stats_done)
            mean, mean2, var, lnv, rstd = (rows[:, i * NTT:(i + 1) * NTT] for i in range(5))
            nc.vector.tensor_scalar_mul(out=mean, in0=ps_sum, scalar1=1.0 / D)
            nc.vector.tensor_tensor(out=mean2, in0=mean, in1=mean, op=OP.mult)
            rd = b.inc(nc.vector.scalar_tensor_tensor(out=var, in0=ps_sq, scalar=1.0 / D,
                                                      in1=mean2, op0=OP.mult, op1=OP.subtract), "sV")
            row_done[tt] = rd
            b.wait(nc.scalar, "sV", rd)
            nc.scalar.activation(out=lnv, in_=var, func=AF.Ln, bias=sb["k_cc"][0:1, 0:1])
            rs = b.inc(nc.scalar.activation(out=rstd, in_=lnv, func=AF.Exp, scale=-0.5), "sA")
            b.wait(nc.tensor, "sA", rs)
            if tt >= 1:
                b.wait(nc.tensor, "sV", norm_done[tt - 1])
            nc.tensor.matmul(ps_mb, sb["k_onesfT"], mean, start=True, stop=True)
            bc = b.inc(nc.tensor.matmul(ps_rb, sb["k_onesfT"], rstd, start=True, stop=True), "sP")
            b.wait(nc.vector, "sP", bc)
            for c in range(16):
                if c >= 2:
                    b.wait(nc.vector, "qG", b.cnt["qG"] - 16)
                nc.vector.tensor_sub(out=t1, in0=buf[:, c * NTT:(c + 1) * NTT], in1=ps_mb)
                v = b.inc(nc.vector.tensor_mul(out=lno[c % 2], in0=t1, in1=ps_rb), "sV")
                b.wait(nc.gpsimd, "sV", v)
                b.dma("qG", dram["ln_s"][c * 128:(c + 1) * 128, tt * NTT:(tt + 1) * NTT],
                      lno[c % 2])
            norm_done[tt] = b.cnt["sV"]
        st.close()

    def emit_proj(wname, act_src, Kdim, Dout, tag, evac, act_rows0=0):
        st = ExitStack()
        KC = Kdim // 128
        OC = Dout // 128
        wld = [st.enter_context(nc.sbuf_tensor(f"{tag}_wl{i}", [128, KC * 128], F32)).ap()
               for i in range(2)]
        wbf = st.enter_context(nc.sbuf_tensor(f"{tag}_wb", [128, OC * KC * 128], BF16)).ap()
        pA = [st.enter_context(nc.sbuf_tensor(f"{tag}_a{i}", [128, KC * NTT], BF16)).ap()
              for i in range(2)]
        pp = [st.enter_context(nc.psum_tensor(f"{tag}_p{i}", [128, NTT], F32)).ap()
              for i in range(2)]
        cvt_done = [0] * OC
        for oc in range(OC):
            if oc >= 2:
                b.wait(nc.sync, "sA", cvt_done[oc - 2])
            for kc in range(KC):
                b.dma("qS", wld[oc % 2][:, kc * 128:(kc + 1) * 128],
                      dram[wname][kc * 128:(kc + 1) * 128, oc * 128:(oc + 1) * 128])
            b.wait(nc.scalar, "qS", b.cnt["qS"])
            cvt_done[oc] = b.inc(
                nc.scalar.activation(out=wbf[:, oc * KC * 128:(oc + 1) * KC * 128],
                                     in_=wld[oc % 2], func=AF.Copy), "sA")
        wdone = b.cnt["sA"]
        evac_done = [0] * NT
        for tt in range(NT):
            abuf = pA[tt % 2]
            if tt >= 2:
                b.wait(nc.sync, "sP", evac_done[tt - 2])
            for c in range(KC):
                b.dma("qS", abuf[:, c * NTT:(c + 1) * NTT],
                      dram[act_src][(act_rows0 + c * 128):(act_rows0 + (c + 1) * 128),
                                    tt * NTT:(tt + 1) * NTT])
            lq = b.cnt["qS"]
            b.wait(nc.tensor, "qS", lq)
            if tt == 0:
                b.wait(nc.tensor, "sA", wdone)
            for oc in range(OC):
                ps = pp[oc % 2]
                if oc >= 2:
                    b.wait(nc.tensor, "sV", b.evac_v[(tt, oc - 2)])
                elif tt > 0 and oc == 0:
                    b.wait(nc.tensor, "sV", b.evac_v[(tt - 1, OC - 2)])
                elif tt > 0:
                    b.wait(nc.tensor, "sV", b.evac_v[(tt - 1, OC - 1)])
                wb = wbf[:, oc * KC * 128:(oc + 1) * KC * 128]
                for kc in range(KC):
                    mm = nc.tensor.matmul(ps, wb[:, kc * 128:(kc + 1) * 128],
                                          abuf[:, kc * NTT:(kc + 1) * NTT],
                                          start=(kc == 0), stop=(kc == KC - 1))
                mp = b.inc(mm, "sP")
                b.wait(nc.vector, "sP", mp)
                evac(oc, tt, ps)
                b.evac_v[(tt, oc)] = b.cnt["sV"]
            evac_done[tt] = b.cnt["sP"]
        st.close()

    def emit_aproj():
        st = ExitStack()
        stage = [st.enter_context(nc.sbuf_tensor(f"aq_s{i}", [128, NTT], BF16)).ap()
                 for i in range(2)]
        def mk_evac(dst):
            def evac(oc, tt, ps):
                s = stage[oc % 2]
                key = oc % 2
                if key in b.stage_dma:
                    b.wait(nc.vector, "qG", b.stage_dma[key])
                v = b.inc(nc.vector.tensor_copy(out=s, in_=ps), "sV")
                b.wait(nc.gpsimd, "sV", v)
                b.stage_dma[key] = b.dma(
                    "qG", dram[dst][oc * 128:(oc + 1) * 128, tt * NTT:(tt + 1) * NTT], s)
            return evac
        b.stage_dma = {}
        b.evac_v = {}
        emit_proj("aa_qkw", "ln_s", D, 2048, "aqk", mk_evac("qk_s"))
        b.barrier()
        b.evac_v = {}
        emit_proj("aa_vgw", "ln_s", D, 4096, "avg", mk_evac("vg_s"))
        st.close()

    def emit_abeta():
        ensure_rows()
        st = ExitStack()
        wl = st.enter_context(nc.sbuf_tensor("ab_wl", [128, 16 * 32], F32)).ap()
        wb = st.enter_context(nc.sbuf_tensor("ab_wb", [128, 16 * 32], BF16)).ap()
        aT = [st.enter_context(nc.sbuf_tensor(f"ab_a{i}", [128, 16 * NTT], BF16)).ap()
              for i in range(2)]
        xsb = st.enter_context(nc.sbuf_tensor("ab_x", [16, NTT], F32)).ap()
        sp = st.enter_context(nc.sbuf_tensor("ab_sp", [16, NTT], F32)).ap()
        l1 = st.enter_context(nc.sbuf_tensor("ab_l1", [16, NTT], F32)).ap()
        ps_b = st.enter_context(nc.psum_tensor("ab_pb", [16, NTT], F32)).ap()
        ps_a = st.enter_context(nc.psum_tensor("ab_pa", [16, NTT], F32)).ap()
        for kc in range(16):
            b.dma("qS", wl[:, kc * 32:(kc + 1) * 32],
                  dram["aa_abw"][kc * 128:(kc + 1) * 128, :])
        b.wait(nc.scalar, "qS", b.cnt["qS"])
        wc = b.inc(nc.scalar.activation(out=wb, in_=wl, func=AF.Copy), "sA")
        b.wait(nc.tensor, "sA", wc)
        prev_done = 0
        for tt in range(NT):
            abuf = aT[tt % 2]
            if tt >= 2:
                b.wait(nc.sync, "sP", b.aprev[tt - 2])
            for c in range(16):
                b.dma("qS", abuf[:, c * NTT:(c + 1) * NTT],
                      dram["ln_s"][c * 128:(c + 1) * 128, tt * NTT:(tt + 1) * NTT])
            b.wait(nc.tensor, "qS", b.cnt["qS"])
            b.wait(nc.tensor, "sA", prev_done)
            for kc in range(16):
                w2 = wb[:, kc * 32:(kc + 1) * 32]
                nc.tensor.matmul(ps_b, w2[:, 0:16], abuf[:, kc * NTT:(kc + 1) * NTT],
                                 start=(kc == 0), stop=(kc == 15))
                mm = nc.tensor.matmul(ps_a, w2[:, 16:32], abuf[:, kc * NTT:(kc + 1) * NTT],
                                      start=(kc == 0), stop=(kc == 15))
            mp = b.inc(mm, "sP")
            b.aprev = getattr(b, "aprev", [0] * NT)
            b.aprev[tt] = mp
            b.wait(nc.scalar, "sP", mp)
            sg = b.inc(nc.scalar.activation(out=sp, in_=ps_b, func=AF.Sigmoid,
                                            bias=sb["k_bbias"]), "sA")
            b.wait(nc.vector, "sA", sg)
            b.inc(nc.vector.tensor_scalar_mul(out=sb["r_beta"][:, tt * NTT:(tt + 1) * NTT],
                                              in0=sp, scalar1=2.0), "sV")
            nc.scalar.activation(out=xsb, in_=ps_a, func=AF.Identity,
                                 bias=sb["k_vecT"][:, 3:4])
            nc.scalar.activation(out=sp, in_=xsb, func=AF.Abs)
            nc.scalar.activation(out=sp, in_=sp, func=AF.Exp, scale=-1.0)
            pd = b.inc(nc.scalar.activation(out=l1, in_=sp, func=AF.Ln,
                                            bias=sb["k_cc"][0:16, 4:5]), "sA")
            prev_done = pd
            b.wait(nc.vector, "sA", pd)
            nc.vector.tensor_scalar_max(out=xsb, in0=xsb, scalar1=0.0)
            nc.vector.tensor_add(out=xsb, in0=xsb, in1=l1)
            b.inc(nc.vector.tensor_scalar_mul(out=sb["r_g"][:, tt * NTT:(tt + 1) * NTT],
                                              in0=xsb, scalar1=sb["k_negeA"]), "sV")
        st.close()

    def emit_aconv():
        ensure_rows()
        st = ExitStack()
        xin = [st.enter_context(nc.sbuf_tensor(f"cv_x{i}", [128, NTT + 3], BF16)).ap()
               for i in range(2)]
        acc = st.enter_context(nc.sbuf_tensor("cv_acc", [128, NTT], F32)).ap()
        acc2 = st.enter_context(nc.sbuf_tensor("cv_acc2", [128, NTT], F32)).ap()
        sil = st.enter_context(nc.sbuf_tensor("cv_sil", [128, NTT], F32)).ap()
        silb = [st.enter_context(nc.sbuf_tensor(f"cv_sb{i}", [128, NTT], BF16)).ap()
                for i in range(2)]
        sq = st.enter_context(nc.sbuf_tensor("cv_sq", [128, NTT], F32)).ap()
        ps_q = st.enter_context(nc.psum_tensor("cv_psq", [16, NTT], F32)).ap()
        ps_k = st.enter_context(nc.psum_tensor("cv_psk", [16, NTT], F32)).ap()
        sq = [sq, st.enter_context(nc.sbuf_tensor("cv_sq2", [128, NTT], F32)).ap()]
        rowcopy_done = 0
        dma_done = {}
        for tt in range(NT):
            t0 = tt * NTT
            for ct in range(32):
                src = "qk_s" if ct < 16 else "vg_s"
                row0 = ct * 128 if ct < 16 else (ct - 16) * 128
                dst_row0 = ct * 128
                buf = xin[ct % 2]
                b.wait(nc.sync, "sV", getattr(b, "cv_done", 0))
                if tt == 0:
                    b.wait(nc.vector, "qS", b.cnt["qS"])
                    mz = b.inc(nc.vector.memset(buf[:, 0:3], 0.0), "sV")
                    b.wait(nc.sync, "sV", mz)
                    b.dma("qS", buf[:, 3:NTT + 3],
                          dram[src][row0:row0 + 128, t0:t0 + NTT])
                else:
                    b.dma("qS", buf[:, 0:NTT + 3],
                          dram[src][row0:row0 + 128, t0 - 3:t0 + NTT])
                lq = b.cnt["qS"]
                b.wait(nc.vector, "qS", lq)
                wcol = lambda tau: sb["k_conv"][:, ct * 4 + tau:ct * 4 + tau + 1]
                nc.vector.tensor_scalar_mul(out=acc, in0=buf[:, 0:NTT], scalar1=wcol(0))
                nc.vector.tensor_scalar_mul(out=acc2, in0=buf[:, 1:NTT + 1], scalar1=wcol(1))
                nc.vector.tensor_add(out=acc, in0=acc, in1=acc2)
                nc.vector.tensor_scalar_mul(out=acc2, in0=buf[:, 2:NTT + 2], scalar1=wcol(2))
                nc.vector.tensor_add(out=acc, in0=acc, in1=acc2)
                nc.vector.tensor_scalar_mul(out=acc2, in0=buf[:, 3:NTT + 3], scalar1=wcol(3))
                cd = b.inc(nc.vector.tensor_add(out=acc, in0=acc, in1=acc2), "sV")
                b.cv_done = cd
                b.wait(nc.scalar, "sV", cd)
                if (tt, ct - 2) in dma_done:
                    b.wait(nc.scalar, "qG", dma_done[(tt, ct - 2)])
                    b.wait(nc.vector, "qG", dma_done[(tt, ct - 2)])
                if ct < 16:
                    ps = ps_q if ct < 8 else ps_k
                    sl = b.inc(nc.scalar.activation(out=sil, in_=acc, func=AF.Silu), "sA")
                    b.wait(nc.vector, "sA", sl)
                    b.inc(nc.vector.tensor_copy(out=silb[ct % 2], in_=sil), "sV")
                    if ct >= 2:
                        b.wait(nc.scalar, "sP", b.cnt["sP"])
                    sqd = b.inc(nc.scalar.activation(out=sq[ct % 2], in_=sil,
                                                     func=AF.Square), "sA")
                    b.wait(nc.tensor, "sA", sqd)
                    if ct % 8 == 0:
                        b.wait(nc.tensor, "sV", rowcopy_done)
                    mm = nc.tensor.matmul(ps, sb["k_bd"][:, (ct % 8) * 16:(ct % 8) * 16 + 16],
                                          sq[ct % 2], start=(ct % 8 == 0), stop=(ct % 8 == 7))
                    mp = b.inc(mm, "sP")
                    if ct % 8 == 7:
                        b.wait(nc.vector, "sP", mp)
                        roff = 0 if ct < 8 else 32
                        rowcopy_done = b.inc(nc.vector.tensor_copy(
                            out=sb["r_ssq"][roff:roff + 16, t0:t0 + NTT], in_=ps), "sV")
                else:
                    b.inc(nc.scalar.activation(out=silb[ct % 2], in_=acc, func=AF.Silu), "sA")
                wsem = "sV" if ct < 16 else "sA"
                b.wait(nc.gpsimd, wsem, b.cnt[wsem])
                dma_done[(tt, ct)] = b.dma("qG", dram["qkv_s"][dst_row0:dst_row0 + 128,
                                                               t0:t0 + NTT], silb[ct % 2])
        st.close()

    def emit_adelta():
        st = ExitStack()
        if dbg_delta is not None:
            dram["ddump"] = nc.dram_tensor("ddump", [128, 24 * 128], F32,
                                           kind="ExternalOutput").ap()
            ddmp = st.enter_context(nc.sbuf_tensor("ad_ddmp", [128, 24 * 128], F32)).ap()
        def dcap(n, h, slot, ap):
            if dbg_delta is not None and (n, h) == dbg_delta:
                nc.vector.tensor_copy(out=ddmp[0:ap.shape[0], slot * 128:
                                               slot * 128 + ap.shape[1]], in_=ap)
        Sst = st.enter_context(nc.sbuf_tensor("ad_S", [64, H * DV], F32)).ap()
        qch = [st.enter_context(nc.sbuf_tensor(f"ad_q{i}", [128, 8 * 128], BF16)).ap()
               for i in range(2)]
        kch = [st.enter_context(nc.sbuf_tensor(f"ad_k{i}", [128, 8 * 128], BF16)).ap()
               for i in range(2)]
        vch = [st.enter_context(nc.sbuf_tensor(f"ad_v{i}", [128, H * 128], BF16)).ap()
               for i in range(2)]
        gch = [st.enter_context(nc.sbuf_tensor(f"ad_g{i}", [128, H * 128], BF16)).ap()
               for i in range(2)]
        och = [st.enter_context(nc.sbuf_tensor(f"ad_o{i}", [128, H * 128], BF16)).ap()
               for i in range(2)]
        cols = {}
        for nm in ["gT", "ngT", "aT", "bT", "rkT", "rqT", "ccT", "barkT",
                   "rqsT", "oi2T", "gam"]:
            cols[nm] = st.enter_context(nc.sbuf_tensor(f"ad_{nm}", [128, 16], F32)).ap()
        aCbc = st.enter_context(nc.sbuf_tensor("ad_aCbc", [64, 16], F32)).ap()
        aCr = st.enter_context(nc.sbuf_tensor("ad_aCr", [1, 16], F32)).ap()
        gchunk = st.enter_context(nc.sbuf_tensor("ad_gch", [16, 128], F32)).ap()

        def mk2(nm, sh, dt):
            return [st.enter_context(nc.sbuf_tensor(f"ad_{nm}{i}", sh, dt)).ap()
                    for i in range(2)]
        D0 = mk2("D0", [128, 128], F32)
        S0b = mk2("S0b", [128, 128], BF16)
        Zs = mk2("Zs", [128, 128], BF16)
        Ys = mk2("Ys", [128, 128], BF16)
        Yn = mk2("Yn", [128, 128], BF16)
        Pqs = mk2("Pqs", [128, 128], BF16)
        Y2s = mk2("Y2s", [128, 128], BF16)
        Y2n = mk2("Y2n", [128, 128], BF16)
        Ps = mk2("Ps", [128, 128], BF16)
        Rb = mk2("Rb", [128, 128], BF16)
        KS0s = mk2("KS0s", [128, 128], F32)
        Zrb = mk2("Zrb", [128, 128], BF16)
        zcs = mk2("zcs", [128, 128], F32)
        Ub = mk2("Ub", [128, 128], BF16)
        OIs = mk2("OIs", [128, 128], F32)
        Of = mk2("Of", [128, 128], F32)
        Onb = mk2("Onb", [128, 128], BF16)
        Klam = mk2("Klam", [128, 64], BF16)
        S0sc = mk2("S0sc", [64, 128], F32)
        osq = mk2("osq", [128, 1], F32)
        lamrk = mk2("lamrk", [128, 1], F32)
        gsil = mk2("gsil", [128, 128], F32)
        Gm = mk2("Gm", [128, 128], F32)
        def mk2p(nm, sh, dt):
            return [st.enter_context(nc.psum_tensor(f"ad_{nm}{i}", sh, dt)).ap()
                    for i in range(2)]
        pfA = mk2p("pfA", [128, 512], F32)
        pfB = mk2p("pfB", [128, 512], F32)
        pbb = mk2p("pbb", [128, 512], BF16)
        pC = st.enter_context(nc.psum_tensor("ad_pC", [128, 512], F32)).ap()
        pCol = pC[:, 0:16]
        pRow = pC[0:1, 16:32]
        pBc = pC[0:64, 32:48]

        b.inc(nc.vector.memset(Sst, 0.0), "sV")
        if dbg_delta is not None:
            b.inc(nc.vector.memset(ddmp, 0.0), "sV")
        qkv_pl = dram["qkv_s"].rearrange("(a p) t -> p a t", p=128)

        def QcS(h, n):
            a = h // 2
            r0 = (h % 2) * 64
            return qch[n % 2][r0:r0 + 64, a * 128:(a + 1) * 128]
        def KcS(h, n):
            a = h // 2
            r0 = (h % 2) * 64
            return kch[n % 2][r0:r0 + 64, a * 128:(a + 1) * 128]

        veV = [0, 0]
        veA = [0, 0]
        peP = [0, 0]
        prev_o_dma = [0, 0]
        vfree = [0] * (NCH + 2)
        colfree = [0] * (NCH + 2)

        for n in range(NCH):
            par = n % 2
            t0 = n * 128
            if n >= 2:
                b.wait(nc.sync, "sV", b.ad_ochfree[n - 2])
            b.dma("qS", qch[par].rearrange("p (a t) -> p a t", t=128),
                  qkv_pl[:, 0:8, t0:t0 + 128])
            b.dma("qS", kch[par].rearrange("p (a t) -> p a t", t=128),
                  qkv_pl[:, 8:16, t0:t0 + 128])
            for half in range(2):
                b.dma("qS", vch[par][:, half * 1024:(half + 1) * 1024]
                      .rearrange("p (a t) -> p a t", t=128),
                      qkv_pl[:, 16 + 8 * half:16 + 8 * (half + 1), t0:t0 + 128])
            vg_pl = dram["vg_s"].rearrange("(a p) t -> p a t", p=128)
            for half in range(2):
                b.dma("qS", gch[par][:, half * 1024:(half + 1) * 1024]
                      .rearrange("p (a t) -> p a t", t=128),
                      vg_pl[:, 16 + 8 * half:16 + 8 * (half + 1), t0:t0 + 128])
            vg_ld = b.cnt["qS"]
            b.wait(nc.vector, "sP", colfree[n])
            sc = b.inc(nc.vector.tensor_tensor_scan(
                out=gchunk, data0=sb["r_g"][:, t0:t0 + 128],
                data1=sb["r_g"][:, t0:t0 + 128], initial=0.0,
                op0=OP.add, op1=OP.bypass), "sV")
            b.wait(nc.tensor, "sV", sc)
            b.wait(nc.tensor, "sA", veA[0])
            b.wait(nc.tensor, "sA", veA[1])
            tr1 = b.inc(nc.tensor.transpose(pCol, gchunk, sb["k_idenf"][0:16, 0:16]), "sP")
            b.wait(nc.vector, "sP", tr1)
            b.dv(nc.vector.tensor_copy(out=cols["gT"], in_=pCol))
            gv = b.inc(nc.vector.tensor_scalar_mul(out=cols["ngT"], in0=cols["gT"],
                                                   scalar1=-1.0), "sV")
            b.wait(nc.scalar, "sV", gv)
            aTd = b.inc(nc.scalar.activation(out=cols["aT"], in_=cols["gT"], func=AF.Exp), "sA")
            gcC = b.inc(nc.scalar.activation(out=cols["gam"][0:16, 0:1],
                                             in_=gchunk[:, 127:128], func=AF.Exp), "sA")
            b.wait(nc.tensor, "sA", gcC)
            tr2 = b.inc(nc.tensor.transpose(pRow, cols["gam"][0:16, 0:1],
                                            sb["k_idenf"][0:16, 0:16]), "sP")
            b.wait(nc.vector, "sP", tr2)
            rw = b.inc(nc.vector.tensor_copy(out=aCr, in_=pRow), "sV")
            b.wait(nc.tensor, "sV", rw)
            bc = b.inc(nc.tensor.matmul(pBc, sb["k_onesfT"][:, 0:64], aCr,
                                        start=True, stop=True), "sP")
            tr3 = b.inc(nc.tensor.transpose(pCol, sb["r_beta"][:, t0:t0 + 128],
                                            sb["k_idenf"][0:16, 0:16]), "sP")
            b.wait(nc.vector, "sP", tr3)
            nc.vector.tensor_copy(out=aCbc, in_=pBc)
            btd = b.dv(nc.vector.tensor_copy(out=cols["bT"], in_=pCol))
            b.wait(nc.tensor, "sV", btd)
            tr4 = b.inc(nc.tensor.transpose(pCol, sb["r_ssq"][0:16, t0:t0 + 128],
                                            sb["k_idenf"][0:16, 0:16]), "sP")
            b.wait(nc.scalar, "sP", tr4)
            nc.scalar.activation(out=cols["rqT"], in_=pCol, func=AF.Ln,
                                 bias=sb["k_cc"][:, 1:2])
            rqd = b.inc(nc.scalar.activation(out=cols["rqT"], in_=cols["rqT"],
                                             func=AF.Exp, scale=-0.5), "sA")
            b.wait(nc.tensor, "sA", rqd)
            tr5 = b.inc(nc.tensor.transpose(pCol, sb["r_ssq"][32:48, t0:t0 + 128],
                                            sb["k_idenf"][32:48, 32:48]), "sP")
            b.wait(nc.scalar, "sP", tr5)
            nc.scalar.activation(out=cols["rkT"], in_=pCol, func=AF.Ln,
                                 bias=sb["k_cc"][:, 1:2])
            rkd = b.inc(nc.scalar.activation(out=cols["rkT"], in_=cols["rkT"],
                                             func=AF.Exp, scale=-0.5), "sA")
            b.wait(nc.vector, "sA", rkd)
            b.dv(nc.vector.tensor_mul(out=cols["ccT"], in0=cols["bT"], in1=cols["rkT"]))
            nc.vector.tensor_mul(out=cols["barkT"], in0=cols["ccT"], in1=cols["aT"])
            b.dv(nc.vector.tensor_scalar_mul(out=cols["rqsT"], in0=cols["rqT"],
                                             scalar1=SCALE_A))
            cold = b.inc(nc.vector.tensor_mul(out=cols["oi2T"], in0=cols["aT"],
                                              in1=cols["rqsT"]), "sV")
            colfree[n + 1] = b.cnt["sP"]
            b.wait(nc.tensor, "sV", cold)
            b.wait(nc.tensor, "qS", vg_ld)
            b.wait(nc.scalar, "qS", vg_ld)

            for hp in range(8):
                pair = (2 * hp, 2 * hp + 1)

                def ch(h):
                    return cols

                def pe_wait(i):
                    if SERIAL_DELTA:
                        b.wait(nc.tensor, "sV", b.cnt["sV"])
                        b.wait(nc.tensor, "sA", b.cnt["sA"])
                    else:
                        b.wait(nc.tensor, "sV", veV[i])
                        b.wait(nc.tensor, "sA", veA[i])

                def ve_wait(i):
                    b.wait(nc.vector, "sP", b.cnt["sP"] if SERIAL_DELTA else peP[i])
                def va_wait(i):
                    b.wait(nc.scalar, "sP", b.cnt["sP"] if SERIAL_DELTA else peP[i])

                for h in pair:
                    i = h % 2
                    pe_wait(i)
                    nc.tensor.matmul(pfA[i][:, 0:128], KcS(h, n), KcS(h, n),
                                     start=True, stop=True)
                    peP[i] = b.inc(nc.tensor.matmul(pfA[i][:, 128:256], KcS(h, n),
                                                    QcS(h, n), start=True, stop=True), "sP")
                for h in pair:
                    i = h % 2
                    b.wait(nc.scalar, "sV", veV[i])
                    veA[i] = b.inc(nc.scalar.activation(
                        out=D0[i], in_=sb["k_mstrict"], func=AF.Identity, scale=0.0,
                        bias=cols["gT"][:, h:h + 1]), "sA")
                for h in pair:
                    i = h % 2
                    b.wait(nc.tensor, "sA", veA[i])
                    peP[i] = b.inc(nc.tensor.transpose(pfB[i][:, 0:128], D0[i],
                                                       sb["k_idenf"]), "sP")
                for h in pair:
                    i = h % 2
                    ve_wait(i)
                    b.wait(nc.vector, "sA", veA[i])
                    gm = b.inc(nc.vector.tensor_scalar(
                        out=Gm[i], in0=pfB[i][:, 0:128],
                        scalar1=cols["ngT"][:, h:h + 1], scalar2=0.0,
                        op0=OP.add, op1=OP.min), "sV")
                    dcap(n, h, 22, pfB[i][:, 0:128])
                    b.wait(nc.scalar, "sV", gm)
                    d0 = b.inc(nc.scalar.activation(out=D0[i], in_=Gm[i],
                                                    func=AF.Exp), "sA")
                    veA[i] = d0
                    b.wait(nc.vector, "sA", d0)
                    nc.vector.scalar_tensor_tensor(out=Zs[i], in0=pfA[i][:, 0:128],
                                                   scalar=cols["rkT"][:, h:h + 1],
                                                   in1=D0[i], op0=OP.mult, op1=OP.mult)
                    nc.vector.tensor_mul(out=Zs[i], in0=Zs[i], in1=sb["k_mstrict"])
                    nc.vector.tensor_scalar_mul(out=Ys[i], in0=Zs[i],
                                                scalar1=cols["ccT"][:, h:h + 1])
                    nc.vector.scalar_tensor_tensor(out=Pqs[i], in0=pfA[i][:, 128:256],
                                                   scalar=cols["rkT"][:, h:h + 1],
                                                   in1=D0[i], op0=OP.mult, op1=OP.mult)
                    nc.vector.tensor_mul(out=Pqs[i], in0=Pqs[i], in1=sb["k_mincl"])
                    b.dv(nc.vector.tensor_mul(out=lamrk[i], in0=D0[i][:, 127:128],
                                               in1=cols["rkT"][:, h:h + 1]))
                    nc.vector.tensor_copy(out=S0b[i][0:64, :], in_=Sst[:, h * DV:(h + 1) * DV])
                    nc.vector.tensor_copy(out=S0b[i][64:128, :], in_=Sst[:, h * DV:(h + 1) * DV])
                    dcap(n, h, 0, D0[i])
                    dcap(n, h, 1, Zs[i])
                    dcap(n, h, 2, Ys[i])
                    dcap(n, h, 10, Pqs[i])
                    dcap(n, h, 21, Gm[i])
                    veV[i] = b.inc(nc.vector.tensor_sub(out=Ps[i], in0=sb["k_idenb"],
                                                        in1=Ys[i]), "sV")
                for h in pair:
                    i = h % 2
                    pe_wait(i)
                    nc.tensor.transpose(pbb[i][:, 0:128], Ys[i], sb["k_idenb"])
                    r0 = (h % 2) * 64
                    peP[i] = b.inc(nc.tensor.transpose(pbb[i][:, 128:192], KcS(h, n),
                                                       sb["k_idenb"][r0:r0 + 64, r0:r0 + 64]), "sP")
                for h in pair:
                    i = h % 2
                    va_wait(i)
                    nc.scalar.activation(out=Yn[i], in_=pbb[i][:, 0:128], func=AF.Copy)
                    veA[i] = b.inc(nc.scalar.activation(
                        out=Klam[i], in_=pbb[i][:, 128:192], func=AF.Copy,
                        scale=lamrk[i]), "sA")
                cur_s, cur_n = Ys, Yn
                for k in range(6):
                    for h in pair:
                        i = h % 2
                        pe_wait(i)
                        nc.tensor.matmul(pfA[i][:, 0:128], cur_n[i], cur_s[i],
                                         start=True, stop=True)
                        peP[i] = b.inc(nc.tensor.matmul(pfA[i][:, 128:256], cur_s[i],
                                                        cur_n[i], start=True, stop=True), "sP")
                    for h in pair:
                        i = h % 2
                        va_wait(i)
                        nc.scalar.activation(out=Y2s[i], in_=pfA[i][:, 0:128],
                                             func=AF.Copy)
                        veA[i] = b.inc(nc.scalar.activation(
                            out=Y2n[i], in_=pfA[i][:, 128:256], func=AF.Copy), "sA")
                    for h in pair:
                        i = h % 2
                        pe_wait(i)
                        peP[i] = b.inc(nc.tensor.matmul(pfB[i][:, 0:128], Y2n[i], Ps[i],
                                                        start=True, stop=True), "sP")
                    for h in pair:
                        i = h % 2
                        ve_wait(i)
                        veV[i] = b.inc(nc.vector.tensor_add(out=Ps[i], in0=pfB[i][:, 0:128],
                                                            in1=Ps[i]), "sV")
                    cur_s, cur_n = Y2s, Y2n
                for h in pair:
                    i = h % 2
                    pe_wait(i)
                    r0 = (h % 2) * 64
                    nc.tensor.matmul(pfA[i][:, 0:128], KcS(h, n), S0b[i][r0:r0 + 64, :],
                                     start=True, stop=True)
                    peP[i] = b.inc(nc.tensor.transpose(pbb[i][:, 256:384],
                                                       vch[par][:, h * 128:(h + 1) * 128],
                                                       sb["k_idenb"]), "sP")
                for h in pair:
                    i = h % 2
                    va_wait(i)
                    b.wait(nc.scalar, "sV", veV[i])
                    ke = b.inc(nc.scalar.activation(out=KS0s[i], in_=pfA[i][:, 0:128],
                                                    func=AF.Copy,
                                                    scale=cols["barkT"][:, h:h + 1]), "sA")
                    veA[i] = ke
                    ve_wait(i)
                    b.wait(nc.vector, "sA", ke)
                    veV[i] = b.inc(nc.vector.scalar_tensor_tensor(
                        out=Rb[i], in0=pbb[i][:, 256:384],
                        scalar=cols["bT"][:, h:h + 1], in1=KS0s[i],
                        op0=OP.mult, op1=OP.subtract), "sV")
                for h in pair:
                    i = h % 2
                    pe_wait(i)
                    peP[i] = b.inc(nc.tensor.matmul(pfB[i][:, 0:128], Zs[i], Rb[i],
                                                    start=True, stop=True), "sP")
                for h in pair:
                    i = h % 2
                    va_wait(i)
                    veA[i] = b.inc(nc.scalar.activation(out=Zrb[i],
                                                        in_=pfB[i][:, 0:128],
                                                        func=AF.Copy), "sA")
                for h in pair:
                    i = h % 2
                    pe_wait(i)
                    peP[i] = b.inc(nc.tensor.matmul(pfA[i][:, 0:128], Ps[i], Zrb[i],
                                                    start=True, stop=True), "sP")
                for h in pair:
                    i = h % 2
                    va_wait(i)
                    b.wait(nc.scalar, "sV", veV[i])
                    zce = b.inc(nc.scalar.activation(out=zcs[i], in_=pfA[i][:, 0:128],
                                                     func=AF.Copy,
                                                     scale=cols["ccT"][:, h:h + 1]), "sA")
                    veA[i] = zce
                    ve_wait(i)
                    b.wait(nc.vector, "sA", zce)
                    veV[i] = b.inc(nc.vector.tensor_sub(out=Ub[i], in0=Rb[i],
                                                        in1=zcs[i]), "sV")
                for h in pair:
                    i = h % 2
                    pe_wait(i)
                    r0 = (h % 2) * 64
                    nc.tensor.matmul(pfB[i][:, 0:128], QcS(h, n), S0b[i][r0:r0 + 64, :],
                                     start=True, stop=True)
                    peP[i] = b.inc(nc.tensor.matmul(pfB[i][:, 128:256], Pqs[i], Ub[i],
                                                    start=True, stop=True), "sP")
                for h in pair:
                    i = h % 2
                    va_wait(i)
                    b.wait(nc.scalar, "sV", veV[i])
                    oie = b.inc(nc.scalar.activation(out=OIs[i], in_=pfB[i][:, 0:128],
                                                     func=AF.Copy,
                                                     scale=cols["oi2T"][:, h:h + 1]), "sA")
                    ve_wait(i)
                    b.wait(nc.vector, "sA", oie)
                    of = b.inc(nc.vector.scalar_tensor_tensor(
                        out=Of[i], in0=pfB[i][:, 128:256],
                        scalar=cols["rqsT"][:, h:h + 1], in1=OIs[i],
                        op0=OP.mult, op1=OP.add), "sV")
                    b.wait(nc.scalar, "sV", of)
                    nc.scalar.activation(out=zcs[i], in_=Of[i], func=AF.Square,
                                         accum_out=osq[i])
                    nc.scalar.activation(out=osq[i], in_=osq[i], func=AF.Ln,
                                         bias=sb["k_cc"][:, 2:3], scale=1.0 / DV)
                    rst = b.inc(nc.scalar.activation(out=osq[i], in_=osq[i],
                                                     func=AF.Exp, scale=-0.5), "sA")
                    veA[i] = rst
                    b.wait(nc.vector, "sA", rst)
                    veV[i] = b.inc(nc.vector.tensor_scalar_mul(out=Onb[i], in0=Of[i],
                                                               scalar1=osq[i]), "sV")
                for h in pair:
                    i = h % 2
                    pe_wait(i)
                    nc.tensor.transpose(pbb[i][:, 384:512], Onb[i], sb["k_idenb"])
                    peP[i] = b.inc(nc.tensor.matmul(pfA[i][0:64, 0:128], Klam[i], Ub[i],
                                                    start=True, stop=True), "sP")
                for h in pair:
                    i = h % 2
                    va_wait(i)
                    b.wait(nc.scalar, "sV", veV[i])
                    gsl = nc.scalar.activation(out=gsil[i],
                                               in_=gch[par][:, h * 128:(h + 1) * 128],
                                               func=AF.Silu)
                    s0s = b.inc(nc.scalar.activation(out=S0sc[i],
                                                     in_=Sst[:, h * DV:(h + 1) * DV],
                                                     func=AF.Copy,
                                                     scale=aCbc[:, h:h + 1]), "sA")
                    veA[i] = s0s
                    ve_wait(i)
                    b.wait(nc.vector, "sA", s0s)
                    if n >= 2:
                        b.wait(nc.vector, "qG", prev_o_dma[par])
                    nc.vector.tensor_mul(out=och[par][:, h * 128:(h + 1) * 128],
                                         in0=pbb[i][:, 384:512], in1=gsil[i])
                    veV[i] = b.inc(nc.vector.tensor_add(out=Sst[:, h * DV:(h + 1) * DV],
                                                        in0=pfA[i][0:64, 0:128],
                                                        in1=S0sc[i]), "sV")
                    if dbg_delta is not None and (n, h) == dbg_delta:
                        cap = [D0[i], Zs[i], Ys[i], Ps[i], Rb[i], Zrb[i], Ub[i],
                               Of[i], Onb[i], KS0s[i], Pqs[i], Yn[i], zcs[i],
                               och[par][:, h * 128:(h + 1) * 128], gsil[i]]
                        for ci, cap_t in enumerate(cap):
                            nc.vector.tensor_copy(out=ddmp[:, ci * 128:(ci + 1) * 128],
                                                  in_=cap_t)
                        for cj, nm in enumerate(["gT", "ngT", "aT", "bT", "rkT",
                                                 "rqT", "ccT", "barkT", "rqsT",
                                                 "oi2T"]):
                            nc.vector.tensor_copy(
                                out=ddmp[:, 15 * 128 + cj * 16:15 * 128 + (cj + 1) * 16],
                                in_=cols[nm])
                        nc.vector.tensor_copy(out=ddmp[0:64, 17 * 128:17 * 128 + 128],
                                              in_=Sst[:, h * DV:(h + 1) * DV])
                        nc.vector.tensor_copy(out=ddmp[0:64, 18 * 128:18 * 128 + 128],
                                              in_=S0b[i][0:64, :])
                        nc.vector.tensor_copy(out=ddmp[:, 19 * 128:19 * 128 + 1],
                                              in_=lamrk[i])
                        nc.vector.tensor_copy(out=ddmp[0:64, 20 * 128:20 * 128 + 16],
                                              in_=aCbc)
                        dd = b.inc(nc.vector.tensor_copy(
                            out=ddmp[:, 21 * 128:21 * 128 + 128], in_=zcs[i]), "sV")
                        b.wait(nc.gpsimd, "sV", dd)
                        b.dma("qG", dram["ddump"], ddmp)
            ochd = b.cnt["sV"]
            b.wait(nc.gpsimd, "sV", ochd)
            for hh in range(16):
                b.dma("qG", dram["o_s"][hh * 128:(hh + 1) * 128, t0:t0 + 128],
                      och[par][:, hh * 128:(hh + 1) * 128])
            prev_o_dma[par] = b.cnt["qG"]
            b.ad_ochfree = getattr(b, "ad_ochfree", [0] * NCH)
            b.ad_ochfree[n] = ochd
            colfree[n + 1] = b.cnt["sP"]
            vfree[n] = b.cnt["sP"]
        st.close()
        rows_stack.close()
        for nm in ["r_beta", "r_g", "r_ssq"]:
            sb.pop(nm, None)

    def emit_outproj(wname, act_src, xsrc, xdst, tag):
        st = ExitStack()
        stage = [st.enter_context(nc.sbuf_tensor(f"{tag}_s{i}", [128, NTT], F32)).ap()
                 for i in range(2)]
        xres = [st.enter_context(nc.sbuf_tensor(f"{tag}_x{i}", [128, NTT], F32)).ap()
                for i in range(2)]
        b.stage_dma = {}
        b.xadd_done = {}
        def evac(oc, tt, ps):
            key = oc % 2
            xr = xres[key]
            if key in b.xadd_done:
                b.wait(nc.scalar, "sV", b.xadd_done[key])
            b.dma("qA", xr, dram[xsrc][oc * 128:(oc + 1) * 128, tt * NTT:(tt + 1) * NTT])
            b.wait(nc.vector, "qA", b.cnt["qA"])
            s = stage[key]
            if key in b.stage_dma:
                b.wait(nc.vector, "qG", b.stage_dma[key])
            v = b.inc(nc.vector.tensor_add(out=s, in0=ps, in1=xr), "sV")
            b.xadd_done[key] = v
            b.wait(nc.gpsimd, "sV", v)
            b.stage_dma[key] = b.dma(
                "qG", dram[xdst][oc * 128:(oc + 1) * 128, tt * NTT:(tt + 1) * NTT], s)
        b.evac_v = {}
        emit_proj(wname, act_src, 2048, D, tag, evac)
        st.close()

    def emit_dequant():
        st = ExitStack()
        pk = [st.enter_context(nc.sbuf_tensor(f"dq_pk{i}", [128, 512], I32)).ap()
              for i in range(2)]
        pk_echo = st.enter_context(nc.sbuf_tensor("dq_pke", [128, 512], I32)).ap()
        scf = st.enter_context(nc.sbuf_tensor("dq_scf", [8, NMLP], F32)).ap()
        scb = st.enter_context(nc.sbuf_tensor("dq_scb", [8, NMLP], BF16)).ap()
        mki = lambda n: st.enter_context(nc.sbuf_tensor(n, [128, 512], I32)).ap()
        vc_, vt_, vi1, vi2, vsg, vbits = (mki(f"dq_v{k}") for k in "012345")
        gc_, gt_, gi1, gi2, gsg, gbits = (mki(f"dq_g{k}") for k in "012345")
        out_lo = [st.enter_context(nc.sbuf_tensor(f"dq_ol{i}", [128, 512], BF16)).ap()
                  for i in range(2)]
        pck = [st.enter_context(nc.sbuf_tensor(f"dq_pc{i}", [128, 512], BF16)).ap()
               for i in range(2)]
        out_hi = [st.enter_context(nc.sbuf_tensor(f"dq_oh{i}", [128, 512], BF16)).ap()
                  for i in range(2)]
        ps_sc = [st.enter_context(nc.psum_tensor(f"dq_ps{i}", [128, 512], F32)).ap()
                 for i in range(2)]
        if dbg_dq:
            dram["dqdump"] = nc.dram_tensor("dqdump", [128, 6 * 512], I32,
                                            kind="ExternalOutput").ap()
            dqd = st.enter_context(nc.sbuf_tensor("dq_dqd", [128, 6 * 512], I32)).ap()

        def decode2(eng, vin, lo, hi):
            (lc, lt, li1, li2, lsg, lb) = lo
            (hc, ht, hi1, hi2, hsg, hb) = hi
            eng.tensor_scalar(out=lc, in0=vin, scalar1=15, scalar2=None,
                              op0=OP.bitwise_and)
            b.dv(eng.tensor_scalar(out=hc, in0=vin, scalar1=4, scalar2=15,
                                   op0=OP.logical_shift_right, op1=OP.bitwise_and))
            for c, t in ((lc, lt), (hc, ht)):
                eng.tensor_scalar(out=t, in0=c, scalar1=7, scalar2=6,
                                  op0=OP.bitwise_and, op1=OP.logical_shift_left)
            for c, sg in ((lc, lsg), (hc, hsg)):
                mm = eng.tensor_scalar(out=sg, in0=c, scalar1=8, scalar2=12,
                                       op0=OP.bitwise_and, op1=OP.logical_shift_left)
            b.dv(mm)
            for t, i1 in ((lt, li1), (ht, hi1)):
                eng.tensor_scalar(out=i1, in0=t, scalar1=64, scalar2=None, op0=OP.is_ge)
            for t, i2 in ((lt, li2), (ht, hi2)):
                mm = eng.tensor_scalar(out=i2, in0=t, scalar1=128, scalar2=None,
                                       op0=OP.is_ge)
            b.dv(mm)
            for i1, t, bits in ((li1, lt, lb), (hi1, ht, hb)):
                eng.scalar_tensor_tensor(out=bits, in0=i1, scalar=16064, in1=t,
                                         op0=OP.mult, op1=OP.add)
            for i2, sg in ((li2, lsg), (hi2, hsg)):
                mm = eng.scalar_tensor_tensor(out=sg, in0=i2, scalar=64, in1=sg,
                                              op0=OP.mult, op1=OP.add)
            b.dv(mm)
            eng.tensor_tensor(out=lb, in0=lb, in1=lsg, op=OP.add)
            return b.dv(eng.tensor_tensor(out=hb, in0=hb, in1=hsg, op=OP.add))

        mats = [("f_gp", "f_gs", "wg_bf", 8, 32, NMLP, True),
                ("f_up", "f_us", "wu_bf", 8, 32, NMLP, True),
                ("f_dp", "f_ds", "wd_bf", 64, 2048 // 512, 2048, False)]
        st_dma = {}
        dec_done = {}
        for mat, msc, mdst, NJ, NS, NCOLS, plane in mats:
            for j in range(NJ):
                b.wait(nc.sync, "sP", b.cnt["sP"])
                b.dma("qS", scf[0:8, 0:NCOLS], dram[msc][8 * j:8 * j + 8, :])
                b.wait(nc.scalar, "qS", b.cnt["qS"])
                b.wait(nc.scalar, "sP", b.cnt["sP"])
                scv = b.inc(nc.scalar.activation(out=scb[0:8, 0:NCOLS],
                                                 in_=scf[0:8, 0:NCOLS], func=AF.Copy), "sA")
                b.wait(nc.tensor, "sA", scv)
                for ns in range(NS):
                    it = (j * NS + ns) % 2
                    if (it, "prev") in dec_done:
                        b.wait(nc.sync, "sV", dec_done[(it, "prev")][1])
                    b.dma("qS", pk[it], dram[mat][j * 128:(j + 1) * 128,
                                                  ns * 512:(ns + 1) * 512])
                    b.dma("qS", pk_echo, dram[mat][j * 128:(j + 1) * 128,
                                                   ns * 512:(ns + 1) * 512])
                    lq = b.cnt["qS"]
                    if (it, "scused") in dec_done:
                        b.wait(nc.tensor, "sV", dec_done[(it, "scused")])
                    smm = b.inc(nc.tensor.matmul(
                        ps_sc[it], sb["k_pat8"],
                        scb[0:8, ns * 512:(ns + 1) * 512],
                        start=True, stop=True), "sP")
                    b.wait(nc.vector, "qS", lq)
                    if (it, "outdma") in dec_done:
                        b.wait(nc.vector, "qG", dec_done[(it, "outdma")])
                    hi_d = decode2(nc.vector, pk[it],
                                   (vc_, vt_, vi1, vi2, vsg, vbits),
                                   (gc_, gt_, gi1, gi2, gsg, gbits))
                    dec_done[(it, "prev")] = (hi_d, hi_d)
                    b.wait(nc.vector, "sP", smm)
                    if dbg_dq and mat == "f_gp" and j == 0 and ns == 0:
                        nc.vector.tensor_copy(out=dqd[:, 0:512], in_=pk[it])
                        nc.vector.tensor_copy(out=dqd[:, 512:1024], in_=vbits)
                        nc.vector.tensor_copy(out=dqd[:, 1024:1536], in_=vc_)
                        nc.vector.tensor_copy(out=dqd[:, 1536:2048], in_=vt_)
                        nc.vector.tensor_copy(out=dqd[:, 2048:2560], in_=vi1)
                        nc.vector.tensor_copy(out=dqd[:, 2560:3072],
                                              in_=ps_sc[it].bitcast(I32))
                        dd = b.inc(nc.vector.tensor_copy(out=dqd[:, 0:1],
                                                         in_=dqd[:, 0:1]), "sV")
                        b.wait(nc.gpsimd, "sV", dd)
                        b.dma("qG", dram["dqdump"], dqd)
                    b.dv(nc.vector.tensor_tensor(out=out_lo[it],
                                                 in0=vbits.bitcast(BF16)[:, 0::2],
                                                 in1=ps_sc[it], op=OP.mult))
                    scu = b.inc(nc.vector.tensor_tensor(
                        out=out_hi[it], in0=gbits.bitcast(BF16)[:, 0::2],
                        in1=ps_sc[it], op=OP.mult), "sV")
                    dec_done[(it, "scused")] = scu
                    dec_done["last_scu"] = scu
                    b.wait(nc.scalar, "sV", scu)
                    if plane:
                        dst_lo = dram[mdst][:, (2 * j) * NMLP + ns * 512:
                                            (2 * j) * NMLP + (ns + 1) * 512]
                        dst_hi = dram[mdst][:, (2 * j + 1) * NMLP + ns * 512:
                                            (2 * j + 1) * NMLP + (ns + 1) * 512]
                    else:
                        dst_lo = dram[mdst][256 * j:256 * j + 128,
                                            ns * 512:(ns + 1) * 512]
                        dst_hi = dram[mdst][256 * j + 128:256 * j + 256,
                                            ns * 512:(ns + 1) * 512]
                    b.wait(nc.gpsimd, "sV", scu)
                    b.dma("qG", dst_lo, out_lo[it])
                    dec_done[(it, "outdma")] = b.dma("qG", dst_hi, out_hi[it])
        st.close()

    def emit_ffn(xsrc, xdst, tag):
        st = ExitStack()
        lnr = st.enter_context(nc.sbuf_tensor(f"{tag}_ln", [128, 16 * T], BF16)).ap()
        wgt = st.enter_context(nc.sbuf_tensor(f"{tag}_wg", [128, 16 * 4 * 128], BF16)).ap()
        wut = st.enter_context(nc.sbuf_tensor(f"{tag}_wu", [128, 16 * 4 * 128], BF16)).ap()
        gl = [st.enter_context(nc.sbuf_tensor(f"{tag}_gl{i}", [128, NTT], F32)).ap()
              for i in range(2)]
        itile = [st.enter_context(nc.sbuf_tensor(f"{tag}_it{i}", [128, NTT], BF16)).ap()
                 for i in range(2)]
        psA = [st.enter_context(nc.psum_tensor(f"{tag}_pa{i}", [128, NTT], F32)).ap()
               for i in range(2)]
        psB = [st.enter_context(nc.psum_tensor(f"{tag}_pb{i}", [128, NTT], F32)).ap()
               for i in range(2)]
        for kc in range(16):
            b.dma("qS", lnr[:, kc * T:(kc + 1) * T],
                  dram["ln_s"][kc * 128:(kc + 1) * 128, :])
        ln_ld = b.cnt["qS"]
        b.wait(nc.tensor, "qS", ln_ld)
        evd = {}
        itdma = {}
        for ng in range(NMLP // 512):
            b.wait(nc.sync, "sP", evd.get(("wfree", ng - 1), 0))
            for kc in range(16):
                b.dma("qS", wgt[:, kc * 512:(kc + 1) * 512],
                      dram["wg_bf"][:, kc * NMLP + ng * 512: kc * NMLP + (ng + 1) * 512])
            for kc in range(16):
                b.dma("qS", wut[:, kc * 512:(kc + 1) * 512],
                      dram["wu_bf"][:, kc * NMLP + ng * 512: kc * NMLP + (ng + 1) * 512])
            wq = b.cnt["qS"]
            b.wait(nc.tensor, "qS", wq)
            for tt in range(NT):
                for n4 in range(4):
                    ncx = ng * 4 + n4
                    it = ncx % 2
                    if ("ev", it) in evd:
                        b.wait(nc.tensor, "sV", evd[("ev", it)])
                        b.wait(nc.tensor, "sA", evd[("evA", it)])
                    for kc in range(16):
                        nc.tensor.matmul(psA[it], wgt[:, kc * 512 + n4 * 128:
                                                      kc * 512 + (n4 + 1) * 128],
                                         lnr[:, kc * T + tt * NTT: kc * T + (tt + 1) * NTT],
                                         start=(kc == 0), stop=(kc == 15))
                    for kc in range(16):
                        mm = nc.tensor.matmul(psB[it], wut[:, kc * 512 + n4 * 128:
                                                           kc * 512 + (n4 + 1) * 128],
                                              lnr[:, kc * T + tt * NTT: kc * T + (tt + 1) * NTT],
                                              start=(kc == 0), stop=(kc == 15))
                    mp = b.inc(mm, "sP")
                    b.wait(nc.scalar, "sP", mp)
                    if ("glfree", it) in evd:
                        b.wait(nc.scalar, "sV", evd[("glfree", it)])
                    ga = b.inc(nc.scalar.activation(out=gl[it], in_=psA[it],
                                                    func=AF.Gelu_apprx_tanh), "sA")
                    evd[("evA", it)] = ga
                    b.wait(nc.vector, "sA", ga)
                    b.wait(nc.vector, "sP", mp)
                    if ("itdma", it) in evd:
                        b.wait(nc.vector, "qG", evd[("itdma", it)])
                    ev = b.inc(nc.vector.tensor_tensor(out=itile[it], in0=gl[it],
                                                       in1=psB[it], op=OP.mult), "sV")
                    evd[("ev", it)] = ev
                    evd[("glfree", it)] = ev
                    b.wait(nc.gpsimd, "sV", ev)
                    evd[("itdma", it)] = b.dma(
                        "qG", dram["inter_s"][ncx * 128:(ncx + 1) * 128,
                                              tt * NTT:(tt + 1) * NTT], itile[it])
            evd[("wfree", ng)] = b.cnt["sP"]
        st.close()
        b.barrier()
        st = ExitStack()
        acc = st.enter_context(nc.sbuf_tensor(f"{tag}_acc", [128, 16 * NTT], F32)).ap()
        wds = [st.enter_context(nc.sbuf_tensor(f"{tag}_wd{i}", [128, 8 * 2048], BF16)).ap()
               for i in range(2)]
        its = [st.enter_context(nc.sbuf_tensor(f"{tag}_is{i}", [128, 8 * NTT], BF16)).ap()
               for i in range(2)]
        xres = [st.enter_context(nc.sbuf_tensor(f"{tag}_xr{i}", [128, NTT], F32)).ap()
                for i in range(2)]
        stage = [st.enter_context(nc.sbuf_tensor(f"{tag}_st{i}", [128, NTT], F32)).ap()
                 for i in range(2)]
        psD = [st.enter_context(nc.psum_tensor(f"{tag}_pd{i}", [128, NTT], F32)).ap()
               for i in range(2)]
        NSUP = NMLP // (8 * 128)
        bd = {}
        for tt in range(NT):
            for dc in range(16):
                if ("odma", dc % 2) in bd:
                    b.wait(nc.vector, "qG", bd[("odma", dc % 2)])
                b.inc(nc.vector.memset(acc[:, dc * NTT:(dc + 1) * NTT], 0.0), "sV")
            z0 = b.cnt["sV"]
            for sup in range(NSUP):
                it = sup % 2
                if ("bfree", it) in bd:
                    b.wait(nc.sync, "sP", bd[("bfree", it)])
                for ncl in range(8):
                    ncx = sup * 8 + ncl
                    b.dma("qS", wds[it][:, ncl * 2048:(ncl + 1) * 2048],
                          dram["wd_bf"][ncx * 128:(ncx + 1) * 128, :])
                    b.dma("qS", its[it][:, ncl * NTT:(ncl + 1) * NTT],
                          dram["inter_s"][ncx * 128:(ncx + 1) * 128,
                                          tt * NTT:(tt + 1) * NTT])
                lq = b.cnt["qS"]
                b.wait(nc.tensor, "qS", lq)
                if sup == 0:
                    b.wait(nc.tensor, "sV", z0)
                for dc in range(16):
                    pit = dc % 2
                    if ("acc", pit) in bd:
                        b.wait(nc.tensor, "sV", bd[("acc", pit)])
                    for ncl in range(8):
                        mm = nc.tensor.matmul(
                            psD[pit], wds[it][:, ncl * 2048 + dc * 128:
                                              ncl * 2048 + (dc + 1) * 128],
                            its[it][:, ncl * NTT:(ncl + 1) * NTT],
                            start=(ncl == 0), stop=(ncl == 7))
                    mp = b.inc(mm, "sP")
                    b.wait(nc.vector, "sP", mp)
                    bd[("acc", pit)] = b.inc(nc.vector.tensor_add(
                        out=acc[:, dc * NTT:(dc + 1) * NTT],
                        in0=psD[pit], in1=acc[:, dc * NTT:(dc + 1) * NTT]), "sV")
                bd[("bfree", it)] = b.cnt["sP"]
            for dc in range(16):
                key = dc % 2
                if ("xadd", key) in bd:
                    b.wait(nc.scalar, "sV", bd[("xadd", key)])
                b.dma("qA", xres[key],
                      dram[xsrc][dc * 128:(dc + 1) * 128, tt * NTT:(tt + 1) * NTT])
                b.wait(nc.vector, "qA", b.cnt["qA"])
                if ("odma", key) in bd:
                    b.wait(nc.vector, "qG", bd[("odma", key)])
                v = b.inc(nc.vector.tensor_add(out=stage[key],
                                               in0=acc[:, dc * NTT:(dc + 1) * NTT],
                                               in1=xres[key]), "sV")
                bd[("xadd", key)] = v
                b.wait(nc.gpsimd, "sV", v)
                dst = dram[xdst]
                bd[("odma", key)] = b.dma(
                    "qG", dst[dc * 128:(dc + 1) * 128, tt * NTT:(tt + 1) * NTT],
                    stage[key])
        st.close()

    def emit_bproj():
        st = ExitStack()
        lnr = st.enter_context(nc.sbuf_tensor("bp_ln", [128, 16 * T], BF16)).ap()
        wld = [st.enter_context(nc.sbuf_tensor(f"bp_wl{i}", [128, 16 * 128], F32)).ap()
               for i in range(2)]
        wbf = [st.enter_context(nc.sbuf_tensor(f"bp_wb{i}", [128, 16 * 128], BF16)).ap()
               for i in range(2)]
        rope = st.enter_context(nc.sbuf_tensor("bp_rope", [64, 4 * NTT], F32)).ap()
        ta = st.enter_context(nc.sbuf_tensor("bp_ta", [64, NTT], F32)).ap()
        tb = st.enter_context(nc.sbuf_tensor("bp_tb", [64, NTT], F32)).ap()
        stg = [st.enter_context(nc.sbuf_tensor(f"bp_st{i}", [128, NTT], BF16)).ap()
               for i in range(2)]
        s1s = st.enter_context(nc.sbuf_tensor("bp_s1", [16, NTT], F32)).ap()
        sb["r_s1b"] = st.enter_context(nc.sbuf_tensor("r_s1b", [16, T], BF16)).ap()
        wl1 = st.enter_context(nc.sbuf_tensor("bp_ws1l", [128, 16 * 16], F32)).ap()
        wb1 = st.enter_context(nc.sbuf_tensor("bp_ws1b", [128, 16 * 16], BF16)).ap()
        pp = [st.enter_context(nc.psum_tensor(f"bp_p{i}", [128, NTT], F32)).ap()
              for i in range(2)]
        ps1 = st.enter_context(nc.psum_tensor("bp_ps1", [16, NTT], F32)).ap()
        for kc in range(16):
            b.dma("qS", lnr[:, kc * T:(kc + 1) * T],
                  dram["ln_s"][kc * 128:(kc + 1) * 128, :])
        for kc in range(16):
            b.dma("qS", wl1[:, kc * 16:(kc + 1) * 16],
                  dram["ab_ws1"][kc * 128:(kc + 1) * 128, :])
        ln_ld = b.cnt["qS"]
        b.wait(nc.scalar, "qS", ln_ld)
        w1c = b.inc(nc.scalar.activation(out=wb1, in_=wl1, func=AF.Copy), "sA")
        b.wait(nc.tensor, "qS", ln_ld)
        b.wait(nc.tensor, "sA", w1c)
        s1done = 0
        for tt in range(NT):
            b.wait(nc.tensor, "sV", s1done)
            for kc in range(16):
                mm = nc.tensor.matmul(ps1, wb1[:, kc * 16:(kc + 1) * 16],
                                      lnr[:, kc * T + tt * NTT: kc * T + (tt + 1) * NTT],
                                      start=(kc == 0), stop=(kc == 15))
            mp = b.inc(mm, "sP")
            b.wait(nc.vector, "sP", mp)
            s1done = b.inc(nc.vector.tensor_copy(
                out=sb["r_s1b"][:, tt * NTT:(tt + 1) * NTT], in_=ps1), "sV")
        dsts = {"ab_wq": ("qkv_s", 0, "q"), "ab_wk": ("qkv_s", 2048, "k"),
                "ab_wv": ("vg_s", 0, None), "ab_wg": ("vg_s", 2048, None)}
        wcvt = {}
        evd = {}
        for mat, (dst, row0, rk) in dsts.items():
            for oc in range(16):
                it = oc % 2
                if ("wfree", it) in wcvt:
                    b.wait(nc.sync, "sP", wcvt[("wfree", it)])
                for kc in range(16):
                    b.dma("qS", wld[it][:, kc * 128:(kc + 1) * 128],
                          dram[mat][kc * 128:(kc + 1) * 128, oc * 128:(oc + 1) * 128])
                b.wait(nc.scalar, "qS", b.cnt["qS"])
                if ("wcvt", it) in wcvt:
                    b.wait(nc.scalar, "sP", wcvt[("wcvt", it)])
                wc = b.inc(nc.scalar.activation(out=wbf[it], in_=wld[it],
                                                func=AF.Copy), "sA")
                b.wait(nc.tensor, "sA", wc)
                for tt in range(NT):
                    if rk:
                        c1 = "c_cosq" if rk == "q" else "c_cos"
                        s1_ = "c_sinq" if rk == "q" else "c_sin"
                        b.wait(nc.sync, "sV", evd.get("ropeused", 0))
                        b.dma("qS", rope[:, 0:NTT],
                              dram[c1][:, tt * NTT:(tt + 1) * NTT])
                        b.dma("qS", rope[:, NTT:2 * NTT],
                              dram[s1_][:, tt * NTT:(tt + 1) * NTT])
                    rl = b.cnt["qS"]
                    pit = (oc * NT + tt) % 2
                    ps = pp[pit]
                    if ("ev", pit) in evd:
                        b.wait(nc.tensor, "sV", evd[("ev", pit)])
                    for kc in range(16):
                        mm = nc.tensor.matmul(ps, wbf[it][:, kc * 128:(kc + 1) * 128],
                                              lnr[:, kc * T + tt * NTT:
                                                  kc * T + (tt + 1) * NTT],
                                              start=(kc == 0), stop=(kc == 15))
                    mp = b.inc(mm, "sP")
                    wcvt[("wcvt", it)] = mp
                    b.wait(nc.vector, "sP", mp)
                    s = stg[pit]
                    if ("sdma", pit) in evd:
                        b.wait(nc.vector, "qG", evd[("sdma", pit)])
                    if rk:
                        b.wait(nc.vector, "qS", rl)
                        cosr = rope[:, 0:NTT]
                        sinr = rope[:, NTT:2 * NTT]
                        nc.vector.tensor_mul(out=ta, in0=ps[0:64, :], in1=cosr)
                        nc.vector.tensor_mul(out=tb, in0=ps[64:128, :], in1=sinr)
                        nc.vector.tensor_sub(out=s[0:64, :], in0=ta, in1=tb)
                        nc.vector.tensor_mul(out=ta, in0=ps[64:128, :], in1=cosr)
                        nc.vector.tensor_mul(out=tb, in0=ps[0:64, :], in1=sinr)
                        ev = b.inc(nc.vector.tensor_add(out=s[64:128, :], in0=ta,
                                                        in1=tb), "sV")
                        evd["ropeused"] = ev
                    else:
                        ev = b.inc(nc.vector.tensor_copy(out=s, in_=ps), "sV")
                    evd[("ev", pit)] = ev
                    b.wait(nc.gpsimd, "sV", ev)
                    evd[("sdma", pit)] = b.dma(
                        "qG", dram[dst][row0 + oc * 128:row0 + (oc + 1) * 128,
                                        tt * NTT:(tt + 1) * NTT], s)
                wcvt[("wfree", it)] = b.cnt["sP"]
        st.close()

    def emit_bheads():
        st = ExitStack()
        Qh = st.enter_context(nc.sbuf_tensor("bh_q", [128, T], BF16)).ap()
        Kh = st.enter_context(nc.sbuf_tensor("bh_k", [128, T], BF16)).ap()
        Vf = st.enter_context(nc.sbuf_tensor("bh_vf", [128, T], BF16)).ap()
        Vtm = st.enter_context(nc.sbuf_tensor("bh_vtm", [128, T], BF16)).ap()
        esTf = st.enter_context(nc.sbuf_tensor("bh_esf", [64, T], F32)).ap()
        esTb = st.enter_context(nc.sbuf_tensor("bh_esb", [64, T], BF16)).ap()
        cesT = st.enter_context(nc.sbuf_tensor("bh_ces", [64, T], F32)).ap()
        estm = st.enter_context(nc.sbuf_tensor("bh_estm", [128, (T // 128) * 64], BF16)).ap()
        enztm = st.enter_context(nc.sbuf_tensor("bh_enztm", [128, (T // 128) * 64], F32)).ap()
        shi = st.enter_context(nc.sbuf_tensor("bh_shi", [64, NTT], F32)).ap()
        Ab = [st.enter_context(nc.sbuf_tensor(f"bh_ab{i}", [128, NTT], BF16)).ap()
              for i in range(3)]
        okf = st.enter_context(nc.sbuf_tensor("bh_okf", [128, 64], F32)).ap()
        oke = st.enter_context(nc.sbuf_tensor("bh_oke", [128, 64], F32)).ap()
        mxc = st.enter_context(nc.sbuf_tensor("bh_mx", [128, 2], F32)).ap()
        qzb = st.enter_context(nc.sbuf_tensor("bh_qz", [128, 64], BF16)).ap()
        qzT = st.enter_context(nc.sbuf_tensor("bh_qzT", [64, NTT], BF16)).ap()
        sqs = st.enter_context(nc.sbuf_tensor("bh_sqs", [128, NTT], F32)).ap()
        rrow = st.enter_context(nc.sbuf_tensor("bh_rrow", [1, NTT], F32)).ap()
        gts = [st.enter_context(nc.sbuf_tensor(f"bh_gt{i}", [128, NTT], BF16)).ap()
               for i in range(2)]
        gsl = st.enter_context(nc.sbuf_tensor("bh_gsl", [128, NTT], F32)).ap()
        ons = st.enter_context(nc.sbuf_tensor("bh_ons", [128, NTT], F32)).ap()
        o1s = st.enter_context(nc.sbuf_tensor("bh_o1", [128, NTT], F32)).ap()
        ogs = [st.enter_context(nc.sbuf_tensor(f"bh_og{i}", [128, NTT], BF16)).ap()
               for i in range(2)]
        pq1 = st.enter_context(nc.psum_tensor("bh_pq1", [128, NTT], F32)).ap()
        pq2 = st.enter_context(nc.psum_tensor("bh_pq2", [128, NTT], F32)).ap()
        pq3 = st.enter_context(nc.psum_tensor("bh_pq3", [128, NTT], F32)).ap()
        pok = st.enter_context(nc.psum_tensor("bh_pok", [128, 256], F32)).ap()
        pO = st.enter_context(nc.psum_tensor("bh_pO", [128, NTT], F32)).ap()
        pbf = st.enter_context(nc.psum_tensor("bh_pbf", [128, 512], BF16)).ap()
        pbf2 = st.enter_context(nc.psum_tensor("bh_pbf2", [128, 512], BF16)).ap()
        NB = T // 128
        for h in range(H):
            hv = b.snap()
            b.wait(nc.sync, "sP", hv["sP"])
            b.wait(nc.sync, "sV", hv["sV"])
            b.dma("qS", Qh, dram["qkv_s"][h * 128:(h + 1) * 128, :])
            b.dma("qS", Kh, dram["qkv_s"][2048 + h * 128:2048 + (h + 1) * 128, :])
            b.dma("qS", Vf, dram["vg_s"][h * 128:(h + 1) * 128, :])
            hld = b.cnt["qS"]
            b.wait(nc.tensor, "sV", hv["sV"])
            b.wait(nc.tensor, "sA", hv["sA"])
            sdone = 0
            for tt in range(NT):
                mp = b.inc(nc.tensor.matmul(
                    pq1[0:64, :], sb["k_ws2Tb"][:, h * 64:(h + 1) * 64],
                    sb["r_s1b"][:, tt * NTT:(tt + 1) * NTT],
                    start=True, stop=True), "sP")
                b.wait(nc.scalar, "sP", mp)
                sa = b.inc(nc.scalar.activation(out=shi, in_=pq1[0:64, :],
                                                func=AF.Identity,
                                                bias=sb["k_bs2T"][:, h:h + 1]), "sA")
                b.wait(nc.vector, "sA", sa)
                cl = b.inc(nc.vector.tensor_scalar(out=shi, in0=shi, scalar1=32.0,
                                                   scalar2=-32.0, op0=OP.min,
                                                   op1=OP.max), "sV")
                b.wait(nc.scalar, "sV", cl)
                ex = b.inc(nc.scalar.activation(
                    out=esTf[:, tt * NTT:(tt + 1) * NTT], in_=shi, func=AF.Exp), "sA")
                b.wait(nc.vector, "sA", ex)
                sdone = b.inc(nc.vector.tensor_copy(
                    out=esTb[:, tt * NTT:(tt + 1) * NTT],
                    in_=esTf[:, tt * NTT:(tt + 1) * NTT]), "sV")
                b.wait(nc.tensor, "sV", sdone)
            sc = b.inc(nc.vector.tensor_tensor_scan(out=cesT, data0=esTf, data1=esTf,
                                                    initial=0.0, op0=OP.add,
                                                    op1=OP.bypass), "sV")
            rc = b.inc(nc.vector.reciprocal(out=cesT, in_=cesT), "sV")
            b.wait(nc.tensor, "qS", hld)
            tp = {}
            for blk in range(NB):
                pb = (pbf, pbf2)[blk % 2]
                if blk >= 2:
                    b.wait(nc.tensor, "sV", tp[blk - 2])
                nc.tensor.transpose(pb[:, 0:128], Vf[:, blk * 128:(blk + 1) * 128],
                                    sb["k_idenb"])
                if blk == 0:
                    b.wait(nc.tensor, "sV", sdone)
                tr = b.inc(nc.tensor.transpose(pb[:, 128:192],
                                               esTb[:, blk * 128:(blk + 1) * 128],
                                               sb["k_idenb"][0:64, 0:64]), "sP")
                b.wait(nc.vector, "sP", tr)
                nc.vector.tensor_copy(out=Vtm[:, blk * 128:(blk + 1) * 128],
                                      in_=pb[:, 0:128])
                tp[blk] = b.inc(nc.vector.tensor_copy(
                    out=estm[:, blk * 64:(blk + 1) * 64], in_=pb[:, 128:192]), "sV")
            for blk in range(NB):
                pf = (pq1, pq2)[blk % 2]
                if blk == 0:
                    b.wait(nc.tensor, "sV", rc)
                    b.wait(nc.tensor, "sA", b.cnt["sA"])
                if blk >= 2:
                    b.wait(nc.tensor, "sV", tp[NB + blk - 2])
                tr = b.inc(nc.tensor.transpose(pf[:, 0:64],
                                               cesT[:, blk * 128:(blk + 1) * 128],
                                               sb["k_idenf"][0:64, 0:64]), "sP")
                b.wait(nc.vector, "sP", tr)
                tp[NB + blk] = b.inc(nc.vector.tensor_copy(
                    out=enztm[:, blk * 64:(blk + 1) * 64], in_=pf[:, 0:64]), "sV")
            prep_done = b.cnt["sV"]
            b.wait(nc.tensor, "sV", prep_done)
            od = {}
            for qt in range(NT):
                nkt = 4 * qt + 4
                amp = {}
                def a_mm(kt):
                    pit = kt % 3
                    pA = (pq1, pq2, pq3)[pit]
                    if ("aev", pit) in od:
                        b.wait(nc.tensor, "sV", od[("aev", pit)])
                    amp[kt] = b.inc(nc.tensor.matmul(
                        pA, Kh[:, kt * 128:(kt + 1) * 128],
                        Qh[:, qt * NTT:(qt + 1) * NTT], start=True, stop=True), "sP")
                a_mm(0)
                if nkt > 1:
                    a_mm(1)
                for kt in range(nkt):
                    pit = kt % 3
                    pA = (pq1, pq2, pq3)[pit]
                    b.wait(nc.vector, "sP", amp[kt])
                    r = kt - 4 * qt
                    if r >= 0:
                        ae = b.inc(nc.vector.tensor_mul(
                            out=Ab[pit], in0=pA,
                            in1=sb["k_dmask"][:, r * 512:(r + 1) * 512]), "sV")
                    else:
                        ae = b.inc(nc.vector.tensor_copy(out=Ab[pit], in_=pA), "sV")
                    od[("aev", pit)] = ae
                    if kt + 2 < nkt:
                        a_mm(kt + 2)
                    b.wait(nc.tensor, "sV", ae)
                    for s4 in range(4):
                        mm = nc.tensor.matmul(
                            pok[:, s4 * 64:(s4 + 1) * 64],
                            Ab[pit][:, s4 * 128:(s4 + 1) * 128],
                            estm[:, kt * 64:(kt + 1) * 64],
                            start=(kt == 0 and s4 == 0),
                            stop=(kt == nkt - 1 and s4 == 3))
                    okp = b.inc(mm, "sP")
                b.wait(nc.vector, "sP", okp)
                for s4 in range(4):
                    tb4 = 4 * qt + s4
                    b.dv(nc.vector.tensor_mul(out=okf, in0=pok[:, s4 * 64:(s4 + 1) * 64],
                                              in1=enztm[:, tb4 * 64:(tb4 + 1) * 64]))
                    mx = b.inc(nc.vector.reduce_max(out=mxc[:, 0:1], in_=okf,
                                                    axis=AX.X, negate=True), "sV")
                    b.wait(nc.scalar, "sV", mx)
                    ea = b.inc(nc.scalar.activation(out=oke, in_=okf, func=AF.Exp,
                                                    bias=mxc[:, 0:1]), "sA")
                    b.wait(nc.vector, "sA", ea)
                    b.dv(nc.vector.reduce_sum(out=mxc[:, 1:2], in_=oke, axis=AX.X))
                    b.dv(nc.vector.reciprocal(out=mxc[:, 1:2], in_=mxc[:, 1:2]))
                    qz = b.inc(nc.vector.scalar_tensor_tensor(
                        out=qzb, in0=oke, scalar=mxc[:, 1:2],
                        in1=enztm[:, tb4 * 64:(tb4 + 1) * 64],
                        op0=OP.mult, op1=OP.mult), "sV")
                    b.wait(nc.tensor, "sV", qz)
                    pb = (pbf, pbf2)[s4 % 2]
                    tr = b.inc(nc.tensor.transpose(pb[0:64, 0:128],
                                                   qzb, sb["k_idenb"]), "sP")
                    b.wait(nc.vector, "sP", tr)
                    b.inc(nc.vector.tensor_copy(
                        out=qzT[:, s4 * 128:(s4 + 1) * 128], in_=pb[0:64, 0:128]), "sV")
                qzd = b.cnt["sV"]
                b.wait(nc.tensor, "sV", qzd)
                wmp = {}
                def w_mm(kt):
                    pit = kt % 3
                    pW = (pq1, pq2, pq3)[pit]
                    if ("wev", pit) in od:
                        b.wait(nc.tensor, "sV", od[("wev", pit)])
                    wmp[kt] = b.inc(nc.tensor.matmul(
                        pW, esTb[:, kt * 128:(kt + 1) * 128], qzT,
                        start=True, stop=True), "sP")
                w_mm(0)
                if nkt > 1:
                    w_mm(1)
                for kt in range(nkt):
                    pit = kt % 3
                    pW = (pq1, pq2, pq3)[pit]
                    b.wait(nc.vector, "sP", wmp[kt])
                    r = kt - 4 * qt
                    if r >= 0:
                        we = b.inc(nc.vector.tensor_mul(
                            out=Ab[pit], in0=pW,
                            in1=sb["k_dmask"][:, r * 512:(r + 1) * 512]), "sV")
                    else:
                        we = b.inc(nc.vector.tensor_copy(out=Ab[pit], in_=pW), "sV")
                    od[("wev", pit)] = we
                    if kt + 2 < nkt:
                        w_mm(kt + 2)
                    b.wait(nc.tensor, "sV", we)
                    mm = nc.tensor.matmul(pO, Vtm[:, kt * 128:(kt + 1) * 128],
                                          Ab[pit], start=(kt == 0),
                                          stop=(kt == nkt - 1))
                op = b.inc(mm, "sP")
                b.wait(nc.scalar, "sP", op)
                sq = b.inc(nc.scalar.activation(out=sqs, in_=pO, func=AF.Square), "sA")
                b.wait(nc.tensor, "sA", sq)
                rm = b.inc(nc.tensor.matmul(pq1[0:1, :], sb["k_onescol"], sqs,
                                            start=True, stop=True), "sP")
                b.wait(nc.scalar, "sP", rm)
                nc.scalar.activation(out=rrow, in_=pq1[0:1, :], func=AF.Ln,
                                     scale=1.0 / 128, bias=sb["k_cc"][0:1, 3:4])
                rr = b.inc(nc.scalar.activation(out=rrow, in_=rrow, func=AF.Exp,
                                                scale=-0.5), "sA")
                b.wait(nc.tensor, "sA", rr)
                bc = b.inc(nc.tensor.matmul(pq2, sb["k_onesfT"], rrow,
                                            start=True, stop=True), "sP")
                git = qt % 2
                if ("gdma", git) in od:
                    b.wait(nc.scalar, "qA", od[("gdma", git)])
                od[("gdma", git)] = b.dma(
                    "qA", gts[git], dram["vg_s"][2048 + h * 128:2048 + (h + 1) * 128,
                                                 qt * NTT:(qt + 1) * NTT])
                b.wait(nc.scalar, "qA", od[("gdma", git)])
                b.wait(nc.scalar, "sV", b.cnt["sV"])
                gs_ = b.inc(nc.scalar.activation(out=gsl, in_=gts[git],
                                                 func=AF.Silu), "sA")
                ons_ = b.inc(nc.scalar.activation(out=ons, in_=pO, func=AF.Copy,
                                                  scale=sb["k_gnw"]), "sA")
                b.wait(nc.vector, "sP", bc)
                b.wait(nc.vector, "sA", ons_)
                nc.vector.tensor_mul(out=o1s, in0=ons, in1=pq2)
                if ("odma", git) in od:
                    b.wait(nc.vector, "qG", od[("odma", git)])
                og = b.inc(nc.vector.tensor_mul(out=ogs[git], in0=o1s, in1=gsl), "sV")
                b.wait(nc.gpsimd, "sV", og)
                od[("odma", git)] = b.dma(
                    "qG", dram["o_s"][h * 128:(h + 1) * 128, qt * NTT:(qt + 1) * NTT],
                    ogs[git])
                for pp_ in range(3):
                    od[("aev", pp_)] = og
                    od[("wev", pp_)] = og
        st.close()

    emits = {
        "init": emit_init,
        "ln1": lambda: emit_ln("x_fm", "ln1"),
        "aproj": emit_aproj,
        "abeta": emit_abeta,
        "aconv": emit_aconv,
        "adelta": emit_adelta,
        "aout": lambda: emit_outproj("aa_ow", "o_s", "x_fm", "xa", "aout"),
        "ln2": lambda: emit_ln("xa", "ln2"),
        "dequant": emit_dequant,
        "ffn1": lambda: emit_ffn("xa", "xb", "f1"),
        "ln3": lambda: emit_ln("xb", "ln3"),
        "battn": lambda: (emit_bproj(), b.barrier(), emit_bheads()),
        "bout": lambda: emit_outproj("ab_wo", "o_s", "xb", "xa", "bout"),
        "ln4": lambda: emit_ln("xa", "ln4"),
        "ffn2": lambda: emit_ffn("xa", "y_fm", "f2"),
    }
    for p in ph:
        if p in emits:
            emits[p]()
            b.barrier()

    nc.sync.wait_ge(b.sem["qG"], b.cnt["qG"])
    nc.sync.wait_ge(b.sem["qA"], b.cnt["qA"])
    nc.sync.wait_ge(b.sem["qS"], b.cnt["qS"])
    return nc, b, dram


def kernel(**inputs):
    T = np.asarray(inputs["L0_attn_a_x"]).shape[1]
    w, xs = host_prep(inputs, T)
    nc, _, _ = build_nc(T)
    in_maps = []
    for c in range(len(xs)):
        m = dict(w)
        m["x_fm"] = xs[c]
        in_maps.append(m)
    res = run_bass_kernel_spmd(nc, in_maps, core_ids=list(range(len(xs))))
    return host_post([r["y_fm"] for r in res.results], T)


# version 05
# speedup vs baseline: 1.1233x; 1.0105x over previous
from contextlib import ExitStack
import numpy as np
import ml_dtypes
import concourse.bass as bass
import concourse.mybir as mybir
from concourse.bass_utils import run_bass_kernel_spmd

F32 = mybir.dt.float32
BF16 = mybir.dt.bfloat16
I32 = mybir.dt.int32
AF = mybir.ActivationFunctionType
OP = mybir.AluOpType
AX = mybir.AxisListType
BF = ml_dtypes.bfloat16

D = 2048
H = 16
DK = 64
DV = 128
SCALE_A = DK ** -0.5
NMLP = 16384
NTT = 512
SERIAL_DELTA = False


def _perm(n):
    c = np.arange(n) // 128
    p = np.arange(n) % 128
    return 256 * (c // 2) + 2 * p + (c % 2)

PERM_D = _perm(D)
PERM_N = _perm(NMLP)

_cc = np.ascontiguousarray

def host_prep(inputs, T):
    g = lambda k: np.asarray(inputs[k])
    w = {}
    w["aa_qkw"] = _cc(g("L0_attn_a_qk_w").T[PERM_D])
    w["aa_vgw"] = _cc(g("L0_attn_a_vg_w").T[PERM_D])
    w["aa_ow"] = _cc(g("L0_attn_a_o_w")[PERM_D].T)
    w["aa_abw"] = _cc(np.concatenate([g("L0_attn_a_b_w").T[PERM_D],
                                      g("L0_attn_a_a_w").T[PERM_D]], axis=1))
    w["aa_vecT"] = _cc(np.stack([g("L0_attn_a_b_bias"), g("L0_attn_a_beta_bias"),
                                 g("L0_attn_a_A_log"), g("L0_attn_a_dt_bias")], axis=1))
    w["aa_conv"] = _cc(np.concatenate([g("L0_attn_a_q_conv_w")[:, 0, :],
                                       g("L0_attn_a_k_conv_w")[:, 0, :],
                                       g("L0_attn_a_v_conv_w")[:, 0, :]], axis=0))
    w["ab_wq"] = _cc(g("L0_attn_b_Wq").T[PERM_D])
    w["ab_wk"] = _cc(g("L0_attn_b_Wk").T[PERM_D])
    w["ab_wv"] = _cc(g("L0_attn_b_Wv").T[PERM_D])
    w["ab_wg"] = _cc(g("L0_attn_b_Wg").T[PERM_D])
    w["ab_wo"] = _cc(g("L0_attn_b_Wo")[PERM_D].T)
    w["ab_ws1"] = _cc(g("L0_attn_b_Ws1").T[PERM_D])
    w["ab_ws2T"] = _cc(g("L0_attn_b_Ws2").T)
    w["ab_bs2T"] = _cc(g("L0_attn_b_bs2").reshape(16, 64).T)
    w["ab_gnw"] = _cc(g("L0_attn_b_g_norm_weight").reshape(128, 1).astype(np.float32))
    w["f_gp"] = _cc(g("L0_ffn_gate_W_packed")[PERM_N].T)
    w["f_up"] = _cc(g("L0_ffn_up_W_packed")[PERM_N].T)
    w["f_dp"] = _cc(g("L0_ffn_down_W_packed")[PERM_D].T)
    w["f_gs"] = _cc(g("L0_ffn_gate_scales")[PERM_N].T)
    w["f_us"] = _cc(g("L0_ffn_up_scales")[PERM_N].T)
    w["f_ds"] = _cc(g("L0_ffn_down_scales")[PERM_D].T)

    i = np.arange(128)
    w["c_mstrict"] = _cc((i[:, None] < i[None, :]).astype(np.float32))
    w["c_mincl"] = _cc((i[:, None] <= i[None, :]).astype(np.float32))
    w["c_idenf"] = np.eye(128, dtype=np.float32)
    w["c_idenb"] = np.eye(128, dtype=np.float32).astype(BF)
    w["c_onesbT"] = np.ones((1, 128), BF)
    w["c_onesfT"] = np.ones((1, 128), np.float32)
    w["c_onescol"] = np.ones((128, 1), np.float32)
    ccs = np.zeros((128, 8), np.float32)
    ccs[:, 0] = 1e-5
    ccs[:, 1] = 1e-6
    ccs[:, 2] = np.finfo(np.float32).eps
    ccs[:, 3] = 1e-5
    ccs[:, 4] = 1.0
    w["c_cc"] = ccs
    bd = np.zeros((128, 8, 16), np.float32)
    for ti in range(8):
        bd[np.arange(128), ti, 2 * ti + np.arange(128) // 64] = 1.0
    w["c_bd"] = _cc(bd.reshape(128, 128))
    pat = np.zeros((8, 128), np.float32)
    pat[np.arange(128) // 16, np.arange(128)] = 1.0
    w["c_pat8"] = pat.astype(BF)
    dm = np.zeros((128, 4, 512), np.float32)
    for r in range(4):
        dm[:, r, :] = (np.arange(128)[:, None] + 128 * r <= np.arange(512)[None, :])
    w["c_dmask"] = _cc(dm.reshape(128, 4 * 512))
    pos = np.arange(T, dtype=np.float64)
    inv_freq = 1.0 / 10000.0 ** (np.arange(0, 128, 2, dtype=np.float64) / 128.0)
    ang = inv_freq[:, None] * pos[None, :]
    w["c_cos"] = np.cos(ang).astype(np.float32)
    w["c_sin"] = np.sin(ang).astype(np.float32)
    w["c_cosq"] = (np.cos(ang) * (128 ** -0.5)).astype(np.float32)
    w["c_sinq"] = (np.sin(ang) * (128 ** -0.5)).astype(np.float32)

    x = np.asarray(inputs["L0_attn_a_x"])
    xs = [_cc(x[c].T[PERM_D]) for c in range(x.shape[0])]
    return w, xs

def host_post(ys, T):
    out = np.empty((len(ys), T, D), np.float32)
    for c, y in enumerate(ys):
        out[c][:, PERM_D] = y.T
    return out


ALL_SEMS = ["qS", "qA", "qG", "sP", "sV", "sA", "sG"]

class Bld:
    def __init__(self, nc):
        self.nc = nc
        self.sem = {}
        self.cnt = {}
        for name in ALL_SEMS:
            self.sem[name] = nc.alloc_semaphore(name=name)
            self.cnt[name] = 0
        self.Q = {"qS": nc.sync, "qA": nc.scalar, "qG": nc.gpsimd}
        self.engs = [nc.sync, nc.scalar, nc.vector, nc.tensor, nc.gpsimd]

    def dma(self, q, out, in_, **kw):
        self.Q[q].dma_start(out=out, in_=in_, **kw).then_inc(self.sem[q], 16)
        self.cnt[q] += 16
        return self.cnt[q]

    def inc(self, ins, s):
        ins.then_inc(self.sem[s], 1)
        self.cnt[s] += 1
        return self.cnt[s]

    def dv(self, ins):
        c = self.inc(ins, "sV")
        self.nc.vector.wait_ge(self.sem["sV"], c)
        return c

    def wait(self, eng, s, val):
        if val > 0:
            eng.wait_ge(self.sem[s], val)

    def snap(self):
        return dict(self.cnt)

    def barrier(self):
        pt = self.snap()
        for eng in self.engs:
            for s in ALL_SEMS:
                self.wait(eng, s, pt[s])
        return pt


def build_nc(T, phases=None, debug_out=False, dbg_delta=None, dbg_dq=False):
    nc = bass.Bass("TRN2", target_bir_lowering=False, debug=False,
                   detect_race_conditions=False)
    b = Bld(nc)
    NT = T // NTT
    NCH = T // 128

    dram = {}
    def din(name, shape, dt):
        dram[name] = nc.dram_tensor(name, list(shape), dt, kind="ExternalInput").ap()
    def dscr(name, shape, dt):
        kind = "ExternalOutput" if debug_out else "Internal"
        dram[name] = nc.dram_tensor(name, list(shape), dt, kind=kind).ap()

    din("x_fm", [D, T], F32)
    dram["y_fm"] = nc.dram_tensor("y_fm", [D, T], F32, kind="ExternalOutput").ap()

    for nm, sh, dt in [
        ("aa_qkw", [D, 2048], F32), ("aa_vgw", [D, 4096], F32), ("aa_ow", [2048, D], F32),
        ("aa_abw", [D, 32], F32), ("aa_vecT", [16, 4], F32), ("aa_conv", [4096, 4], F32),
        ("ab_wq", [D, 2048], F32), ("ab_wk", [D, 2048], F32), ("ab_wv", [D, 2048], F32),
        ("ab_wg", [D, 2048], F32), ("ab_wo", [2048, D], F32), ("ab_ws1", [D, 16], F32),
        ("ab_ws2T", [16, 1024], F32), ("ab_bs2T", [64, 16], F32), ("ab_gnw", [128, 1], F32),
        ("f_gp", [1024, NMLP], I32), ("f_up", [1024, NMLP], I32), ("f_dp", [8192, D], I32),
        ("f_gs", [64, NMLP], F32), ("f_us", [64, NMLP], F32), ("f_ds", [512, D], F32),
        ("c_mstrict", [128, 128], F32), ("c_mincl", [128, 128], F32),
        ("c_idenf", [128, 128], F32), ("c_idenb", [128, 128], BF16),
        ("c_onesbT", [1, 128], BF16), ("c_onesfT", [1, 128], F32),
        ("c_onescol", [128, 1], F32), ("c_cc", [128, 8], F32),
        ("c_bd", [128, 128], F32), ("c_pat8", [8, 128], BF16),
        ("c_dmask", [128, 4 * 512], F32),
        ("c_cos", [64, T], F32), ("c_sin", [64, T], F32),
        ("c_cosq", [64, T], F32), ("c_sinq", [64, T], F32),
    ]:
        din(nm, sh, dt)

    dscr("xa", [D, T], F32)
    dscr("xb", [D, T], F32)
    dscr("ln_s", [D, T], BF16)
    dscr("qk_s", [2048, T], BF16)
    dscr("qkv_s", [4096, T], BF16)
    dscr("vg_s", [4096, T], BF16)
    dscr("o_s", [2048, T], BF16)
    dscr("inter_s", [NMLP, T], BF16)
    dscr("wg_bf", [128, 16 * NMLP], BF16)
    dscr("wu_bf", [128, 16 * NMLP], BF16)
    dscr("wd_bf", [NMLP, D], BF16)

    sb = {}
    def asb(name, shape, dt):
        sb[name] = nc.alloc_sbuf_tensor(name, list(shape), dt).ap()
        return sb[name]

    for nm, sh, dt in [
        ("k_mstrict", [128, 128], F32), ("k_mincl", [128, 128], F32),
        ("k_idenf", [128, 128], F32), ("k_idenb", [128, 128], BF16),
        ("k_onesbT", [1, 128], BF16), ("k_onesfT", [1, 128], F32),
        ("k_onescol", [128, 1], F32), ("k_cc", [128, 8], F32),
        ("k_bd", [128, 128], F32), ("k_pat8", [8, 128], BF16),
        ("k_dmask", [128, 4 * 512], F32),
        ("k_vecT", [16, 4], F32), ("k_conv", [128, 32 * 4], F32),
        ("k_bs2T", [64, 16], F32), ("k_gnw", [128, 1], F32),
        ("k_ws2Tb", [16, 1024], BF16),
        ("k_negeA", [16, 1], F32), ("k_bbias", [16, 1], F32),
    ]:
        asb(nm, sh, dt)

    rows_stack = ExitStack()
    def ensure_rows():
        if "r_beta" not in sb:
            for nm, sh, dt in [("r_beta", [16, T], F32), ("r_g", [16, T], F32),
                               ("r_ssq", [64, T], F32)]:
                sb[nm] = rows_stack.enter_context(nc.sbuf_tensor(nm, sh, dt)).ap()

    ph = list(phases) if phases is not None else [
        "init", "dequant", "ln1", "aproj", "abeta", "aconv", "adelta", "aout",
        "ln2", "ffn1", "ln3", "battn", "bout", "ln4", "ffn2"]

    if debug_out:
        din("xa_init", [D, T], F32)
        din("xb_init", [D, T], F32)

    def emit_init():
        tmp = nc.alloc_sbuf_tensor("i_ws2T", [16, 1024], F32).ap()
        for src, dst in [("c_mstrict", "k_mstrict"), ("c_mincl", "k_mincl"),
                         ("c_idenf", "k_idenf"), ("c_idenb", "k_idenb"),
                         ("c_onesbT", "k_onesbT"), ("c_onesfT", "k_onesfT"),
                         ("c_onescol", "k_onescol"), ("c_cc", "k_cc"),
                         ("c_bd", "k_bd"), ("c_pat8", "k_pat8"), ("c_dmask", "k_dmask"),
                         ("aa_vecT", "k_vecT"), ("ab_bs2T", "k_bs2T"), ("ab_gnw", "k_gnw")]:
            b.dma("qS", sb[dst], dram[src])
        b.dma("qS", tmp, dram["ab_ws2T"])
        b.dma("qS", sb["k_conv"].rearrange("p (a c) -> p a c", c=4),
              dram["aa_conv"].rearrange("(a p) c -> p a c", p=128))
        ld0 = b.cnt["qS"]
        b.wait(nc.scalar, "qS", ld0)
        nc.scalar.activation(out=sb["k_negeA"], in_=sb["k_vecT"][:, 2:3], func=AF.Exp)
        b.inc(nc.scalar.activation(out=sb["k_ws2Tb"], in_=tmp, func=AF.Copy), "sA")
        b.wait(nc.vector, "sA", b.cnt["sA"])
        nc.vector.tensor_scalar_mul(out=sb["k_negeA"], in0=sb["k_negeA"], scalar1=-1.0)
        b.inc(nc.vector.tensor_add(out=sb["k_bbias"], in0=sb["k_vecT"][:, 0:1],
                                   in1=sb["k_vecT"][:, 1:2]), "sV")
        if debug_out:
            b.dma("qS", dram["xa"], dram["xa_init"])
            b.dma("qS", dram["xb"], dram["xb_init"])

    def emit_ln(xsrc, tag):
        st = ExitStack()
        pX = [st.enter_context(nc.sbuf_tensor(f"{tag}_x{i}", [128, 16 * NTT], F32)).ap()
              for i in range(2)]
        sq = [st.enter_context(nc.sbuf_tensor(f"{tag}_sq{i}", [128, NTT], F32)).ap()
              for i in range(2)]
        rows = st.enter_context(nc.sbuf_tensor(f"{tag}_rows", [1, 5 * NTT], F32)).ap()
        lno = [st.enter_context(nc.sbuf_tensor(f"{tag}_o{i}", [128, NTT], BF16)).ap()
               for i in range(2)]
        t1 = st.enter_context(nc.sbuf_tensor(f"{tag}_t1", [128, NTT], F32)).ap()
        ps_sum = st.enter_context(nc.psum_tensor(f"{tag}_psum", [1, NTT], F32)).ap()
        ps_sq = st.enter_context(nc.psum_tensor(f"{tag}_psq", [1, NTT], F32)).ap()
        ps_mb = st.enter_context(nc.psum_tensor(f"{tag}_pmb", [128, NTT], F32)).ap()
        ps_rb = st.enter_context(nc.psum_tensor(f"{tag}_prb", [128, NTT], F32)).ap()
        norm_done = [0] * NT
        row_done = [0] * NT
        for tt in range(NT):
            buf = pX[tt % 2]
            if tt >= 2:
                b.wait(nc.sync, "sV", norm_done[tt - 2])
            for c in range(16):
                b.dma("qS", buf[:, c * NTT:(c + 1) * NTT],
                      dram[xsrc][c * 128:(c + 1) * 128, tt * NTT:(tt + 1) * NTT])
            lq = b.cnt["qS"]
            b.wait(nc.scalar, "qS", lq)
            b.wait(nc.tensor, "qS", lq)
            if tt >= 1:
                b.wait(nc.tensor, "sV", row_done[tt - 1])
            sA0 = b.cnt["sA"]
            sP0 = b.cnt["sP"]
            for c in range(16):
                if c >= 2:
                    b.wait(nc.scalar, "sP", sP0 + c - 1)
                b.inc(nc.scalar.activation(out=sq[c % 2], in_=buf[:, c * NTT:(c + 1) * NTT],
                                           func=AF.Square), "sA")
                nc.tensor.matmul(ps_sum, sb["k_onescol"], buf[:, c * NTT:(c + 1) * NTT],
                                 start=(c == 0), stop=(c == 15))
                b.wait(nc.tensor, "sA", sA0 + c + 1)
                mm = nc.tensor.matmul(ps_sq, sb["k_onescol"], sq[c % 2],
                                      start=(c == 0), stop=(c == 15))
                b.inc(mm, "sP")
            stats_done = b.cnt["sP"]
            b.wait(nc.vector, "sP", stats_done)
            mean, mean2, var, lnv, rstd = (rows[:, i * NTT:(i + 1) * NTT] for i in range(5))
            nc.vector.tensor_scalar_mul(out=mean, in0=ps_sum, scalar1=1.0 / D)
            nc.vector.tensor_tensor(out=mean2, in0=mean, in1=mean, op=OP.mult)
            rd = b.inc(nc.vector.scalar_tensor_tensor(out=var, in0=ps_sq, scalar=1.0 / D,
                                                      in1=mean2, op0=OP.mult, op1=OP.subtract), "sV")
            row_done[tt] = rd
            b.wait(nc.scalar, "sV", rd)
            nc.scalar.activation(out=lnv, in_=var, func=AF.Ln, bias=sb["k_cc"][0:1, 0:1])
            rs = b.inc(nc.scalar.activation(out=rstd, in_=lnv, func=AF.Exp, scale=-0.5), "sA")
            b.wait(nc.tensor, "sA", rs)
            if tt >= 1:
                b.wait(nc.tensor, "sV", norm_done[tt - 1])
            nc.tensor.matmul(ps_mb, sb["k_onesfT"], mean, start=True, stop=True)
            bc = b.inc(nc.tensor.matmul(ps_rb, sb["k_onesfT"], rstd, start=True, stop=True), "sP")
            b.wait(nc.vector, "sP", bc)
            for c in range(16):
                if c >= 2:
                    b.wait(nc.vector, "qG", b.cnt["qG"] - 16)
                nc.vector.tensor_sub(out=t1, in0=buf[:, c * NTT:(c + 1) * NTT], in1=ps_mb)
                v = b.inc(nc.vector.tensor_mul(out=lno[c % 2], in0=t1, in1=ps_rb), "sV")
                b.wait(nc.gpsimd, "sV", v)
                b.dma("qG", dram["ln_s"][c * 128:(c + 1) * 128, tt * NTT:(tt + 1) * NTT],
                      lno[c % 2])
            norm_done[tt] = b.cnt["sV"]
        st.close()

    def emit_proj(wname, act_src, Kdim, Dout, tag, evac, act_rows0=0):
        st = ExitStack()
        KC = Kdim // 128
        OC = Dout // 128
        wld = [st.enter_context(nc.sbuf_tensor(f"{tag}_wl{i}", [128, KC * 128], F32)).ap()
               for i in range(2)]
        wbf = st.enter_context(nc.sbuf_tensor(f"{tag}_wb", [128, OC * KC * 128], BF16)).ap()
        pA = [st.enter_context(nc.sbuf_tensor(f"{tag}_a{i}", [128, KC * NTT], BF16)).ap()
              for i in range(2)]
        pp = [st.enter_context(nc.psum_tensor(f"{tag}_p{i}", [128, NTT], F32)).ap()
              for i in range(2)]
        cvt_done = [0] * OC
        for oc in range(OC):
            if oc >= 2:
                b.wait(nc.sync, "sA", cvt_done[oc - 2])
            for kc in range(KC):
                b.dma("qS", wld[oc % 2][:, kc * 128:(kc + 1) * 128],
                      dram[wname][kc * 128:(kc + 1) * 128, oc * 128:(oc + 1) * 128])
            b.wait(nc.scalar, "qS", b.cnt["qS"])
            cvt_done[oc] = b.inc(
                nc.scalar.activation(out=wbf[:, oc * KC * 128:(oc + 1) * KC * 128],
                                     in_=wld[oc % 2], func=AF.Copy), "sA")
        wdone = b.cnt["sA"]
        evac_done = [0] * NT
        for tt in range(NT):
            abuf = pA[tt % 2]
            if tt >= 2:
                b.wait(nc.sync, "sP", evac_done[tt - 2])
            for c in range(KC):
                b.dma("qS", abuf[:, c * NTT:(c + 1) * NTT],
                      dram[act_src][(act_rows0 + c * 128):(act_rows0 + (c + 1) * 128),
                                    tt * NTT:(tt + 1) * NTT])
            lq = b.cnt["qS"]
            b.wait(nc.tensor, "qS", lq)
            if tt == 0:
                b.wait(nc.tensor, "sA", wdone)
            for oc in range(OC):
                ps = pp[oc % 2]
                if oc >= 2:
                    b.wait(nc.tensor, "sV", b.evac_v[(tt, oc - 2)])
                elif tt > 0 and oc == 0:
                    b.wait(nc.tensor, "sV", b.evac_v[(tt - 1, OC - 2)])
                elif tt > 0:
                    b.wait(nc.tensor, "sV", b.evac_v[(tt - 1, OC - 1)])
                wb = wbf[:, oc * KC * 128:(oc + 1) * KC * 128]
                for kc in range(KC):
                    mm = nc.tensor.matmul(ps, wb[:, kc * 128:(kc + 1) * 128],
                                          abuf[:, kc * NTT:(kc + 1) * NTT],
                                          start=(kc == 0), stop=(kc == KC - 1))
                mp = b.inc(mm, "sP")
                b.wait(nc.vector, "sP", mp)
                evac(oc, tt, ps)
                b.evac_v[(tt, oc)] = b.cnt["sV"]
            evac_done[tt] = b.cnt["sP"]
        st.close()

    def emit_aproj():
        st = ExitStack()
        stage = [st.enter_context(nc.sbuf_tensor(f"aq_s{i}", [128, NTT], BF16)).ap()
                 for i in range(2)]
        def mk_evac(dst):
            def evac(oc, tt, ps):
                s = stage[oc % 2]
                key = oc % 2
                if key in b.stage_dma:
                    b.wait(nc.vector, "qG", b.stage_dma[key])
                v = b.inc(nc.vector.tensor_copy(out=s, in_=ps), "sV")
                b.wait(nc.gpsimd, "sV", v)
                b.stage_dma[key] = b.dma(
                    "qG", dram[dst][oc * 128:(oc + 1) * 128, tt * NTT:(tt + 1) * NTT], s)
            return evac
        b.stage_dma = {}
        b.evac_v = {}
        emit_proj("aa_qkw", "ln_s", D, 2048, "aqk", mk_evac("qk_s"))
        b.barrier()
        b.evac_v = {}
        emit_proj("aa_vgw", "ln_s", D, 4096, "avg", mk_evac("vg_s"))
        st.close()

    def emit_abeta():
        ensure_rows()
        st = ExitStack()
        wl = st.enter_context(nc.sbuf_tensor("ab_wl", [128, 16 * 32], F32)).ap()
        wb = st.enter_context(nc.sbuf_tensor("ab_wb", [128, 16 * 32], BF16)).ap()
        aT = [st.enter_context(nc.sbuf_tensor(f"ab_a{i}", [128, 16 * NTT], BF16)).ap()
              for i in range(2)]
        xsb = st.enter_context(nc.sbuf_tensor("ab_x", [16, NTT], F32)).ap()
        sp = st.enter_context(nc.sbuf_tensor("ab_sp", [16, NTT], F32)).ap()
        l1 = st.enter_context(nc.sbuf_tensor("ab_l1", [16, NTT], F32)).ap()
        ps_b = st.enter_context(nc.psum_tensor("ab_pb", [16, NTT], F32)).ap()
        ps_a = st.enter_context(nc.psum_tensor("ab_pa", [16, NTT], F32)).ap()
        for kc in range(16):
            b.dma("qS", wl[:, kc * 32:(kc + 1) * 32],
                  dram["aa_abw"][kc * 128:(kc + 1) * 128, :])
        b.wait(nc.scalar, "qS", b.cnt["qS"])
        wc = b.inc(nc.scalar.activation(out=wb, in_=wl, func=AF.Copy), "sA")
        b.wait(nc.tensor, "sA", wc)
        prev_done = 0
        for tt in range(NT):
            abuf = aT[tt % 2]
            if tt >= 2:
                b.wait(nc.sync, "sP", b.aprev[tt - 2])
            for c in range(16):
                b.dma("qS", abuf[:, c * NTT:(c + 1) * NTT],
                      dram["ln_s"][c * 128:(c + 1) * 128, tt * NTT:(tt + 1) * NTT])
            b.wait(nc.tensor, "qS", b.cnt["qS"])
            b.wait(nc.tensor, "sA", prev_done)
            for kc in range(16):
                w2 = wb[:, kc * 32:(kc + 1) * 32]
                nc.tensor.matmul(ps_b, w2[:, 0:16], abuf[:, kc * NTT:(kc + 1) * NTT],
                                 start=(kc == 0), stop=(kc == 15))
                mm = nc.tensor.matmul(ps_a, w2[:, 16:32], abuf[:, kc * NTT:(kc + 1) * NTT],
                                      start=(kc == 0), stop=(kc == 15))
            mp = b.inc(mm, "sP")
            b.aprev = getattr(b, "aprev", [0] * NT)
            b.aprev[tt] = mp
            b.wait(nc.scalar, "sP", mp)
            sg = b.inc(nc.scalar.activation(out=sp, in_=ps_b, func=AF.Sigmoid,
                                            bias=sb["k_bbias"]), "sA")
            b.wait(nc.vector, "sA", sg)
            b.inc(nc.vector.tensor_scalar_mul(out=sb["r_beta"][:, tt * NTT:(tt + 1) * NTT],
                                              in0=sp, scalar1=2.0), "sV")
            nc.scalar.activation(out=xsb, in_=ps_a, func=AF.Identity,
                                 bias=sb["k_vecT"][:, 3:4])
            nc.scalar.activation(out=sp, in_=xsb, func=AF.Abs)
            nc.scalar.activation(out=sp, in_=sp, func=AF.Exp, scale=-1.0)
            pd = b.inc(nc.scalar.activation(out=l1, in_=sp, func=AF.Ln,
                                            bias=sb["k_cc"][0:16, 4:5]), "sA")
            prev_done = pd
            b.wait(nc.vector, "sA", pd)
            nc.vector.tensor_scalar_max(out=xsb, in0=xsb, scalar1=0.0)
            nc.vector.tensor_add(out=xsb, in0=xsb, in1=l1)
            b.inc(nc.vector.tensor_scalar_mul(out=sb["r_g"][:, tt * NTT:(tt + 1) * NTT],
                                              in0=xsb, scalar1=sb["k_negeA"]), "sV")
        st.close()

    def emit_aconv():
        ensure_rows()
        st = ExitStack()
        xin = [st.enter_context(nc.sbuf_tensor(f"cv_x{i}", [128, NTT + 3], BF16)).ap()
               for i in range(2)]
        acc = st.enter_context(nc.sbuf_tensor("cv_acc", [128, NTT], F32)).ap()
        acc2 = st.enter_context(nc.sbuf_tensor("cv_acc2", [128, NTT], F32)).ap()
        sil = st.enter_context(nc.sbuf_tensor("cv_sil", [128, NTT], F32)).ap()
        silb = [st.enter_context(nc.sbuf_tensor(f"cv_sb{i}", [128, NTT], BF16)).ap()
                for i in range(2)]
        sq = st.enter_context(nc.sbuf_tensor("cv_sq", [128, NTT], F32)).ap()
        ps_q = st.enter_context(nc.psum_tensor("cv_psq", [16, NTT], F32)).ap()
        ps_k = st.enter_context(nc.psum_tensor("cv_psk", [16, NTT], F32)).ap()
        sq = [sq, st.enter_context(nc.sbuf_tensor("cv_sq2", [128, NTT], F32)).ap()]
        rowcopy_done = 0
        dma_done = {}
        for tt in range(NT):
            t0 = tt * NTT
            for ct in range(32):
                src = "qk_s" if ct < 16 else "vg_s"
                row0 = ct * 128 if ct < 16 else (ct - 16) * 128
                dst_row0 = ct * 128
                buf = xin[ct % 2]
                b.wait(nc.sync, "sV", getattr(b, "cv_done", 0))
                if tt == 0:
                    b.wait(nc.vector, "qS", b.cnt["qS"])
                    mz = b.inc(nc.vector.memset(buf[:, 0:3], 0.0), "sV")
                    b.wait(nc.sync, "sV", mz)
                    b.dma("qS", buf[:, 3:NTT + 3],
                          dram[src][row0:row0 + 128, t0:t0 + NTT])
                else:
                    b.dma("qS", buf[:, 0:NTT + 3],
                          dram[src][row0:row0 + 128, t0 - 3:t0 + NTT])
                lq = b.cnt["qS"]
                b.wait(nc.vector, "qS", lq)
                wcol = lambda tau: sb["k_conv"][:, ct * 4 + tau:ct * 4 + tau + 1]
                nc.vector.tensor_scalar_mul(out=acc, in0=buf[:, 0:NTT], scalar1=wcol(0))
                nc.vector.tensor_scalar_mul(out=acc2, in0=buf[:, 1:NTT + 1], scalar1=wcol(1))
                nc.vector.tensor_add(out=acc, in0=acc, in1=acc2)
                nc.vector.tensor_scalar_mul(out=acc2, in0=buf[:, 2:NTT + 2], scalar1=wcol(2))
                nc.vector.tensor_add(out=acc, in0=acc, in1=acc2)
                nc.vector.tensor_scalar_mul(out=acc2, in0=buf[:, 3:NTT + 3], scalar1=wcol(3))
                cd = b.inc(nc.vector.tensor_add(out=acc, in0=acc, in1=acc2), "sV")
                b.cv_done = cd
                b.wait(nc.scalar, "sV", cd)
                if (tt, ct - 2) in dma_done:
                    b.wait(nc.scalar, "qG", dma_done[(tt, ct - 2)])
                    b.wait(nc.vector, "qG", dma_done[(tt, ct - 2)])
                if ct < 16:
                    ps = ps_q if ct < 8 else ps_k
                    sl = b.inc(nc.scalar.activation(out=sil, in_=acc, func=AF.Silu), "sA")
                    b.wait(nc.vector, "sA", sl)
                    b.inc(nc.vector.tensor_copy(out=silb[ct % 2], in_=sil), "sV")
                    if ct >= 2:
                        b.wait(nc.scalar, "sP", b.cnt["sP"])
                    sqd = b.inc(nc.scalar.activation(out=sq[ct % 2], in_=sil,
                                                     func=AF.Square), "sA")
                    b.wait(nc.tensor, "sA", sqd)
                    if ct % 8 == 0:
                        b.wait(nc.tensor, "sV", rowcopy_done)
                    mm = nc.tensor.matmul(ps, sb["k_bd"][:, (ct % 8) * 16:(ct % 8) * 16 + 16],
                                          sq[ct % 2], start=(ct % 8 == 0), stop=(ct % 8 == 7))
                    mp = b.inc(mm, "sP")
                    if ct % 8 == 7:
                        b.wait(nc.vector, "sP", mp)
                        roff = 0 if ct < 8 else 32
                        rowcopy_done = b.inc(nc.vector.tensor_copy(
                            out=sb["r_ssq"][roff:roff + 16, t0:t0 + NTT], in_=ps), "sV")
                else:
                    b.inc(nc.scalar.activation(out=silb[ct % 2], in_=acc, func=AF.Silu), "sA")
                wsem = "sV" if ct < 16 else "sA"
                b.wait(nc.gpsimd, wsem, b.cnt[wsem])
                dma_done[(tt, ct)] = b.dma("qG", dram["qkv_s"][dst_row0:dst_row0 + 128,
                                                               t0:t0 + NTT], silb[ct % 2])
        st.close()

    def emit_adelta():
        st = ExitStack()
        if dbg_delta is not None:
            dram["ddump"] = nc.dram_tensor("ddump", [128, 24 * 128], F32,
                                           kind="ExternalOutput").ap()
            ddmp = st.enter_context(nc.sbuf_tensor("ad_ddmp", [128, 24 * 128], F32)).ap()
        def dcap(n, h, slot, ap):
            if dbg_delta is not None and (n, h) == dbg_delta:
                nc.vector.tensor_copy(out=ddmp[0:ap.shape[0], slot * 128:
                                               slot * 128 + ap.shape[1]], in_=ap)
        Sst = st.enter_context(nc.sbuf_tensor("ad_S", [64, H * DV], F32)).ap()
        qch = [st.enter_context(nc.sbuf_tensor(f"ad_q{i}", [128, 8 * 128], BF16)).ap()
               for i in range(2)]
        kch = [st.enter_context(nc.sbuf_tensor(f"ad_k{i}", [128, 8 * 128], BF16)).ap()
               for i in range(2)]
        vch = [st.enter_context(nc.sbuf_tensor(f"ad_v{i}", [128, H * 128], BF16)).ap()
               for i in range(2)]
        gch = [st.enter_context(nc.sbuf_tensor(f"ad_g{i}", [128, H * 128], BF16)).ap()
               for i in range(2)]
        och = [st.enter_context(nc.sbuf_tensor(f"ad_o{i}", [128, H * 128], BF16)).ap()
               for i in range(2)]
        cols = {}
        for nm in ["gT", "ngT", "aT", "bT", "rkT", "rqT", "ccT", "barkT",
                   "rqsT", "oi2T", "gam"]:
            cols[nm] = st.enter_context(nc.sbuf_tensor(f"ad_{nm}", [128, 16], F32)).ap()
        aCbc = st.enter_context(nc.sbuf_tensor("ad_aCbc", [64, 16], F32)).ap()
        aCr = st.enter_context(nc.sbuf_tensor("ad_aCr", [1, 16], F32)).ap()
        gchunk = st.enter_context(nc.sbuf_tensor("ad_gch", [16, 128], F32)).ap()

        def mk2(nm, sh, dt):
            return [st.enter_context(nc.sbuf_tensor(f"ad_{nm}{i}", sh, dt)).ap()
                    for i in range(2)]
        D0 = mk2("D0", [128, 128], F32)
        S0b = mk2("S0b", [128, 128], BF16)
        Zs = mk2("Zs", [128, 128], BF16)
        Ys = mk2("Ys", [128, 128], BF16)
        Yn = mk2("Yn", [128, 128], BF16)
        Pqs = mk2("Pqs", [128, 128], BF16)
        Y2s = mk2("Y2s", [128, 128], BF16)
        Y2n = mk2("Y2n", [128, 128], BF16)
        Ps = mk2("Ps", [128, 128], BF16)
        Rb = mk2("Rb", [128, 128], BF16)
        KS0s = mk2("KS0s", [128, 128], F32)
        Zrb = mk2("Zrb", [128, 128], BF16)
        zcs = mk2("zcs", [128, 128], F32)
        Ub = mk2("Ub", [128, 128], BF16)
        OIs = mk2("OIs", [128, 128], F32)
        Of = mk2("Of", [128, 128], F32)
        Onb = mk2("Onb", [128, 128], BF16)
        Klam = mk2("Klam", [128, 64], BF16)
        S0sc = mk2("S0sc", [64, 128], F32)
        osq = mk2("osq", [128, 1], F32)
        lamrk = mk2("lamrk", [128, 1], F32)
        gsil = mk2("gsil", [128, 128], F32)
        Gm = mk2("Gm", [128, 128], F32)
        def mk2p(nm, sh, dt):
            return [st.enter_context(nc.psum_tensor(f"ad_{nm}{i}", sh, dt)).ap()
                    for i in range(2)]
        pfA = mk2p("pfA", [128, 512], F32)
        pfB = mk2p("pfB", [128, 512], F32)
        pbb = mk2p("pbb", [128, 512], BF16)
        pC = st.enter_context(nc.psum_tensor("ad_pC", [128, 512], F32)).ap()
        pCol = pC[:, 0:16]
        pRow = pC[0:1, 16:32]
        pBc = pC[0:64, 32:48]

        b.inc(nc.vector.memset(Sst, 0.0), "sV")
        if dbg_delta is not None:
            b.inc(nc.vector.memset(ddmp, 0.0), "sV")
        qkv_pl = dram["qkv_s"].rearrange("(a p) t -> p a t", p=128)

        def QcS(h, n):
            a = h // 2
            r0 = (h % 2) * 64
            return qch[n % 2][r0:r0 + 64, a * 128:(a + 1) * 128]
        def KcS(h, n):
            a = h // 2
            r0 = (h % 2) * 64
            return kch[n % 2][r0:r0 + 64, a * 128:(a + 1) * 128]

        veV = [0, 0]
        veA = [0, 0]
        peP = [0, 0]
        prev_o_dma = [0, 0]
        vfree = [0] * (NCH + 2)
        colfree = [0] * (NCH + 2)

        for n in range(NCH):
            par = n % 2
            t0 = n * 128
            if n >= 2:
                b.wait(nc.sync, "sV", b.ad_ochfree[n - 2])
            b.dma("qS", qch[par].rearrange("p (a t) -> p a t", t=128),
                  qkv_pl[:, 0:8, t0:t0 + 128])
            b.dma("qS", kch[par].rearrange("p (a t) -> p a t", t=128),
                  qkv_pl[:, 8:16, t0:t0 + 128])
            for half in range(2):
                b.dma("qS", vch[par][:, half * 1024:(half + 1) * 1024]
                      .rearrange("p (a t) -> p a t", t=128),
                      qkv_pl[:, 16 + 8 * half:16 + 8 * (half + 1), t0:t0 + 128])
            vg_pl = dram["vg_s"].rearrange("(a p) t -> p a t", p=128)
            for half in range(2):
                b.dma("qS", gch[par][:, half * 1024:(half + 1) * 1024]
                      .rearrange("p (a t) -> p a t", t=128),
                      vg_pl[:, 16 + 8 * half:16 + 8 * (half + 1), t0:t0 + 128])
            vg_ld = b.cnt["qS"]
            b.wait(nc.vector, "sP", colfree[n])
            sc = b.inc(nc.vector.tensor_tensor_scan(
                out=gchunk, data0=sb["r_g"][:, t0:t0 + 128],
                data1=sb["r_g"][:, t0:t0 + 128], initial=0.0,
                op0=OP.add, op1=OP.bypass), "sV")
            b.wait(nc.tensor, "sV", sc)
            b.wait(nc.tensor, "sA", veA[0])
            b.wait(nc.tensor, "sA", veA[1])
            tr1 = b.inc(nc.tensor.transpose(pCol, gchunk, sb["k_idenf"][0:16, 0:16]), "sP")
            b.wait(nc.vector, "sP", tr1)
            b.dv(nc.vector.tensor_copy(out=cols["gT"], in_=pCol))
            gv = b.inc(nc.vector.tensor_scalar_mul(out=cols["ngT"], in0=cols["gT"],
                                                   scalar1=-1.0), "sV")
            b.wait(nc.scalar, "sV", gv)
            aTd = b.inc(nc.scalar.activation(out=cols["aT"], in_=cols["gT"], func=AF.Exp), "sA")
            gcC = b.inc(nc.scalar.activation(out=cols["gam"][0:16, 0:1],
                                             in_=gchunk[:, 127:128], func=AF.Exp), "sA")
            b.wait(nc.tensor, "sA", gcC)
            tr2 = b.inc(nc.tensor.transpose(pRow, cols["gam"][0:16, 0:1],
                                            sb["k_idenf"][0:16, 0:16]), "sP")
            b.wait(nc.vector, "sP", tr2)
            rw = b.inc(nc.vector.tensor_copy(out=aCr, in_=pRow), "sV")
            b.wait(nc.tensor, "sV", rw)
            bc = b.inc(nc.tensor.matmul(pBc, sb["k_onesfT"][:, 0:64], aCr,
                                        start=True, stop=True), "sP")
            tr3 = b.inc(nc.tensor.transpose(pCol, sb["r_beta"][:, t0:t0 + 128],
                                            sb["k_idenf"][0:16, 0:16]), "sP")
            b.wait(nc.vector, "sP", tr3)
            nc.vector.tensor_copy(out=aCbc, in_=pBc)
            btd = b.dv(nc.vector.tensor_copy(out=cols["bT"], in_=pCol))
            b.wait(nc.tensor, "sV", btd)
            tr4 = b.inc(nc.tensor.transpose(pCol, sb["r_ssq"][0:16, t0:t0 + 128],
                                            sb["k_idenf"][0:16, 0:16]), "sP")
            b.wait(nc.scalar, "sP", tr4)
            nc.scalar.activation(out=cols["rqT"], in_=pCol, func=AF.Ln,
                                 bias=sb["k_cc"][:, 1:2])
            rqd = b.inc(nc.scalar.activation(out=cols["rqT"], in_=cols["rqT"],
                                             func=AF.Exp, scale=-0.5), "sA")
            b.wait(nc.tensor, "sA", rqd)
            tr5 = b.inc(nc.tensor.transpose(pCol, sb["r_ssq"][32:48, t0:t0 + 128],
                                            sb["k_idenf"][32:48, 32:48]), "sP")
            b.wait(nc.scalar, "sP", tr5)
            nc.scalar.activation(out=cols["rkT"], in_=pCol, func=AF.Ln,
                                 bias=sb["k_cc"][:, 1:2])
            rkd = b.inc(nc.scalar.activation(out=cols["rkT"], in_=cols["rkT"],
                                             func=AF.Exp, scale=-0.5), "sA")
            b.wait(nc.vector, "sA", rkd)
            b.dv(nc.vector.tensor_mul(out=cols["ccT"], in0=cols["bT"], in1=cols["rkT"]))
            nc.vector.tensor_mul(out=cols["barkT"], in0=cols["ccT"], in1=cols["aT"])
            b.dv(nc.vector.tensor_scalar_mul(out=cols["rqsT"], in0=cols["rqT"],
                                             scalar1=SCALE_A))
            cold = b.inc(nc.vector.tensor_mul(out=cols["oi2T"], in0=cols["aT"],
                                              in1=cols["rqsT"]), "sV")
            colfree[n + 1] = b.cnt["sP"]
            b.wait(nc.tensor, "sV", cold)
            b.wait(nc.tensor, "qS", vg_ld)
            b.wait(nc.scalar, "qS", vg_ld)

            for hp in range(8):
                pair = (2 * hp, 2 * hp + 1)

                def ch(h):
                    return cols

                def pe_wait(i):
                    if SERIAL_DELTA:
                        b.wait(nc.tensor, "sV", b.cnt["sV"])
                        b.wait(nc.tensor, "sA", b.cnt["sA"])
                    else:
                        b.wait(nc.tensor, "sV", veV[i])
                        b.wait(nc.tensor, "sA", veA[i])

                def ve_wait(i):
                    b.wait(nc.vector, "sP", b.cnt["sP"] if SERIAL_DELTA else peP[i])
                def va_wait(i):
                    b.wait(nc.scalar, "sP", b.cnt["sP"] if SERIAL_DELTA else peP[i])

                for h in pair:
                    i = h % 2
                    pe_wait(i)
                    nc.tensor.matmul(pfA[i][:, 0:128], KcS(h, n), KcS(h, n),
                                     start=True, stop=True)
                    peP[i] = b.inc(nc.tensor.matmul(pfA[i][:, 128:256], KcS(h, n),
                                                    QcS(h, n), start=True, stop=True), "sP")
                for h in pair:
                    i = h % 2
                    b.wait(nc.scalar, "sV", veV[i])
                    veA[i] = b.inc(nc.scalar.activation(
                        out=D0[i], in_=sb["k_mstrict"], func=AF.Identity, scale=0.0,
                        bias=cols["gT"][:, h:h + 1]), "sA")
                for h in pair:
                    i = h % 2
                    b.wait(nc.tensor, "sA", veA[i])
                    peP[i] = b.inc(nc.tensor.transpose(pfB[i][:, 0:128], D0[i],
                                                       sb["k_idenf"]), "sP")
                for h in pair:
                    i = h % 2
                    ve_wait(i)
                    b.wait(nc.vector, "sA", veA[i])
                    gm = b.inc(nc.vector.tensor_scalar(
                        out=Gm[i], in0=pfB[i][:, 0:128],
                        scalar1=cols["ngT"][:, h:h + 1], scalar2=0.0,
                        op0=OP.add, op1=OP.min), "sV")
                    dcap(n, h, 22, pfB[i][:, 0:128])
                    b.wait(nc.scalar, "sV", gm)
                    d0 = b.inc(nc.scalar.activation(out=D0[i], in_=Gm[i],
                                                    func=AF.Exp), "sA")
                    veA[i] = d0
                    b.wait(nc.vector, "sA", d0)
                    nc.vector.scalar_tensor_tensor(out=Zs[i], in0=pfA[i][:, 0:128],
                                                   scalar=cols["rkT"][:, h:h + 1],
                                                   in1=D0[i], op0=OP.mult, op1=OP.mult)
                    nc.vector.tensor_mul(out=Zs[i], in0=Zs[i], in1=sb["k_mstrict"])
                    nc.vector.tensor_scalar_mul(out=Ys[i], in0=Zs[i],
                                                scalar1=cols["ccT"][:, h:h + 1])
                    nc.vector.scalar_tensor_tensor(out=Pqs[i], in0=pfA[i][:, 128:256],
                                                   scalar=cols["rkT"][:, h:h + 1],
                                                   in1=D0[i], op0=OP.mult, op1=OP.mult)
                    nc.vector.tensor_mul(out=Pqs[i], in0=Pqs[i], in1=sb["k_mincl"])
                    b.dv(nc.vector.tensor_mul(out=lamrk[i], in0=D0[i][:, 127:128],
                                               in1=cols["rkT"][:, h:h + 1]))
                    nc.vector.tensor_copy(out=S0b[i][0:64, :], in_=Sst[:, h * DV:(h + 1) * DV])
                    nc.vector.tensor_copy(out=S0b[i][64:128, :], in_=Sst[:, h * DV:(h + 1) * DV])
                    dcap(n, h, 0, D0[i])
                    dcap(n, h, 1, Zs[i])
                    dcap(n, h, 2, Ys[i])
                    dcap(n, h, 10, Pqs[i])
                    dcap(n, h, 21, Gm[i])
                    veV[i] = b.inc(nc.vector.tensor_sub(out=Ps[i], in0=sb["k_idenb"],
                                                        in1=Ys[i]), "sV")
                for h in pair:
                    i = h % 2
                    pe_wait(i)
                    nc.tensor.transpose(pbb[i][:, 0:128], Ys[i], sb["k_idenb"])
                    r0 = (h % 2) * 64
                    peP[i] = b.inc(nc.tensor.transpose(pbb[i][:, 128:192], KcS(h, n),
                                                       sb["k_idenb"][r0:r0 + 64, r0:r0 + 64]), "sP")
                for h in pair:
                    i = h % 2
                    va_wait(i)
                    nc.scalar.activation(out=Yn[i], in_=pbb[i][:, 0:128], func=AF.Copy)
                    veA[i] = b.inc(nc.scalar.activation(
                        out=Klam[i], in_=pbb[i][:, 128:192], func=AF.Copy,
                        scale=lamrk[i]), "sA")
                cur_s, cur_n = Ys, Yn
                for k in range(6):
                    for h in pair:
                        i = h % 2
                        pe_wait(i)
                        nc.tensor.matmul(pfA[i][:, 0:128], cur_n[i], cur_s[i],
                                         start=True, stop=True)
                        peP[i] = b.inc(nc.tensor.matmul(pfA[i][:, 128:256], cur_s[i],
                                                        cur_n[i], start=True, stop=True), "sP")
                    for h in pair:
                        i = h % 2
                        va_wait(i)
                        nc.scalar.activation(out=Y2s[i], in_=pfA[i][:, 0:128],
                                             func=AF.Copy)
                        veA[i] = b.inc(nc.scalar.activation(
                            out=Y2n[i], in_=pfA[i][:, 128:256], func=AF.Copy), "sA")
                    for h in pair:
                        i = h % 2
                        pe_wait(i)
                        peP[i] = b.inc(nc.tensor.matmul(pfB[i][:, 0:128], Y2n[i], Ps[i],
                                                        start=True, stop=True), "sP")
                    for h in pair:
                        i = h % 2
                        ve_wait(i)
                        veV[i] = b.inc(nc.vector.tensor_add(out=Ps[i], in0=pfB[i][:, 0:128],
                                                            in1=Ps[i]), "sV")
                    cur_s, cur_n = Y2s, Y2n
                for h in pair:
                    i = h % 2
                    pe_wait(i)
                    r0 = (h % 2) * 64
                    nc.tensor.matmul(pfA[i][:, 0:128], KcS(h, n), S0b[i][r0:r0 + 64, :],
                                     start=True, stop=True)
                    peP[i] = b.inc(nc.tensor.transpose(pbb[i][:, 256:384],
                                                       vch[par][:, h * 128:(h + 1) * 128],
                                                       sb["k_idenb"]), "sP")
                for h in pair:
                    i = h % 2
                    va_wait(i)
                    b.wait(nc.scalar, "sV", veV[i])
                    ke = b.inc(nc.scalar.activation(out=KS0s[i], in_=pfA[i][:, 0:128],
                                                    func=AF.Copy,
                                                    scale=cols["barkT"][:, h:h + 1]), "sA")
                    veA[i] = ke
                    ve_wait(i)
                    b.wait(nc.vector, "sA", ke)
                    veV[i] = b.inc(nc.vector.scalar_tensor_tensor(
                        out=Rb[i], in0=pbb[i][:, 256:384],
                        scalar=cols["bT"][:, h:h + 1], in1=KS0s[i],
                        op0=OP.mult, op1=OP.subtract), "sV")
                for h in pair:
                    i = h % 2
                    pe_wait(i)
                    peP[i] = b.inc(nc.tensor.matmul(pfB[i][:, 0:128], Zs[i], Rb[i],
                                                    start=True, stop=True), "sP")
                for h in pair:
                    i = h % 2
                    va_wait(i)
                    veA[i] = b.inc(nc.scalar.activation(out=Zrb[i],
                                                        in_=pfB[i][:, 0:128],
                                                        func=AF.Copy), "sA")
                for h in pair:
                    i = h % 2
                    pe_wait(i)
                    peP[i] = b.inc(nc.tensor.matmul(pfA[i][:, 0:128], Ps[i], Zrb[i],
                                                    start=True, stop=True), "sP")
                for h in pair:
                    i = h % 2
                    va_wait(i)
                    b.wait(nc.scalar, "sV", veV[i])
                    zce = b.inc(nc.scalar.activation(out=zcs[i], in_=pfA[i][:, 0:128],
                                                     func=AF.Copy,
                                                     scale=cols["ccT"][:, h:h + 1]), "sA")
                    veA[i] = zce
                    ve_wait(i)
                    b.wait(nc.vector, "sA", zce)
                    veV[i] = b.inc(nc.vector.tensor_sub(out=Ub[i], in0=Rb[i],
                                                        in1=zcs[i]), "sV")
                for h in pair:
                    i = h % 2
                    pe_wait(i)
                    r0 = (h % 2) * 64
                    nc.tensor.matmul(pfB[i][:, 0:128], QcS(h, n), S0b[i][r0:r0 + 64, :],
                                     start=True, stop=True)
                    peP[i] = b.inc(nc.tensor.matmul(pfB[i][:, 128:256], Pqs[i], Ub[i],
                                                    start=True, stop=True), "sP")
                for h in pair:
                    i = h % 2
                    va_wait(i)
                    b.wait(nc.scalar, "sV", veV[i])
                    oie = b.inc(nc.scalar.activation(out=OIs[i], in_=pfB[i][:, 0:128],
                                                     func=AF.Copy,
                                                     scale=cols["oi2T"][:, h:h + 1]), "sA")
                    ve_wait(i)
                    b.wait(nc.vector, "sA", oie)
                    of = b.inc(nc.vector.scalar_tensor_tensor(
                        out=Of[i], in0=pfB[i][:, 128:256],
                        scalar=cols["rqsT"][:, h:h + 1], in1=OIs[i],
                        op0=OP.mult, op1=OP.add), "sV")
                    b.wait(nc.scalar, "sV", of)
                    nc.scalar.activation(out=zcs[i], in_=Of[i], func=AF.Square,
                                         accum_out=osq[i])
                    nc.scalar.activation(out=osq[i], in_=osq[i], func=AF.Ln,
                                         bias=sb["k_cc"][:, 2:3], scale=1.0 / DV)
                    rst = b.inc(nc.scalar.activation(out=osq[i], in_=osq[i],
                                                     func=AF.Exp, scale=-0.5), "sA")
                    veA[i] = rst
                    b.wait(nc.vector, "sA", rst)
                    veV[i] = b.inc(nc.vector.tensor_scalar_mul(out=Onb[i], in0=Of[i],
                                                               scalar1=osq[i]), "sV")
                for h in pair:
                    i = h % 2
                    pe_wait(i)
                    nc.tensor.transpose(pbb[i][:, 384:512], Onb[i], sb["k_idenb"])
                    peP[i] = b.inc(nc.tensor.matmul(pfA[i][0:64, 0:128], Klam[i], Ub[i],
                                                    start=True, stop=True), "sP")
                for h in pair:
                    i = h % 2
                    va_wait(i)
                    b.wait(nc.scalar, "sV", veV[i])
                    gsl = nc.scalar.activation(out=gsil[i],
                                               in_=gch[par][:, h * 128:(h + 1) * 128],
                                               func=AF.Silu)
                    s0s = b.inc(nc.scalar.activation(out=S0sc[i],
                                                     in_=Sst[:, h * DV:(h + 1) * DV],
                                                     func=AF.Copy,
                                                     scale=aCbc[:, h:h + 1]), "sA")
                    veA[i] = s0s
                    ve_wait(i)
                    b.wait(nc.vector, "sA", s0s)
                    if n >= 2:
                        b.wait(nc.vector, "qG", prev_o_dma[par])
                    nc.vector.tensor_mul(out=och[par][:, h * 128:(h + 1) * 128],
                                         in0=pbb[i][:, 384:512], in1=gsil[i])
                    veV[i] = b.inc(nc.vector.tensor_add(out=Sst[:, h * DV:(h + 1) * DV],
                                                        in0=pfA[i][0:64, 0:128],
                                                        in1=S0sc[i]), "sV")
                    if dbg_delta is not None and (n, h) == dbg_delta:
                        cap = [D0[i], Zs[i], Ys[i], Ps[i], Rb[i], Zrb[i], Ub[i],
                               Of[i], Onb[i], KS0s[i], Pqs[i], Yn[i], zcs[i],
                               och[par][:, h * 128:(h + 1) * 128], gsil[i]]
                        for ci, cap_t in enumerate(cap):
                            nc.vector.tensor_copy(out=ddmp[:, ci * 128:(ci + 1) * 128],
                                                  in_=cap_t)
                        for cj, nm in enumerate(["gT", "ngT", "aT", "bT", "rkT",
                                                 "rqT", "ccT", "barkT", "rqsT",
                                                 "oi2T"]):
                            nc.vector.tensor_copy(
                                out=ddmp[:, 15 * 128 + cj * 16:15 * 128 + (cj + 1) * 16],
                                in_=cols[nm])
                        nc.vector.tensor_copy(out=ddmp[0:64, 17 * 128:17 * 128 + 128],
                                              in_=Sst[:, h * DV:(h + 1) * DV])
                        nc.vector.tensor_copy(out=ddmp[0:64, 18 * 128:18 * 128 + 128],
                                              in_=S0b[i][0:64, :])
                        nc.vector.tensor_copy(out=ddmp[:, 19 * 128:19 * 128 + 1],
                                              in_=lamrk[i])
                        nc.vector.tensor_copy(out=ddmp[0:64, 20 * 128:20 * 128 + 16],
                                              in_=aCbc)
                        dd = b.inc(nc.vector.tensor_copy(
                            out=ddmp[:, 21 * 128:21 * 128 + 128], in_=zcs[i]), "sV")
                        b.wait(nc.gpsimd, "sV", dd)
                        b.dma("qG", dram["ddump"], ddmp)
            ochd = b.cnt["sV"]
            b.wait(nc.gpsimd, "sV", ochd)
            for hh in range(16):
                b.dma("qG", dram["o_s"][hh * 128:(hh + 1) * 128, t0:t0 + 128],
                      och[par][:, hh * 128:(hh + 1) * 128])
            prev_o_dma[par] = b.cnt["qG"]
            b.ad_ochfree = getattr(b, "ad_ochfree", [0] * NCH)
            b.ad_ochfree[n] = ochd
            colfree[n + 1] = b.cnt["sP"]
            vfree[n] = b.cnt["sP"]
        st.close()
        rows_stack.close()
        for nm in ["r_beta", "r_g", "r_ssq"]:
            sb.pop(nm, None)

    def emit_outproj(wname, act_src, xsrc, xdst, tag):
        st = ExitStack()
        stage = [st.enter_context(nc.sbuf_tensor(f"{tag}_s{i}", [128, NTT], F32)).ap()
                 for i in range(2)]
        xres = [st.enter_context(nc.sbuf_tensor(f"{tag}_x{i}", [128, NTT], F32)).ap()
                for i in range(2)]
        b.stage_dma = {}
        b.xadd_done = {}
        def evac(oc, tt, ps):
            key = oc % 2
            xr = xres[key]
            if key in b.xadd_done:
                b.wait(nc.scalar, "sV", b.xadd_done[key])
            b.dma("qA", xr, dram[xsrc][oc * 128:(oc + 1) * 128, tt * NTT:(tt + 1) * NTT])
            b.wait(nc.vector, "qA", b.cnt["qA"])
            s = stage[key]
            if key in b.stage_dma:
                b.wait(nc.vector, "qG", b.stage_dma[key])
            v = b.inc(nc.vector.tensor_add(out=s, in0=ps, in1=xr), "sV")
            b.xadd_done[key] = v
            b.wait(nc.gpsimd, "sV", v)
            b.stage_dma[key] = b.dma(
                "qG", dram[xdst][oc * 128:(oc + 1) * 128, tt * NTT:(tt + 1) * NTT], s)
        b.evac_v = {}
        emit_proj(wname, act_src, 2048, D, tag, evac)
        st.close()

    def emit_dequant():
        st = ExitStack()
        pk = [st.enter_context(nc.sbuf_tensor(f"dq_pk{i}", [128, 512], I32)).ap()
              for i in range(2)]
        pk_echo = st.enter_context(nc.sbuf_tensor("dq_pke", [128, 512], I32)).ap()
        scf = st.enter_context(nc.sbuf_tensor("dq_scf", [8, NMLP], F32)).ap()
        scb = st.enter_context(nc.sbuf_tensor("dq_scb", [8, NMLP], BF16)).ap()
        mki = lambda n: st.enter_context(nc.sbuf_tensor(n, [128, 512], I32)).ap()
        vc_, vt_, vi1, vi2, vsg, vbits = (mki(f"dq_v{k}") for k in "012345")
        gc_, gt_, gi1, gi2, gsg, gbits = (mki(f"dq_g{k}") for k in "012345")
        out_lo = [st.enter_context(nc.sbuf_tensor(f"dq_ol{i}", [128, 512], BF16)).ap()
                  for i in range(2)]
        pck = [st.enter_context(nc.sbuf_tensor(f"dq_pc{i}", [128, 512], BF16)).ap()
               for i in range(2)]
        out_hi = [st.enter_context(nc.sbuf_tensor(f"dq_oh{i}", [128, 512], BF16)).ap()
                  for i in range(2)]
        ps_sc = [st.enter_context(nc.psum_tensor(f"dq_ps{i}", [128, 512], F32)).ap()
                 for i in range(2)]
        if dbg_dq:
            dram["dqdump"] = nc.dram_tensor("dqdump", [128, 6 * 512], I32,
                                            kind="ExternalOutput").ap()
            dqd = st.enter_context(nc.sbuf_tensor("dq_dqd", [128, 6 * 512], I32)).ap()

        def decode2(eng, vin, lo, hi):
            (lc, lt, li1, li2, lsg, lb) = lo
            (hc, ht, hi1, hi2, hsg, hb) = hi
            eng.tensor_scalar(out=lc, in0=vin, scalar1=15, scalar2=None,
                              op0=OP.bitwise_and)
            b.dv(eng.tensor_scalar(out=hc, in0=vin, scalar1=4, scalar2=15,
                                   op0=OP.logical_shift_right, op1=OP.bitwise_and))
            for c, t in ((lc, lt), (hc, ht)):
                eng.tensor_scalar(out=t, in0=c, scalar1=7, scalar2=6,
                                  op0=OP.bitwise_and, op1=OP.logical_shift_left)
            for c, sg in ((lc, lsg), (hc, hsg)):
                mm = eng.tensor_scalar(out=sg, in0=c, scalar1=8, scalar2=12,
                                       op0=OP.bitwise_and, op1=OP.logical_shift_left)
            b.dv(mm)
            for t, i1 in ((lt, li1), (ht, hi1)):
                eng.tensor_scalar(out=i1, in0=t, scalar1=64, scalar2=None, op0=OP.is_ge)
            for t, i2 in ((lt, li2), (ht, hi2)):
                mm = eng.tensor_scalar(out=i2, in0=t, scalar1=128, scalar2=None,
                                       op0=OP.is_ge)
            b.dv(mm)
            for i1, t, bits in ((li1, lt, lb), (hi1, ht, hb)):
                eng.scalar_tensor_tensor(out=bits, in0=i1, scalar=16064, in1=t,
                                         op0=OP.mult, op1=OP.add)
            for i2, sg in ((li2, lsg), (hi2, hsg)):
                mm = eng.scalar_tensor_tensor(out=sg, in0=i2, scalar=64, in1=sg,
                                              op0=OP.mult, op1=OP.add)
            b.dv(mm)
            eng.tensor_tensor(out=lb, in0=lb, in1=lsg, op=OP.add)
            return b.dv(eng.tensor_tensor(out=hb, in0=hb, in1=hsg, op=OP.add))

        mats = [("f_gp", "f_gs", "wg_bf", 8, 32, NMLP, True),
                ("f_up", "f_us", "wu_bf", 8, 32, NMLP, True),
                ("f_dp", "f_ds", "wd_bf", 64, 2048 // 512, 2048, False)]
        st_dma = {}
        dec_done = {}
        for mat, msc, mdst, NJ, NS, NCOLS, plane in mats:
            for j in range(NJ):
                b.wait(nc.sync, "sP", b.cnt["sP"])
                b.dma("qS", scf[0:8, 0:NCOLS], dram[msc][8 * j:8 * j + 8, :])
                b.wait(nc.scalar, "qS", b.cnt["qS"])
                b.wait(nc.scalar, "sP", b.cnt["sP"])
                scv = b.inc(nc.scalar.activation(out=scb[0:8, 0:NCOLS],
                                                 in_=scf[0:8, 0:NCOLS], func=AF.Copy), "sA")
                b.wait(nc.tensor, "sA", scv)
                for ns in range(NS):
                    it = (j * NS + ns) % 2
                    if (it, "prev") in dec_done:
                        b.wait(nc.sync, "sV", dec_done[(it, "prev")][1])
                    b.dma("qS", pk[it], dram[mat][j * 128:(j + 1) * 128,
                                                  ns * 512:(ns + 1) * 512])
                    b.dma("qS", pk_echo, dram[mat][j * 128:(j + 1) * 128,
                                                   ns * 512:(ns + 1) * 512])
                    lq = b.cnt["qS"]
                    if (it, "scused") in dec_done:
                        b.wait(nc.tensor, "sV", dec_done[(it, "scused")])
                    smm = b.inc(nc.tensor.matmul(
                        ps_sc[it], sb["k_pat8"],
                        scb[0:8, ns * 512:(ns + 1) * 512],
                        start=True, stop=True), "sP")
                    b.wait(nc.vector, "qS", lq)
                    if (it, "outdma") in dec_done:
                        b.wait(nc.vector, "qG", dec_done[(it, "outdma")])
                    hi_d = decode2(nc.vector, pk[it],
                                   (vc_, vt_, vi1, vi2, vsg, vbits),
                                   (gc_, gt_, gi1, gi2, gsg, gbits))
                    dec_done[(it, "prev")] = (hi_d, hi_d)
                    b.wait(nc.vector, "sP", smm)
                    if dbg_dq and mat == "f_gp" and j == 0 and ns == 0:
                        nc.vector.tensor_copy(out=dqd[:, 0:512], in_=pk[it])
                        nc.vector.tensor_copy(out=dqd[:, 512:1024], in_=vbits)
                        nc.vector.tensor_copy(out=dqd[:, 1024:1536], in_=vc_)
                        nc.vector.tensor_copy(out=dqd[:, 1536:2048], in_=vt_)
                        nc.vector.tensor_copy(out=dqd[:, 2048:2560], in_=vi1)
                        nc.vector.tensor_copy(out=dqd[:, 2560:3072],
                                              in_=ps_sc[it].bitcast(I32))
                        dd = b.inc(nc.vector.tensor_copy(out=dqd[:, 0:1],
                                                         in_=dqd[:, 0:1]), "sV")
                        b.wait(nc.gpsimd, "sV", dd)
                        b.dma("qG", dram["dqdump"], dqd)
                    b.dv(nc.vector.tensor_tensor(out=out_lo[it],
                                                 in0=vbits.bitcast(BF16)[:, 0::2],
                                                 in1=ps_sc[it], op=OP.mult))
                    scu = b.inc(nc.vector.tensor_tensor(
                        out=out_hi[it], in0=gbits.bitcast(BF16)[:, 0::2],
                        in1=ps_sc[it], op=OP.mult), "sV")
                    dec_done[(it, "scused")] = scu
                    dec_done["last_scu"] = scu
                    b.wait(nc.scalar, "sV", scu)
                    if plane:
                        dst_lo = dram[mdst][:, (2 * j) * NMLP + ns * 512:
                                            (2 * j) * NMLP + (ns + 1) * 512]
                        dst_hi = dram[mdst][:, (2 * j + 1) * NMLP + ns * 512:
                                            (2 * j + 1) * NMLP + (ns + 1) * 512]
                    else:
                        dst_lo = dram[mdst][256 * j:256 * j + 128,
                                            ns * 512:(ns + 1) * 512]
                        dst_hi = dram[mdst][256 * j + 128:256 * j + 256,
                                            ns * 512:(ns + 1) * 512]
                    b.wait(nc.gpsimd, "sV", scu)
                    b.dma("qG", dst_lo, out_lo[it])
                    dec_done[(it, "outdma")] = b.dma("qG", dst_hi, out_hi[it])
        st.close()

    def emit_ffn(xsrc, xdst, tag):
        st = ExitStack()
        lnr = st.enter_context(nc.sbuf_tensor(f"{tag}_ln", [128, 16 * T], BF16)).ap()
        wgt = st.enter_context(nc.sbuf_tensor(f"{tag}_wg", [128, 16 * 4 * 128], BF16)).ap()
        wut = st.enter_context(nc.sbuf_tensor(f"{tag}_wu", [128, 16 * 4 * 128], BF16)).ap()
        gl = [st.enter_context(nc.sbuf_tensor(f"{tag}_gl{i}", [128, NTT], F32)).ap()
              for i in range(2)]
        itile = [st.enter_context(nc.sbuf_tensor(f"{tag}_it{i}", [128, NTT], BF16)).ap()
                 for i in range(2)]
        psA = [st.enter_context(nc.psum_tensor(f"{tag}_pa{i}", [128, NTT], F32)).ap()
               for i in range(2)]
        psB = [st.enter_context(nc.psum_tensor(f"{tag}_pb{i}", [128, NTT], F32)).ap()
               for i in range(2)]
        for kc in range(16):
            b.dma("qS", lnr[:, kc * T:(kc + 1) * T],
                  dram["ln_s"][kc * 128:(kc + 1) * 128, :])
        ln_ld = b.cnt["qS"]
        b.wait(nc.tensor, "qS", ln_ld)
        evd = {}
        itdma = {}
        for ng in range(NMLP // 512):
            b.wait(nc.sync, "sP", evd.get(("wfree", ng - 1), 0))
            for kc in range(16):
                b.dma("qS", wgt[:, kc * 512:(kc + 1) * 512],
                      dram["wg_bf"][:, kc * NMLP + ng * 512: kc * NMLP + (ng + 1) * 512])
            for kc in range(16):
                b.dma("qS", wut[:, kc * 512:(kc + 1) * 512],
                      dram["wu_bf"][:, kc * NMLP + ng * 512: kc * NMLP + (ng + 1) * 512])
            wq = b.cnt["qS"]
            b.wait(nc.tensor, "qS", wq)
            for tt in range(NT):
                for n4 in range(4):
                    ncx = ng * 4 + n4
                    it = ncx % 2
                    if ("ev", it) in evd:
                        b.wait(nc.tensor, "sV", evd[("ev", it)])
                        b.wait(nc.tensor, "sA", evd[("evA", it)])
                    for kc in range(16):
                        nc.tensor.matmul(psA[it], wgt[:, kc * 512 + n4 * 128:
                                                      kc * 512 + (n4 + 1) * 128],
                                         lnr[:, kc * T + tt * NTT: kc * T + (tt + 1) * NTT],
                                         start=(kc == 0), stop=(kc == 15))
                    for kc in range(16):
                        mm = nc.tensor.matmul(psB[it], wut[:, kc * 512 + n4 * 128:
                                                           kc * 512 + (n4 + 1) * 128],
                                              lnr[:, kc * T + tt * NTT: kc * T + (tt + 1) * NTT],
                                              start=(kc == 0), stop=(kc == 15))
                    mp = b.inc(mm, "sP")
                    b.wait(nc.scalar, "sP", mp)
                    if ("glfree", it) in evd:
                        b.wait(nc.scalar, "sV", evd[("glfree", it)])
                    ga = b.inc(nc.scalar.activation(out=gl[it], in_=psA[it],
                                                    func=AF.Gelu_apprx_tanh), "sA")
                    evd[("evA", it)] = ga
                    b.wait(nc.vector, "sA", ga)
                    b.wait(nc.vector, "sP", mp)
                    if ("itdma", it) in evd:
                        b.wait(nc.vector, "qG", evd[("itdma", it)])
                    ev = b.inc(nc.vector.tensor_tensor(out=itile[it], in0=gl[it],
                                                       in1=psB[it], op=OP.mult), "sV")
                    evd[("ev", it)] = ev
                    evd[("glfree", it)] = ev
                    b.wait(nc.gpsimd, "sV", ev)
                    evd[("itdma", it)] = b.dma(
                        "qG", dram["inter_s"][ncx * 128:(ncx + 1) * 128,
                                              tt * NTT:(tt + 1) * NTT], itile[it])
            evd[("wfree", ng)] = b.cnt["sP"]
        st.close()
        b.barrier()
        st = ExitStack()
        acc = st.enter_context(nc.sbuf_tensor(f"{tag}_acc", [128, 16 * NTT], F32)).ap()
        wds = [st.enter_context(nc.sbuf_tensor(f"{tag}_wd{i}", [128, 8 * 2048], BF16)).ap()
               for i in range(2)]
        its = [st.enter_context(nc.sbuf_tensor(f"{tag}_is{i}", [128, 8 * NTT], BF16)).ap()
               for i in range(2)]
        xres = [st.enter_context(nc.sbuf_tensor(f"{tag}_xr{i}", [128, NTT], F32)).ap()
                for i in range(2)]
        stage = [st.enter_context(nc.sbuf_tensor(f"{tag}_st{i}", [128, NTT], F32)).ap()
                 for i in range(2)]
        psD = [st.enter_context(nc.psum_tensor(f"{tag}_pd{i}", [128, NTT], F32)).ap()
               for i in range(2)]
        NSUP = NMLP // (8 * 128)
        bd = {}
        for tt in range(NT):
            for dc in range(16):
                if ("odma", dc % 2) in bd:
                    b.wait(nc.vector, "qG", bd[("odma", dc % 2)])
                b.inc(nc.vector.memset(acc[:, dc * NTT:(dc + 1) * NTT], 0.0), "sV")
            z0 = b.cnt["sV"]
            for sup in range(NSUP):
                it = sup % 2
                if ("bfree", it) in bd:
                    b.wait(nc.sync, "sP", bd[("bfree", it)])
                for ncl in range(8):
                    ncx = sup * 8 + ncl
                    b.dma("qS", wds[it][:, ncl * 2048:(ncl + 1) * 2048],
                          dram["wd_bf"][ncx * 128:(ncx + 1) * 128, :])
                    b.dma("qS", its[it][:, ncl * NTT:(ncl + 1) * NTT],
                          dram["inter_s"][ncx * 128:(ncx + 1) * 128,
                                          tt * NTT:(tt + 1) * NTT])
                lq = b.cnt["qS"]
                b.wait(nc.tensor, "qS", lq)
                if sup == 0:
                    b.wait(nc.tensor, "sV", z0)
                for dc in range(16):
                    pit = dc % 2
                    if ("acc", pit) in bd:
                        b.wait(nc.tensor, "sV", bd[("acc", pit)])
                    for ncl in range(8):
                        mm = nc.tensor.matmul(
                            psD[pit], wds[it][:, ncl * 2048 + dc * 128:
                                              ncl * 2048 + (dc + 1) * 128],
                            its[it][:, ncl * NTT:(ncl + 1) * NTT],
                            start=(ncl == 0), stop=(ncl == 7))
                    mp = b.inc(mm, "sP")
                    b.wait(nc.vector, "sP", mp)
                    bd[("acc", pit)] = b.inc(nc.vector.tensor_add(
                        out=acc[:, dc * NTT:(dc + 1) * NTT],
                        in0=psD[pit], in1=acc[:, dc * NTT:(dc + 1) * NTT]), "sV")
                bd[("bfree", it)] = b.cnt["sP"]
            for dc in range(16):
                key = dc % 2
                if ("xadd", key) in bd:
                    b.wait(nc.scalar, "sV", bd[("xadd", key)])
                b.dma("qA", xres[key],
                      dram[xsrc][dc * 128:(dc + 1) * 128, tt * NTT:(tt + 1) * NTT])
                b.wait(nc.vector, "qA", b.cnt["qA"])
                if ("odma", key) in bd:
                    b.wait(nc.vector, "qG", bd[("odma", key)])
                v = b.inc(nc.vector.tensor_add(out=stage[key],
                                               in0=acc[:, dc * NTT:(dc + 1) * NTT],
                                               in1=xres[key]), "sV")
                bd[("xadd", key)] = v
                b.wait(nc.gpsimd, "sV", v)
                dst = dram[xdst]
                bd[("odma", key)] = b.dma(
                    "qG", dst[dc * 128:(dc + 1) * 128, tt * NTT:(tt + 1) * NTT],
                    stage[key])
        st.close()

    def emit_bproj():
        st = ExitStack()
        lnr = st.enter_context(nc.sbuf_tensor("bp_ln", [128, 16 * T], BF16)).ap()
        wld = [st.enter_context(nc.sbuf_tensor(f"bp_wl{i}", [128, 16 * 128], F32)).ap()
               for i in range(2)]
        wbf = [st.enter_context(nc.sbuf_tensor(f"bp_wb{i}", [128, 16 * 128], BF16)).ap()
               for i in range(2)]
        rope = st.enter_context(nc.sbuf_tensor("bp_rope", [64, 4 * NTT], F32)).ap()
        ta = st.enter_context(nc.sbuf_tensor("bp_ta", [64, NTT], F32)).ap()
        tb = st.enter_context(nc.sbuf_tensor("bp_tb", [64, NTT], F32)).ap()
        stg = [st.enter_context(nc.sbuf_tensor(f"bp_st{i}", [128, NTT], BF16)).ap()
               for i in range(2)]
        s1s = st.enter_context(nc.sbuf_tensor("bp_s1", [16, NTT], F32)).ap()
        sb["r_s1b"] = st.enter_context(nc.sbuf_tensor("r_s1b", [16, T], BF16)).ap()
        wl1 = st.enter_context(nc.sbuf_tensor("bp_ws1l", [128, 16 * 16], F32)).ap()
        wb1 = st.enter_context(nc.sbuf_tensor("bp_ws1b", [128, 16 * 16], BF16)).ap()
        pp = [st.enter_context(nc.psum_tensor(f"bp_p{i}", [128, NTT], F32)).ap()
              for i in range(2)]
        ps1 = st.enter_context(nc.psum_tensor("bp_ps1", [16, NTT], F32)).ap()
        for kc in range(16):
            b.dma("qS", lnr[:, kc * T:(kc + 1) * T],
                  dram["ln_s"][kc * 128:(kc + 1) * 128, :])
        for kc in range(16):
            b.dma("qS", wl1[:, kc * 16:(kc + 1) * 16],
                  dram["ab_ws1"][kc * 128:(kc + 1) * 128, :])
        ln_ld = b.cnt["qS"]
        b.wait(nc.scalar, "qS", ln_ld)
        w1c = b.inc(nc.scalar.activation(out=wb1, in_=wl1, func=AF.Copy), "sA")
        b.wait(nc.tensor, "qS", ln_ld)
        b.wait(nc.tensor, "sA", w1c)
        s1done = 0
        for tt in range(NT):
            b.wait(nc.tensor, "sV", s1done)
            for kc in range(16):
                mm = nc.tensor.matmul(ps1, wb1[:, kc * 16:(kc + 1) * 16],
                                      lnr[:, kc * T + tt * NTT: kc * T + (tt + 1) * NTT],
                                      start=(kc == 0), stop=(kc == 15))
            mp = b.inc(mm, "sP")
            b.wait(nc.vector, "sP", mp)
            s1done = b.inc(nc.vector.tensor_copy(
                out=sb["r_s1b"][:, tt * NTT:(tt + 1) * NTT], in_=ps1), "sV")
        dsts = {"ab_wq": ("qkv_s", 0, "q"), "ab_wk": ("qkv_s", 2048, "k"),
                "ab_wv": ("vg_s", 0, None), "ab_wg": ("vg_s", 2048, None)}
        wcvt = {}
        evd = {}
        for mat, (dst, row0, rk) in dsts.items():
            for oc in range(16):
                it = oc % 2
                if ("wfree", it) in wcvt:
                    b.wait(nc.sync, "sP", wcvt[("wfree", it)])
                for kc in range(16):
                    b.dma("qS", wld[it][:, kc * 128:(kc + 1) * 128],
                          dram[mat][kc * 128:(kc + 1) * 128, oc * 128:(oc + 1) * 128])
                b.wait(nc.scalar, "qS", b.cnt["qS"])
                if ("wcvt", it) in wcvt:
                    b.wait(nc.scalar, "sP", wcvt[("wcvt", it)])
                wc = b.inc(nc.scalar.activation(out=wbf[it], in_=wld[it],
                                                func=AF.Copy), "sA")
                b.wait(nc.tensor, "sA", wc)
                for tt in range(NT):
                    if rk:
                        c1 = "c_cosq" if rk == "q" else "c_cos"
                        s1_ = "c_sinq" if rk == "q" else "c_sin"
                        b.wait(nc.sync, "sV", evd.get("ropeused", 0))
                        b.dma("qS", rope[:, 0:NTT],
                              dram[c1][:, tt * NTT:(tt + 1) * NTT])
                        b.dma("qS", rope[:, NTT:2 * NTT],
                              dram[s1_][:, tt * NTT:(tt + 1) * NTT])
                    rl = b.cnt["qS"]
                    pit = (oc * NT + tt) % 2
                    ps = pp[pit]
                    if ("ev", pit) in evd:
                        b.wait(nc.tensor, "sV", evd[("ev", pit)])
                    for kc in range(16):
                        mm = nc.tensor.matmul(ps, wbf[it][:, kc * 128:(kc + 1) * 128],
                                              lnr[:, kc * T + tt * NTT:
                                                  kc * T + (tt + 1) * NTT],
                                              start=(kc == 0), stop=(kc == 15))
                    mp = b.inc(mm, "sP")
                    wcvt[("wcvt", it)] = mp
                    b.wait(nc.vector, "sP", mp)
                    s = stg[pit]
                    if ("sdma", pit) in evd:
                        b.wait(nc.vector, "qG", evd[("sdma", pit)])
                    if rk:
                        b.wait(nc.vector, "qS", rl)
                        cosr = rope[:, 0:NTT]
                        sinr = rope[:, NTT:2 * NTT]
                        nc.vector.tensor_mul(out=ta, in0=ps[0:64, :], in1=cosr)
                        nc.vector.tensor_mul(out=tb, in0=ps[64:128, :], in1=sinr)
                        nc.vector.tensor_sub(out=s[0:64, :], in0=ta, in1=tb)
                        nc.vector.tensor_mul(out=ta, in0=ps[64:128, :], in1=cosr)
                        nc.vector.tensor_mul(out=tb, in0=ps[0:64, :], in1=sinr)
                        ev = b.inc(nc.vector.tensor_add(out=s[64:128, :], in0=ta,
                                                        in1=tb), "sV")
                        evd["ropeused"] = ev
                    else:
                        ev = b.inc(nc.vector.tensor_copy(out=s, in_=ps), "sV")
                    evd[("ev", pit)] = ev
                    b.wait(nc.gpsimd, "sV", ev)
                    evd[("sdma", pit)] = b.dma(
                        "qG", dram[dst][row0 + oc * 128:row0 + (oc + 1) * 128,
                                        tt * NTT:(tt + 1) * NTT], s)
                wcvt[("wfree", it)] = b.cnt["sP"]
        st.close()

    def emit_bheads():
        st = ExitStack()
        Qh = st.enter_context(nc.sbuf_tensor("bh_q", [128, T], BF16)).ap()
        Kh = st.enter_context(nc.sbuf_tensor("bh_k", [128, T], BF16)).ap()
        Vf = st.enter_context(nc.sbuf_tensor("bh_vf", [128, T], BF16)).ap()
        Vtm = st.enter_context(nc.sbuf_tensor("bh_vtm", [128, T], BF16)).ap()
        esTf = st.enter_context(nc.sbuf_tensor("bh_esf", [64, T], F32)).ap()
        esTb = st.enter_context(nc.sbuf_tensor("bh_esb", [64, T], BF16)).ap()
        cesT = st.enter_context(nc.sbuf_tensor("bh_ces", [64, T], F32)).ap()
        estm = st.enter_context(nc.sbuf_tensor("bh_estm", [128, (T // 128) * 64], BF16)).ap()
        enztm = st.enter_context(nc.sbuf_tensor("bh_enztm", [128, (T // 128) * 64], F32)).ap()
        shi = st.enter_context(nc.sbuf_tensor("bh_shi", [64, NTT], F32)).ap()
        Ab = [st.enter_context(nc.sbuf_tensor(f"bh_ab{i}", [128, NTT], BF16)).ap()
              for i in range(3)]
        okf = st.enter_context(nc.sbuf_tensor("bh_okf", [128, 64], F32)).ap()
        oke = st.enter_context(nc.sbuf_tensor("bh_oke", [128, 64], F32)).ap()
        mxc = st.enter_context(nc.sbuf_tensor("bh_mx", [128, 2], F32)).ap()
        qzb = st.enter_context(nc.sbuf_tensor("bh_qz", [128, 64], BF16)).ap()
        qzT = st.enter_context(nc.sbuf_tensor("bh_qzT", [64, NTT], BF16)).ap()
        sqs = st.enter_context(nc.sbuf_tensor("bh_sqs", [128, NTT], F32)).ap()
        rrow = st.enter_context(nc.sbuf_tensor("bh_rrow", [1, NTT], F32)).ap()
        gts = [st.enter_context(nc.sbuf_tensor(f"bh_gt{i}", [128, NTT], BF16)).ap()
               for i in range(2)]
        gsl = st.enter_context(nc.sbuf_tensor("bh_gsl", [128, NTT], F32)).ap()
        ons = st.enter_context(nc.sbuf_tensor("bh_ons", [128, NTT], F32)).ap()
        o1s = st.enter_context(nc.sbuf_tensor("bh_o1", [128, NTT], F32)).ap()
        ogs = [st.enter_context(nc.sbuf_tensor(f"bh_og{i}", [128, NTT], BF16)).ap()
               for i in range(2)]
        pq1 = st.enter_context(nc.psum_tensor("bh_pq1", [128, NTT], F32)).ap()
        pq2 = st.enter_context(nc.psum_tensor("bh_pq2", [128, NTT], F32)).ap()
        pq3 = st.enter_context(nc.psum_tensor("bh_pq3", [128, NTT], F32)).ap()
        pok = st.enter_context(nc.psum_tensor("bh_pok", [128, 256], F32)).ap()
        pO = st.enter_context(nc.psum_tensor("bh_pO", [128, NTT], F32)).ap()
        pbf = st.enter_context(nc.psum_tensor("bh_pbf", [128, 512], BF16)).ap()
        pbf2 = st.enter_context(nc.psum_tensor("bh_pbf2", [128, 512], BF16)).ap()
        NB = T // 128
        for h in range(H):
            hv = b.snap()
            b.wait(nc.sync, "sP", hv["sP"])
            b.wait(nc.sync, "sV", hv["sV"])
            b.dma("qS", Qh, dram["qkv_s"][h * 128:(h + 1) * 128, :])
            b.dma("qS", Kh, dram["qkv_s"][2048 + h * 128:2048 + (h + 1) * 128, :])
            b.dma("qS", Vf, dram["vg_s"][h * 128:(h + 1) * 128, :])
            hld = b.cnt["qS"]
            b.wait(nc.tensor, "sV", hv["sV"])
            b.wait(nc.tensor, "sA", hv["sA"])
            sdone = 0
            for tt in range(NT):
                mp = b.inc(nc.tensor.matmul(
                    pq1[0:64, :], sb["k_ws2Tb"][:, h * 64:(h + 1) * 64],
                    sb["r_s1b"][:, tt * NTT:(tt + 1) * NTT],
                    start=True, stop=True), "sP")
                b.wait(nc.scalar, "sP", mp)
                sa = b.inc(nc.scalar.activation(out=shi, in_=pq1[0:64, :],
                                                func=AF.Identity,
                                                bias=sb["k_bs2T"][:, h:h + 1]), "sA")
                b.wait(nc.vector, "sA", sa)
                cl = b.inc(nc.vector.tensor_scalar(out=shi, in0=shi, scalar1=32.0,
                                                   scalar2=-32.0, op0=OP.min,
                                                   op1=OP.max), "sV")
                b.wait(nc.scalar, "sV", cl)
                ex = b.inc(nc.scalar.activation(
                    out=esTf[:, tt * NTT:(tt + 1) * NTT], in_=shi, func=AF.Exp), "sA")
                b.wait(nc.vector, "sA", ex)
                sdone = b.inc(nc.vector.tensor_copy(
                    out=esTb[:, tt * NTT:(tt + 1) * NTT],
                    in_=esTf[:, tt * NTT:(tt + 1) * NTT]), "sV")
                b.wait(nc.tensor, "sV", sdone)
            sc = b.inc(nc.vector.tensor_tensor_scan(out=cesT, data0=esTf, data1=esTf,
                                                    initial=0.0, op0=OP.add,
                                                    op1=OP.bypass), "sV")
            rc = b.inc(nc.vector.reciprocal(out=cesT, in_=cesT), "sV")
            b.wait(nc.tensor, "qS", hld)
            tp = {}
            for blk in range(NB):
                pb = (pbf, pbf2)[blk % 2]
                if blk >= 2:
                    b.wait(nc.tensor, "sV", tp[blk - 2])
                nc.tensor.transpose(pb[:, 0:128], Vf[:, blk * 128:(blk + 1) * 128],
                                    sb["k_idenb"])
                if blk == 0:
                    b.wait(nc.tensor, "sV", sdone)
                tr = b.inc(nc.tensor.transpose(pb[:, 128:192],
                                               esTb[:, blk * 128:(blk + 1) * 128],
                                               sb["k_idenb"][0:64, 0:64]), "sP")
                b.wait(nc.vector, "sP", tr)
                nc.vector.tensor_copy(out=Vtm[:, blk * 128:(blk + 1) * 128],
                                      in_=pb[:, 0:128])
                tp[blk] = b.inc(nc.vector.tensor_copy(
                    out=estm[:, blk * 64:(blk + 1) * 64], in_=pb[:, 128:192]), "sV")
            for blk in range(NB):
                pf = (pq1, pq2)[blk % 2]
                if blk == 0:
                    b.wait(nc.tensor, "sV", rc)
                    b.wait(nc.tensor, "sA", b.cnt["sA"])
                if blk >= 2:
                    b.wait(nc.tensor, "sV", tp[NB + blk - 2])
                tr = b.inc(nc.tensor.transpose(pf[:, 0:64],
                                               cesT[:, blk * 128:(blk + 1) * 128],
                                               sb["k_idenf"][0:64, 0:64]), "sP")
                b.wait(nc.vector, "sP", tr)
                tp[NB + blk] = b.inc(nc.vector.tensor_copy(
                    out=enztm[:, blk * 64:(blk + 1) * 64], in_=pf[:, 0:64]), "sV")
            prep_done = b.cnt["sV"]
            b.wait(nc.tensor, "sV", prep_done)
            od = {}
            for qt in range(NT):
                nkt = 4 * qt + 4
                amp = {}
                def a_mm(kt):
                    pit = kt % 3
                    pA = (pq1, pq2, pq3)[pit]
                    if ("aev", pit) in od:
                        sm, cv = od[("aev", pit)]
                        b.wait(nc.tensor, sm, cv)
                    amp[kt] = b.inc(nc.tensor.matmul(
                        pA, Kh[:, kt * 128:(kt + 1) * 128],
                        Qh[:, qt * NTT:(qt + 1) * NTT], start=True, stop=True), "sP")
                a_mm(0)
                if nkt > 1:
                    a_mm(1)
                for kt in range(nkt):
                    pit = kt % 3
                    pA = (pq1, pq2, pq3)[pit]
                    r = kt - 4 * qt
                    if r >= 0:
                        b.wait(nc.vector, "sP", amp[kt])
                        ae = ("sV", b.inc(nc.vector.tensor_mul(
                            out=Ab[pit], in0=pA,
                            in1=sb["k_dmask"][:, r * 512:(r + 1) * 512]), "sV"))
                    else:
                        b.wait(nc.scalar, "sP", amp[kt])
                        ae = ("sA", b.inc(nc.scalar.activation(
                            out=Ab[pit], in_=pA, func=AF.Copy), "sA"))
                    od[("aev", pit)] = ae
                    if kt + 2 < nkt:
                        a_mm(kt + 2)
                    b.wait(nc.tensor, ae[0], ae[1])
                    for s4 in range(4):
                        mm = nc.tensor.matmul(
                            pok[:, s4 * 64:(s4 + 1) * 64],
                            Ab[pit][:, s4 * 128:(s4 + 1) * 128],
                            estm[:, kt * 64:(kt + 1) * 64],
                            start=(kt == 0 and s4 == 0),
                            stop=(kt == nkt - 1 and s4 == 3))
                    okp = b.inc(mm, "sP")
                b.wait(nc.vector, "sP", okp)
                for s4 in range(4):
                    tb4 = 4 * qt + s4
                    b.dv(nc.vector.tensor_mul(out=okf, in0=pok[:, s4 * 64:(s4 + 1) * 64],
                                              in1=enztm[:, tb4 * 64:(tb4 + 1) * 64]))
                    mx = b.inc(nc.vector.reduce_max(out=mxc[:, 0:1], in_=okf,
                                                    axis=AX.X, negate=True), "sV")
                    b.wait(nc.scalar, "sV", mx)
                    ea = b.inc(nc.scalar.activation(out=oke, in_=okf, func=AF.Exp,
                                                    bias=mxc[:, 0:1]), "sA")
                    b.wait(nc.vector, "sA", ea)
                    b.dv(nc.vector.reduce_sum(out=mxc[:, 1:2], in_=oke, axis=AX.X))
                    b.dv(nc.vector.reciprocal(out=mxc[:, 1:2], in_=mxc[:, 1:2]))
                    qz = b.inc(nc.vector.scalar_tensor_tensor(
                        out=qzb, in0=oke, scalar=mxc[:, 1:2],
                        in1=enztm[:, tb4 * 64:(tb4 + 1) * 64],
                        op0=OP.mult, op1=OP.mult), "sV")
                    b.wait(nc.tensor, "sV", qz)
                    pb = (pbf, pbf2)[s4 % 2]
                    tr = b.inc(nc.tensor.transpose(pb[0:64, 0:128],
                                                   qzb, sb["k_idenb"]), "sP")
                    b.wait(nc.vector, "sP", tr)
                    b.inc(nc.vector.tensor_copy(
                        out=qzT[:, s4 * 128:(s4 + 1) * 128], in_=pb[0:64, 0:128]), "sV")
                qzd = b.cnt["sV"]
                b.wait(nc.tensor, "sV", qzd)
                wmp = {}
                def w_mm(kt):
                    pit = kt % 3
                    pW = (pq1, pq2, pq3)[pit]
                    if ("wev", pit) in od:
                        sm, cv = od[("wev", pit)]
                        b.wait(nc.tensor, sm, cv)
                    wmp[kt] = b.inc(nc.tensor.matmul(
                        pW, esTb[:, kt * 128:(kt + 1) * 128], qzT,
                        start=True, stop=True), "sP")
                w_mm(0)
                if nkt > 1:
                    w_mm(1)
                for kt in range(nkt):
                    pit = kt % 3
                    pW = (pq1, pq2, pq3)[pit]
                    r = kt - 4 * qt
                    if r >= 0:
                        b.wait(nc.vector, "sP", wmp[kt])
                        we = ("sV", b.inc(nc.vector.tensor_mul(
                            out=Ab[pit], in0=pW,
                            in1=sb["k_dmask"][:, r * 512:(r + 1) * 512]), "sV"))
                    else:
                        b.wait(nc.scalar, "sP", wmp[kt])
                        we = ("sA", b.inc(nc.scalar.activation(
                            out=Ab[pit], in_=pW, func=AF.Copy), "sA"))
                    od[("wev", pit)] = we
                    if kt + 2 < nkt:
                        w_mm(kt + 2)
                    b.wait(nc.tensor, we[0], we[1])
                    mm = nc.tensor.matmul(pO, Vtm[:, kt * 128:(kt + 1) * 128],
                                          Ab[pit], start=(kt == 0),
                                          stop=(kt == nkt - 1))
                op = b.inc(mm, "sP")
                b.wait(nc.scalar, "sP", op)
                sq = b.inc(nc.scalar.activation(out=sqs, in_=pO, func=AF.Square), "sA")
                b.wait(nc.tensor, "sA", sq)
                rm = b.inc(nc.tensor.matmul(pq1[0:1, :], sb["k_onescol"], sqs,
                                            start=True, stop=True), "sP")
                b.wait(nc.scalar, "sP", rm)
                nc.scalar.activation(out=rrow, in_=pq1[0:1, :], func=AF.Ln,
                                     scale=1.0 / 128, bias=sb["k_cc"][0:1, 3:4])
                rr = b.inc(nc.scalar.activation(out=rrow, in_=rrow, func=AF.Exp,
                                                scale=-0.5), "sA")
                b.wait(nc.tensor, "sA", rr)
                bc = b.inc(nc.tensor.matmul(pq2, sb["k_onesfT"], rrow,
                                            start=True, stop=True), "sP")
                git = qt % 2
                if ("gdma", git) in od:
                    b.wait(nc.scalar, "qA", od[("gdma", git)])
                od[("gdma", git)] = b.dma(
                    "qA", gts[git], dram["vg_s"][2048 + h * 128:2048 + (h + 1) * 128,
                                                 qt * NTT:(qt + 1) * NTT])
                b.wait(nc.scalar, "qA", od[("gdma", git)])
                b.wait(nc.scalar, "sV", b.cnt["sV"])
                gs_ = b.inc(nc.scalar.activation(out=gsl, in_=gts[git],
                                                 func=AF.Silu), "sA")
                ons_ = b.inc(nc.scalar.activation(out=ons, in_=pO, func=AF.Copy,
                                                  scale=sb["k_gnw"]), "sA")
                b.wait(nc.vector, "sP", bc)
                b.wait(nc.vector, "sA", ons_)
                nc.vector.tensor_mul(out=o1s, in0=ons, in1=pq2)
                if ("odma", git) in od:
                    b.wait(nc.vector, "qG", od[("odma", git)])
                og = b.inc(nc.vector.tensor_mul(out=ogs[git], in0=o1s, in1=gsl), "sV")
                b.wait(nc.gpsimd, "sV", og)
                od[("odma", git)] = b.dma(
                    "qG", dram["o_s"][h * 128:(h + 1) * 128, qt * NTT:(qt + 1) * NTT],
                    ogs[git])
                for pp_ in range(3):
                    od[("aev", pp_)] = ("sV", og)
                    od[("wev", pp_)] = ("sV", og)
        st.close()

    emits = {
        "init": emit_init,
        "ln1": lambda: emit_ln("x_fm", "ln1"),
        "aproj": emit_aproj,
        "abeta": emit_abeta,
        "aconv": emit_aconv,
        "adelta": emit_adelta,
        "aout": lambda: emit_outproj("aa_ow", "o_s", "x_fm", "xa", "aout"),
        "ln2": lambda: emit_ln("xa", "ln2"),
        "dequant": emit_dequant,
        "ffn1": lambda: emit_ffn("xa", "xb", "f1"),
        "ln3": lambda: emit_ln("xb", "ln3"),
        "battn": lambda: (emit_bproj(), b.barrier(), emit_bheads()),
        "bout": lambda: emit_outproj("ab_wo", "o_s", "xb", "xa", "bout"),
        "ln4": lambda: emit_ln("xa", "ln4"),
        "ffn2": lambda: emit_ffn("xa", "y_fm", "f2"),
    }
    for p in ph:
        if p in emits:
            emits[p]()
            b.barrier()

    nc.sync.wait_ge(b.sem["qG"], b.cnt["qG"])
    nc.sync.wait_ge(b.sem["qA"], b.cnt["qA"])
    nc.sync.wait_ge(b.sem["qS"], b.cnt["qS"])
    return nc, b, dram


def kernel(**inputs):
    T = np.asarray(inputs["L0_attn_a_x"]).shape[1]
    w, xs = host_prep(inputs, T)
    nc, _, _ = build_nc(T)
    in_maps = []
    for c in range(len(xs)):
        m = dict(w)
        m["x_fm"] = xs[c]
        in_maps.append(m)
    res = run_bass_kernel_spmd(nc, in_maps, core_ids=list(range(len(xs))))
    return host_post([r["y_fm"] for r in res.results], T)
